# version 15
# baseline (speedup 1.0000x reference)
import os
import numpy as np
import concourse.bass as bass
import concourse.mybir as mybir
from concourse.bass_utils import run_bass_kernel_spmd

F32 = mybir.dt.float32
BF16 = mybir.dt.bfloat16
AF = mybir.ActivationFunctionType
ALU = mybir.AluOpType
AX = mybir.AxisListType.X

NH, HD, BLK = 16, 128, 256
D = 2048
T = 1024
C = 16
NB = 4
E = 4
EPS_LN = 1e-5
EPS_RMS = 1e-6
KB = 1024

OFF_A = 0 * KB
OFF_B = 32 * KB
OFF_C = 64 * KB
OFF_D = 96 * KB
OFF_E = 128 * KB
OFF_G = 160 * KB
OFF_W = 175 * KB
OFF_S = 191 * KB
ARENA = 196 * KB


class Prog:

    ENGS = ("sync", "gpsimd", "tensor", "vector", "scalar")

    def __init__(self):
        self.ops = {e: [] for e in self.ENGS}
        self.count = {}

    def emit(self, eng, fn, deps=(), inc=None, n=1):
        deps = tuple(d for d in deps if d is not None)
        self.ops[eng].append((fn, deps, inc, n))
        if inc is not None:
            self.count[inc] = self.count.get(inc, 0) + n
            return (inc, self.count[inc])
        return None

    def run(self, engname, eng, sems):
        seen = {}
        for fn, deps, inc, n in self.ops[engname]:
            for s, v in deps:
                if seen.get(s, 0) < v:
                    eng.wait_ge(sems[s], v)
                    seen[s] = v
            ins = fn(eng)
            if inc is not None:
                ins.then_inc(sems[inc], n)


_DEBUG = bool(os.environ.get("KDEBUG"))


def build_nc():
    nc = bass.Bass("TRN2", target_bir_lowering=False)
    dbg_t = {}
    dbg_dram = {}

    x_in = nc.dram_tensor("x_sh", [T, D], F32, kind="ExternalInput").ap()
    wqkvT = nc.dram_tensor("wqkvT", [16 * 128, C * 256], F32, kind="ExternalInput").ap()
    wv = nc.dram_tensor("wv", [4 * 128, C * 512], F32, kind="ExternalInput").ap()
    wgT = nc.dram_tensor("wgT", [8 * 128, C * 256], F32, kind="ExternalInput").ap()
    woT = nc.dram_tensor("woT", [4 * 128, C * 512], F32, kind="ExternalInput").ap()
    gup_w = nc.dram_tensor("gup_w", [E * 8 * 128, C * 512], F32, kind="ExternalInput").ap()
    down_w = nc.dram_tensor("down_w", [E * 4 * 128, C * 512], F32, kind="ExternalInput").ap()
    slope_b = nc.dram_tensor("slope_b", [128, NH], F32, kind="ExternalInput").ap()
    normw_in = nc.dram_tensor("normw", [128, C], F32, kind="ExternalInput").ap()
    flag_in = nc.dram_tensor("flag", [128, 1], F32, kind="ExternalInput").ap()
    out_d = nc.dram_tensor("out", [T, D], F32, kind="ExternalOutput").ap()

    cc_in = nc.dram_tensor("cc_in", [128, NH * HD], F32, kind="Internal")
    cc_out = nc.dram_tensor("cc_out", [256, NH * HD], F32, kind="Internal")
    x2sp = nc.dram_tensor("x2spill", [T, D], F32, kind="Internal").ap()
    rstd_d = nc.dram_tensor("rstd_d", [1, T], F32, kind="Internal").ap()
    kvl_d = nc.dram_tensor("kvloc_d", [NH * 128, 3 * HD], F32, kind="Internal").ap()

    idxt_np = np.empty((128, 2, BLK), np.float32)
    for j in range(2):
        n_idx = (j * 128 + np.arange(128))[:, None]
        m_idx = np.arange(BLK)[None, :]
        idxt_np[:, j, :] = np.where(m_idx >= n_idx,
                                    (m_idx - n_idx).astype(np.float32), 1e9)
    arrq_np = np.arange(1, BLK + 1, dtype=np.float32).reshape(1, BLK)
    arrk_np = np.empty((128, 2), np.float32)
    for j in range(2):
        arrk_np[:, j] = 255.0 - (j * 128 + np.arange(128))
    idxt_d = nc.inline_tensor(idxt_np, "idxt_c").ap()
    arrq_d = nc.inline_tensor(arrq_np, "arrq_c").ap()
    arrk_d = nc.inline_tensor(arrk_np, "arrk_c").ap()
    ident_d = nc.inline_tensor(np.eye(128, dtype=np.float32), "ident_c").ap()
    ones_c_d = nc.inline_tensor(np.ones((128, 1), np.float32), "ones_c_c").ap()
    ones_r_d = nc.inline_tensor(np.ones((1, 128), np.float32), "ones_r_c").ap()

    nc.alloc_sbuf_tensor("arena", [128, ARENA], mybir.dt.uint8)
    base = nc.sbuf_base - ARENA

    def place(name, shape, dtype, off):
        return nc.alloc_sbuf_tensor_at(name, shape, dtype, offset=base + off).ap()

    xn1T = place("xn1T", [128, C, T], BF16, OFF_A)
    kvinit = place("kvinit", [128, NH * HD], F32, OFF_A)
    xn2T = place("xn2T", [128, C, T], BF16, OFF_A)
    qf = place("qf", [128, NH, T], BF16, OFF_B)
    vbuf = [place(f"vb{i}", [128, C, 512], BF16, OFF_B + i * 16 * KB)
            for i in range(2)]
    woB = place("woB", [128, 2, C, 512], BF16, OFF_B)
    inter = place("inter", [128, C, 256], BF16, OFF_B)
    x2r = place("x2r", [128, 2, D], F32, OFF_B + 8 * KB)
    kf = place("kf", [128, NH, T], BF16, OFF_C)
    xin = place("xin", [128, 2, D], F32, OFF_C)
    aout = place("aout", [128, 2, D], F32, OFF_C + 8 * KB)
    xn2s = place("xn2s", [128, 2, D], BF16, OFF_C + 24 * KB)
    sqtC = place("sqtC", [128, 512], F32, OFF_G + 7 * KB)
    v_tm = place("v_tm", [128, 8, D], BF16, OFF_D)
    gz = place("gz", [128, C, T], BF16, OFF_D)
    gp = place("gp", [128, C, T], BF16, OFF_E)
    woE = place("woE", [128, 2, C, 512], BF16, OFF_E)
    obuf = place("obuf", [128, 2, D], F32, OFF_E)
    sgate = place("sgate", [128, 2, 256], F32, OFF_B + 24 * KB)
    kvrun = place("kvrun", [128, 2, HD], F32, OFF_G)
    kvst = place("kvst", [128, 2, 3, HD], F32, OFF_G + 1 * KB)
    kvld = place("kvld", [128, 2, 3, HD], F32, OFF_G + 4 * KB)
    diagT = place("diagT", [128, 2, BLK], F32, OFF_G + 7 * KB)
    qdecb = place("qdecb", [128, BLK], F32, OFF_G + 9 * KB)
    kdec = place("kdec", [128, 2], F32, OFF_G + 10 * KB)
    lamt = place("lamt", [128, NB], F32, OFF_G + 10 * KB + 32)
    qtil = place("qtil", [128, 2, BLK], BF16, OFF_G + 10 * KB + 256)
    qks = place("qks", [128, 2, 2, BLK], BF16, OFF_G + 11 * KB + 256)
    kvt = place("kvt", [128, 2, HD], BF16, OFF_G + 13 * KB + 256)
    ktil = place("ktil", [128, 2, HD], BF16, OFF_G + 13 * KB + 768)
    arrk = place("arrk", [128, 2], F32, OFF_G + 14 * KB + 256)
    sqG = place("sqG", [128, 2, T], BF16, OFF_G)
    rrow = place("rrow", [1, T], F32, OFF_G + 4 * KB)
    wsl = [place(f"wsl{i}", [128, C, 256], BF16, OFF_W + i * 8 * KB) for i in range(2)]
    dw = [place(f"dw{i}", [128, C, 512], BF16, OFF_D + i * 16 * KB) for i in range(2)]
    rw8 = place("rw8", [128, C, 512], BF16, OFF_W)
    ident = place("ident", [128, 128], BF16, OFF_S)
    ones_c = place("ones_c", [128, 1], BF16, OFF_S + 256)
    ones_r = place("ones_r", [1, 128], BF16, OFF_S + 288)
    arrq = place("arrq", [1, BLK], F32, OFF_S + 544)
    qdrow = place("qdrow", [1, BLK], BF16, OFF_S + 1568)
    normw = place("normw", [128, C], F32, OFF_S + 2080)
    slopes = place("slopes", [128, NH], F32, OFF_S + 2144)
    negs = place("negs", [128, NH], F32, OFF_S + 2208)
    flag = place("flag", [128, 1], F32, OFF_S + 2272)
    rstdc = place("rstdc", [128, 8], F32, OFF_S + 2304)
    stat = place("stat", [128, 2, 8], F32, OFF_S + 2336)
    idxt = place("idxt", [128, 2, BLK], F32, OFF_S + 2432)

    fb = [nc.alloc_psum_tensor(f"fb{i}", [128, 512], F32).ap() for i in range(6)]
    tbk = [nc.alloc_psum_tensor(f"tb{i}", [128, 1024], BF16).ap() for i in range(2)]

    P = Prog()

    def dma(q, out, in_, deps=(), sem=None):
        return P.emit(q, lambda e, o=out, i=in_: e.dma_start(out=o, in_=i),
                      deps, sem or ("S" if q == "sync" else "G"), 16)

    def mm(out, lhsT, rhs, start, stop, deps=(), inc=False):
        return P.emit("tensor",
                      lambda e, o=out, l=lhsT, r=rhs, st=start, sp=stop:
                      e.matmul(o, l, r, start=st, stop=sp),
                      deps, "T" if inc else None, 1)

    def tr(out, in_, deps=(), inc=False):
        return P.emit("tensor", lambda e, o=out, i=in_: e.transpose(o, i, ident),
                      deps, "T" if inc else None, 1)

    def act(fn, deps=(), inc=False):
        return P.emit("scalar", fn, deps, "A" if inc else None, 1)

    def vec(fn, deps=(), inc=False):
        return P.emit("vector", fn, deps, "V" if inc else None, 1)

    bank_free = [None] * 6
    grp_i = [0]

    def next_bank():
        i = grp_i[0] % 6
        grp_i[0] += 1
        return i, bank_free[i]

    def dbg_dump(name, src_ap, deps):
        if not _DEBUG:
            return None
        th = nc.dram_tensor("d_" + name, list(src_ap.shape), src_ap.dtype,
                            kind="ExternalOutput")
        dbg_dram[name] = th
        dbg_t[name] = dma("sync", th.ap(), src_ap, deps)
        return dbg_t[name]

    def ln_stats(src, b, sq_scr, deps):
        t_sum = vec(lambda e, src=src, b=b: e.reduce_sum(stat[:, b, 0:1], src, AX),
                    deps, inc=True)
        t_prev = None
        for s in range(4):
            d = list(deps) + [t_prev]
            t_q = act(lambda e, src=src, b=b, s=s, sq=sq_scr:
                      e.activation(sq, src[:, s * 512:(s + 1) * 512], AF.Square,
                                   accum_out=stat[:, b, 1:2] if s == 0
                                   else stat[:, b, 7:8]), d, inc=True)
            if s > 0:
                t_prev = vec(lambda e, b=b: e.tensor_tensor(stat[:, b, 1:2],
                                                            stat[:, b, 1:2],
                                                            stat[:, b, 7:8], ALU.add),
                             [t_q, t_prev], inc=True)
            else:
                t_prev = t_q
        return t_sum, t_prev

    def cols_ln_emit(b, deps, eps=EPS_LN):
        t1 = vec(lambda e: e.tensor_scalar(stat[:, b, 2:3], stat[:, b, 0:1],
                                           1.0 / D, None, ALU.mult), deps, inc=True)
        t2 = vec(lambda e: e.tensor_scalar(stat[:, b, 3:4], stat[:, b, 1:2],
                                           1.0 / D, None, ALU.mult), deps, inc=True)
        t3 = vec(lambda e: e.tensor_tensor(stat[:, b, 4:5], stat[:, b, 2:3],
                                           stat[:, b, 2:3], ALU.mult), [t1], inc=True)
        t4 = vec(lambda e: e.tensor_tensor(stat[:, b, 4:5], stat[:, b, 3:4],
                                           stat[:, b, 4:5], ALU.subtract),
                 [t2, t3], inc=True)
        t5 = vec(lambda e: e.tensor_scalar(stat[:, b, 5:6], stat[:, b, 4:5], eps,
                                           None, ALU.add), [t4], inc=True)
        return vec(lambda e: e.reciprocal(stat[:, b, 5:6], stat[:, b, 5:6]),
                   [t5], inc=True)

    dma("gpsimd", ident[:, :], ident_d[:, :])
    dma("gpsimd", ones_c[:, :], ones_c_d[:, :])
    dma("gpsimd", ones_r[:, :], ones_r_d[:, :])
    dma("gpsimd", idxt[:, :, :], idxt_d[:, :, :])
    dma("gpsimd", arrq[:, :], arrq_d[:, :])
    dma("gpsimd", arrk[:, :], arrk_d[:, :])
    dma("gpsimd", normw[:, :], normw_in[:, :])
    dma("gpsimd", slopes[:, :], slope_b[:, :])
    t_cst = dma("gpsimd", flag[:, :], flag_in[:, :])
    t_negs = vec(lambda e: e.tensor_scalar(negs[:, :], slopes[:, :], -1.0, None,
                                           ALU.mult), [t_cst], inc=True)

    wq_r = wqkvT.rearrange("(s p) w -> p s w", p=128)
    wv_r = wv.rearrange("(s p) w -> p s w", p=128)
    wg_r = wgT.rearrange("(s p) w -> p s w", p=128)
    wo_r = woT.rearrange("(s p) w -> p s w", p=128)
    gup_r = gup_w.rearrange("(ee s p) w -> p ee s w", p=128, ee=E)
    dwn_r = down_w.rearrange("(ee s p) w -> p ee s w", p=128, ee=E)
    slab_done = {}
    slab_loaded = {}
    slab_serial = [0]

    def load_slab(src_ap, extra=()):
        s = slab_serial[0]
        d = list(extra)
        if s >= 2:
            d.append(slab_done.get(s - 2))
        slab_loaded[s] = dma("gpsimd", wsl[s % 2].rearrange("p c w -> p (c w)"),
                             src_ap, d)
        slab_serial[0] = s + 1
        return s, wsl[s % 2]

    class Ring:
        def __init__(self, bufs, init_tok=None, sembase="R"):
            self.bufs = bufs
            self.n = len(bufs)
            self.s = 0
            self.done = {}
            self.loaded = {}
            self.init_tok = init_tok
            self.sembase = sembase

        def load(self, src_ap, extra=()):
            i = self.s
            d = list(extra) + [self.done.get(i - self.n, self.init_tok)]
            buf = self.bufs[i % self.n]
            tok = dma("gpsimd", buf.rearrange("p c w -> p (c w)"), src_ap, d,
                      sem=f"{self.sembase}{i % self.n}")
            self.loaded[i] = tok
            self.s = i + 1
            return i, buf, tok

    x_r = x_in.rearrange("(tt p) dd -> p tt dd", p=128)
    p0_evac = []
    for tt in range(8):
        b = tt % 2
        d = [p0_evac[2 * (tt - 2) + 1]] if tt >= 2 else []
        t_x = dma("sync", xin[:, b, :], x_r[:, tt, :], d)
        t_sum, t_ss = ln_stats(xin[:, b, :], b, sqtC[:, :], [t_x])
        t_cols = cols_ln_emit(b, [t_sum, t_ss])
        t_rstd = act(lambda e, b=b: e.activation(stat[:, b, 5:6], stat[:, b, 5:6],
                                                 AF.Sqrt), [t_cols], inc=True)
        t_app = vec(lambda e, b=b: e.tensor_scalar(xn2s[:, b, :], xin[:, b, :],
                                                   stat[:, b, 2:3], stat[:, b, 5:6],
                                                   ALU.subtract, ALU.mult),
                    [t_rstd], inc=True)
        for half in range(2):
            bank = tbk[half]
            tdeps = [t_app, p0_evac[-1] if p0_evac else None]
            tok_t = None
            for c8 in range(8):
                c = half * 8 + c8
                tok_t = tr(bank[:, c8 * 128:(c8 + 1) * 128],
                           xn2s[:, b, c * 128:(c + 1) * 128],
                           tdeps if c8 == 0 else (), inc=(c8 == 7))
            off = 512 * (tt % 2) + tt // 2
            t_ev = act(lambda e, bank=bank, off=off, half=half:
                       e.activation(xn1T[:, half * 8:(half + 1) * 8,
                                         off:off + 509:4],
                                    bank[:, :], AF.Copy), [tok_t], inc=True)
            p0_evac.append(t_ev)
    t_xn1T = p0_evac[-1]
    dbg_dump("xn1T", xn1T[:, :, :], [t_xn1T])

    silu_evac = {}

    def proj_fm(buf, dst, kind, heads, tok_load):
        tok_mm = None
        for obi, h in enumerate(heads):
            for th in range(2):
                bi, bfree = next_bank()
                for kc in range(C):
                    deps = [tok_load, bfree, t_xn1T] if kc == 0 else ()
                    tok_mm = mm(fb[bi][:, :], buf[:, kc, obi * 128:(obi + 1) * 128],
                                xn1T[:, kc, th * 512:(th + 1) * 512],
                                start=(kc == 0), stop=(kc == C - 1),
                                deps=deps, inc=(kc == C - 1))
                t_ev = act(lambda e, bi=bi, dst=dst, h=h, th=th:
                           e.activation(dst[:, h, th * 512:(th + 1) * 512],
                                        fb[bi][:, :], AF.Silu), [tok_mm], inc=True)
                bank_free[bi] = t_ev
                silu_evac[(kind, h, th)] = t_ev
        return tok_mm

    for s in range(8):
        ss, buf = load_slab(wq_r[:, s, :])
        slab_done[ss] = proj_fm(buf, kf, "k", [2 * s, 2 * s + 1], slab_loaded[ss])
    dbg_dump("k", kf[:, :, :], [silu_evac[("k", NH - 1, 1)]])
    tb_guard = t_xn1T
    vring = Ring(vbuf, sembase="VB")
    for s in range(4):
        vi, buf, tok_ld = vring.load(wv_r[:, s, :])
        tok_mm = None
        for tt in range(8):
            bi, bfree = next_bank()
            for kc in range(C):
                deps = [tok_ld, bfree, t_xn1T] if kc == 0 else ()
                tok_mm = mm(fb[bi][:, :], xn1T[:, kc, tt * 128:(tt + 1) * 128],
                            buf[:, kc, :], start=(kc == 0), stop=(kc == C - 1),
                            deps=deps, inc=(kc == C - 1))
            t_ev = act(lambda e, bi=bi, tt=tt, s=s:
                       e.activation(v_tm[:, tt, 512 * s:512 * (s + 1)],
                                    fb[bi][:, :], AF.Silu), [tok_mm], inc=True)
            bank_free[bi] = t_ev
            silu_evac[("v", tt, s)] = t_ev
        vring.done[vi] = tok_mm
    v_last_mm = tok_mm
    dbg_dump("v", v_tm[:, :, :], [silu_evac[("v", 7, 3)]])

    cc_dma = {}
    kvst_dma = {}
    pass1_prev = None
    for h in range(NH):
        hb = h % 2
        t_lam = act(lambda e, h=h: e.activation(lamt[:, 0:1], negs[:, h:h + 1],
                                                AF.Exp, scale=256.0),
                    [t_negs, pass1_prev], inc=True)
        t_kd = act(lambda e, h=h: e.activation(kdec[:, :], arrk[:, :], AF.Exp,
                                               scale=negs[:, h:h + 1]),
                   [t_negs, t_cst], inc=True)
        tok_kvup = None
        tok_st = None
        for t in range(NB):
            ktoks = []
            for j in range(2):
                bank = tbk[j]
                deps = [silu_evac[("k", h, (t * 256 + j * 128) // 512)], tb_guard]
                tok_t = tr(bank[:, 0:128],
                           kf[:, h, t * 256 + j * 128:t * 256 + (j + 1) * 128],
                           deps, inc=True)
                t_ke = vec(lambda e, j=j, bank=bank:
                           e.tensor_scalar(ktil[:, j, :], bank[:, 0:128],
                                           kdec[:, j:j + 1], None, ALU.mult),
                           [tok_t, t_kd], inc=True)
                tb_guard = t_ke
                ktoks.append(t_ke)
            if t >= 1:
                tok_st = vec(lambda e, t=t, hb=hb:
                             e.tensor_copy(kvst[:, hb, t - 1, :], kvrun[:, hb, :]),
                             [tok_kvup, kvst_dma.get(h - 2)], inc=True)
            bi, bfree = next_bank()
            tok_c = None
            for j in range(2):
                deps = [ktoks[j], silu_evac[("v", 2 * t + j, h // 4)]]
                if j == 0:
                    deps.append(bfree)
                tok_c = mm(fb[bi][:, 0:128], ktil[:, j, :],
                           v_tm[:, 2 * t + j, h * 128:(h + 1) * 128],
                           start=(j == 0), stop=(j == 1), deps=deps, inc=(j == 1))
            if t == 0:
                tok_kvup = vec(lambda e, hb=hb, bi=bi:
                               e.tensor_copy(kvrun[:, hb, :], fb[bi][:, 0:128]),
                               [tok_c, cc_dma.get(h - 2)], inc=True)
            else:
                tok_kvup = vec(lambda e, hb=hb, bi=bi:
                               e.scalar_tensor_tensor(kvrun[:, hb, :], kvrun[:, hb, :],
                                                      lamt[:, 0:1], fb[bi][:, 0:128],
                                                      ALU.mult, ALU.add),
                               [tok_c, t_lam, tok_kvup], inc=True)
            bank_free[bi] = tok_kvup
        kvst_dma[h] = dma("sync", kvl_d[h * 128:(h + 1) * 128, :],
                          kvst[:, hb, :, :].rearrange("p a w -> p (a w)"), [tok_st])
        cc_dma[h] = dma("sync", cc_in.ap()[:, h * 128:(h + 1) * 128],
                        kvrun[:, hb, :], [tok_kvup])
        pass1_prev = tok_kvup

    for s in range(8):
        ss, buf = load_slab(wq_r[:, 8 + s, :])
        slab_done[ss] = proj_fm(buf, qf, "q", [2 * s, 2 * s + 1], slab_loaded[ss])
    dbg_dump("q", qf[:, :, :], [silu_evac[("q", NH - 1, 1)]])

    for s in range(8):
        ss, buf = load_slab(wg_r[:, s, :])
        tok_mm = None
        for obi in range(2):
            c = 2 * s + obi
            for th in range(2):
                bi, bfree = next_bank()
                for kc in range(C):
                    deps = [slab_loaded[ss], bfree, t_xn1T] if kc == 0 else ()
                    tok_mm = mm(fb[bi][:, :], buf[:, kc, obi * 128:(obi + 1) * 128],
                                xn1T[:, kc, th * 512:(th + 1) * 512],
                                start=(kc == 0), stop=(kc == C - 1),
                                deps=deps, inc=(kc == C - 1))
                t_sg = act(lambda e, bi=bi, c=c, th=th:
                           e.activation(gp[:, c, th * 512:(th + 1) * 512],
                                        fb[bi][:, :], AF.Sigmoid), [tok_mm], inc=True)
                t_nw = vec(lambda e, c=c, th=th:
                           e.tensor_scalar(gp[:, c, th * 512:(th + 1) * 512],
                                           gp[:, c, th * 512:(th + 1) * 512],
                                           normw[:, c:c + 1], None, ALU.mult),
                           [t_sg, t_cst], inc=True)
                bank_free[bi] = t_nw
        slab_done[ss] = tok_mm
        t_gp_last = t_nw
    t_gate_mm_last = slab_done[23]
    dbg_dump("gp", gp[:, :, :], [t_gp_last])

    t_cc = P.emit("gpsimd",
                  lambda e: e.collective_compute(
                      "AllGather", ALU.bypass,
                      replica_groups=[[0, 1], [2, 3], [4, 5], [6, 7]],
                      ins=[cc_in.ap()[:, :]], outs=[cc_out.ap()[:, :]]),
                  [cc_dma[h] for h in range(NH)], "CC", 1)

    t_kvld = dma("sync", kvinit[:, :], cc_out.ap()[0:128, :],
                 [t_cc, t_gate_mm_last])
    t_kvfl = vec(lambda e: e.tensor_scalar(kvinit[:, :], kvinit[:, :], flag[:, 0:1],
                                           None, ALU.mult), [t_kvld, t_cst], inc=True)
    dbg_dump("kvinit", kvinit[:, :], [t_kvfl])
    t_qkz = vec(lambda e: e.memset(qks[:, 0, 1, 0:128], 0.0), [], inc=True)
    t_qkz2 = vec(lambda e: e.memset(qks[:, 1, 1, 0:128], 0.0), [], inc=True)

    o_evac = {}
    pass2_last = None
    prev_reader = None
    kvt_free = [None, None]
    qtil_free = [None, None]
    qks_free = [t_qkz, t_qkz2]
    kvld_dma = {}
    kvld_free = [None, None]
    for h in range(NH):
        kvld_dma[h] = dma("sync",
                          kvld[:, h % 2, :, :].rearrange("p a w -> p (a w)"),
                          kvl_d[h * 128:(h + 1) * 128, :],
                          [kvst_dma[NH - 1] if h == 0 else None,
                           kvld_free[h % 2]], sem=f"K{h % 2}")
        t_dg = act(lambda e, h=h: e.activation(diagT[:, :, :], idxt[:, :, :], AF.Exp,
                                               scale=negs[:, h:h + 1]),
                   [t_negs, t_cst, prev_reader], inc=True)
        t_qr = act(lambda e, h=h: e.activation(qdrow[:, :], arrq[:, :], AF.Exp,
                                               scale=negs[0:1, h:h + 1]),
                   [t_cst, prev_reader], inc=True)
        bi_q, bfree_q = next_bank()
        t_qb = mm(fb[bi_q][:, 0:256], ones_r[:, :], qdrow[:, :], True, True,
                  [t_qr, bfree_q], inc=True)
        t_qbe = act(lambda e, bi_q=bi_q: e.activation(qdecb[:, :], fb[bi_q][:, 0:256],
                                                      AF.Copy), [t_qb], inc=True)
        bank_free[bi_q] = t_qbe
        lam_toks = [act(lambda e, h=h, t=t:
                        e.activation(lamt[:, t:t + 1], negs[:, h:h + 1], AF.Exp,
                                     scale=256.0 * t), [t_negs, prev_reader], inc=True)
                    for t in range(1, NB)]
        for t in range(NB):
            tb2 = t % 2
            if t == 0:
                t_kvt = vec(lambda e, h=h, tb2=tb2:
                            e.tensor_copy(kvt[:, tb2, :],
                                          kvinit[:, h * 128:(h + 1) * 128]),
                            [t_kvfl, kvt_free[tb2]], inc=True)
            else:
                t_kvt = vec(lambda e, h=h, t=t, tb2=tb2:
                            e.scalar_tensor_tensor(kvt[:, tb2, :],
                                                   kvinit[:, h * 128:(h + 1) * 128],
                                                   lamt[:, t:t + 1],
                                                   kvld[:, h % 2, t - 1, :],
                                                   ALU.mult, ALU.add),
                            [t_kvfl, lam_toks[t - 1], kvld_dma[h], kvt_free[tb2]],
                            inc=True)
            th = t // 2
            t_qt = vec(lambda e, h=h, t=t, tb2=tb2:
                       e.tensor_tensor(qtil[:, tb2, :],
                                       qf[:, h, t * 256:(t + 1) * 256],
                                       qdecb[:, :], ALU.mult),
                       [t_qbe, silu_evac[("q", h, th)], qtil_free[tb2]], inc=True)
            bi_qk, bfree_qk = next_bank()
            qk_mm = []
            for j in range(2):
                deps = [silu_evac[("k", h, th)], silu_evac[("q", h, th)]]
                if j == 0:
                    deps.append(bfree_qk)
                qk_mm.append(mm(fb[bi_qk][:, j * 256:(j + 1) * 256],
                                kf[:, h, t * 256 + j * 128:t * 256 + (j + 1) * 128],
                                qf[:, h, t * 256:(t + 1) * 256],
                                start=True, stop=True, deps=deps, inc=True))
            t_qke0 = vec(lambda e, bi_qk=bi_qk, tb2=tb2:
                         e.tensor_tensor(qks[:, tb2, 0, :], fb[bi_qk][:, 0:256],
                                         diagT[:, 0, :], ALU.mult),
                         [qk_mm[0], t_dg, qks_free[tb2]], inc=True)
            t_qke1 = vec(lambda e, bi_qk=bi_qk, tb2=tb2:
                         e.tensor_tensor(qks[:, tb2, 1, 128:256],
                                         fb[bi_qk][:, 384:512],
                                         diagT[:, 1, 128:256], ALU.mult),
                         [qk_mm[1], t_dg], inc=True)
            bank_free[bi_qk] = t_qke1
            bi_o, bfree_o = next_bank()
            mm(fb[bi_o][:, 0:256], v_tm[:, 2 * t, h * 128:(h + 1) * 128],
               qks[:, tb2, 0, :], start=True, stop=False,
               deps=[t_qke0, bfree_o, silu_evac[("v", 2 * t, h // 4)]])
            mm(fb[bi_o][:, 128:256], v_tm[:, 2 * t + 1, h * 128:(h + 1) * 128],
               qks[:, tb2, 1, 128:256], start=False, stop=False,
               deps=[t_qke1, silu_evac[("v", 2 * t + 1, h // 4)]])
            tok_o = mm(fb[bi_o][:, 0:256], kvt[:, tb2, :], qtil[:, tb2, :],
                       start=False, stop=True, deps=[t_kvt, t_qt], inc=True)
            kvt_free[tb2] = tok_o
            qtil_free[tb2] = tok_o
            qks_free[tb2] = tok_o
            t_oe = act(lambda e, bi_o=bi_o, h=h, t=t:
                       e.activation(kf[:, h, t * 256:(t + 1) * 256],
                                    fb[bi_o][:, 0:256], AF.Copy),
                       [tok_o, dbg_t.get("k") if (h == 0 and t == 0) else None],
                       inc=True)
            o_evac[(h, t)] = t_oe
            bank_free[bi_o] = t_oe
            pass2_last = t_oe
        prev_reader = pass2_last
        kvld_free[h % 2] = pass2_last

    dbg_dump("o", kf[:, :, :], [pass2_last])
    bi_r0, bfree_r0 = next_bank()
    bi_r1, bfree_r1 = next_bank()
    mm_hist = []
    for c in range(C):
        b = c % 2
        deps = [o_evac[(c, t)] for t in range(NB)] + [pass2_last]
        if len(mm_hist) >= 2:
            deps += mm_hist[-2]
        t_sq = vec(lambda e, c=c, b=b: e.tensor_tensor(sqG[:, b, :], kf[:, c, :],
                                                       kf[:, c, :], ALU.mult),
                   deps, inc=True)
        cur = []
        for th in range(2):
            bi = bi_r0 if th == 0 else bi_r1
            rhs = sqG[:, b, :].rearrange("p (tq t4) -> p t4 tq",
                                         t4=4)[:, 2 * th:2 * th + 2, :]
            deps2 = [t_sq]
            if c == 0:
                deps2.append(bfree_r0 if th == 0 else bfree_r1)
            cur.append(mm(fb[bi][0:1, :], ones_c[:, :], rhs,
                          start=(c == 0), stop=(c == C - 1), deps=deps2, inc=True))
        mm_hist.append(cur)

    r1 = vec(lambda e: e.tensor_scalar(rrow[0:1, 0:512], fb[bi_r0][0:1, :],
                                       1.0 / D, EPS_RMS, ALU.mult, ALU.add),
             mm_hist[-1], inc=True)
    r2 = vec(lambda e: e.tensor_scalar(rrow[0:1, 512:1024], fb[bi_r1][0:1, :],
                                       1.0 / D, EPS_RMS, ALU.mult, ALU.add),
             mm_hist[-1], inc=True)
    r3 = vec(lambda e: e.reciprocal(rrow[0:1, 0:512], rrow[0:1, 0:512]),
             [r1], inc=True)
    t_rm = vec(lambda e: e.reciprocal(rrow[0:1, 512:1024], rrow[0:1, 512:1024]),
               [r2, r3], inc=True)
    t_rs = act(lambda e: e.activation(rrow[0:1, :], rrow[0:1, :], AF.Sqrt), [t_rm],
               inc=True)
    bank_free[bi_r0] = t_rs
    bank_free[bi_r1] = t_rs
    t_rw = dma("sync", rstd_d[0:1, :], rrow[0:1, :], [t_rs])
    t_rc = []
    for gt in range(8):
        t_rc.append(dma("sync", rstdc[:, gt:gt + 1],
                        rstd_d.rearrange("o (gt p) -> p (o gt)", p=128)[:, gt:gt + 1],
                        [t_rw]))

    gz_tok = {}
    for c in range(C):
        for e4 in range(4):
            deps = [o_evac[(c, t)] for t in range(NB)] + [pass2_last]
            gz_tok[(c, e4)] = vec(
                lambda e, c=c, e4=e4: e.tensor_tensor(
                    gz[:, c, e4 * 256:(e4 + 1) * 256],
                    gp[:, c, e4:T:4], kf[:, c, e4:T:4], ALU.mult),
                deps, inc=True)
    t_gz_all = gz_tok[(C - 1, 3)]

    wo_tok = {}
    for s in range(2):
        wo_tok[s] = dma("gpsimd", woB[:, s, :, :].rearrange("p c w -> p (c w)"),
                        wo_r[:, s, :], [pass2_last, dbg_t.get("q")])
    for s in range(2, 4):
        wo_tok[s] = dma("gpsimd", woE[:, s - 2, :, :].rearrange("p c w -> p (c w)"),
                        wo_r[:, s, :], [t_gz_all])
    wo_all = wo_tok[3]

    def wo_res(s):
        return woB[:, s, :, :] if s < 2 else woE[:, s - 2, :, :]

    xin_free = None
    aout_free = [None, None]
    p6_xn2_evac = []
    p6_spill = {}
    pend_tr = []
    p6_last_mm_tok = None
    for gt in range(8):
        gi = gt % 2
        ev_toks2 = []
        for s in range(4):
            bi, bfree = next_bank()
            for kc in range(C):
                deps = ([wo_all, bfree, t_gz_all, aout_free[gi],
                         dbg_t.get("o"), dbg_t.get("kvinit")]
                        if kc == 0 else ())
                tok_mm = mm(fb[bi][:, :], gz[:, kc, gt * 128:(gt + 1) * 128],
                            wo_res(s)[:, kc, :], start=(kc == 0), stop=(kc == C - 1),
                            deps=deps, inc=(kc == C - 1))
            t_ev = vec(lambda e, bi=bi, gi=gi, s=s:
                       e.tensor_copy(aout[:, gi, s * 512:(s + 1) * 512],
                                     fb[bi][:, :]), [tok_mm], inc=True)
            bank_free[bi] = t_ev
            ev_toks2.append(t_ev)
            p6_last_mm_tok = tok_mm
        if True:
            ev_toks = {gi: ev_toks2}
            b = gt % 2
            t_sum, t_ss = ln_stats(aout[:, gi, :], b, sqtC[:, :], ev_toks[gi])

            u1 = vec(lambda e, b=b: e.tensor_scalar(stat[:, b, 2:3],
                                                    stat[:, b, 0:1], 1.0 / D, None,
                                                    ALU.mult), [t_sum], inc=True)
            u2 = vec(lambda e, b=b: e.tensor_scalar(stat[:, b, 3:4],
                                                    stat[:, b, 1:2], 1.0 / D, None,
                                                    ALU.mult), [t_ss], inc=True)
            u3 = vec(lambda e, b=b: e.tensor_tensor(stat[:, b, 4:5],
                                                    stat[:, b, 2:3],
                                                    stat[:, b, 2:3], ALU.mult),
                     [u1], inc=True)
            u4 = vec(lambda e, b=b: e.tensor_tensor(stat[:, b, 4:5],
                                                    stat[:, b, 3:4],
                                                    stat[:, b, 4:5], ALU.subtract),
                     [u2, u3], inc=True)
            u5 = vec(lambda e, b=b, gt=gt: e.tensor_tensor(stat[:, b, 5:6],
                                                           rstdc[:, gt:gt + 1],
                                                           rstdc[:, gt:gt + 1],
                                                           ALU.mult),
                     t_rc + [u2], inc=True)
            u6 = vec(lambda e, b=b: e.tensor_tensor(stat[:, b, 5:6],
                                                    stat[:, b, 4:5],
                                                    stat[:, b, 5:6], ALU.mult),
                     [u4, u5], inc=True)
            u7 = vec(lambda e, b=b: e.tensor_scalar(stat[:, b, 5:6],
                                                    stat[:, b, 5:6], EPS_LN, None,
                                                    ALU.add), [u6], inc=True)
            t_c1 = vec(lambda e, b=b: e.reciprocal(stat[:, b, 5:6],
                                                   stat[:, b, 5:6]), [u7], inc=True)
            t_c2 = act(lambda e, b=b: e.activation(stat[:, b, 5:6], stat[:, b, 5:6],
                                                   AF.Sqrt), [t_c1], inc=True)
            t_c3 = vec(lambda e, b=b, gt=gt:
                       e.tensor_tensor(stat[:, b, 6:7], stat[:, b, 5:6],
                                       rstdc[:, gt:gt + 1], ALU.mult), [t_c2],
                       inc=True)
            t_x = dma("sync", xin[:, 0, :], x_r[:, gt, :], [xin_free, t_gz_all])
            t_a = vec(lambda e, b=b, gi=gi:
                      e.tensor_scalar(aout[:, gi, :], aout[:, gi, :], stat[:, b, 2:3],
                                      stat[:, b, 6:7], ALU.subtract, ALU.mult),
                      [t_c3], inc=True)
            t_a2 = vec(lambda e, gi=gi:
                       e.tensor_tensor(aout[:, gi, :], aout[:, gi, :], xin[:, 0, :],
                                       ALU.add), [t_a, t_x], inc=True)
            xin_free = t_a2
            t_sum2, t_ss2 = ln_stats(aout[:, gi, :], b, sqtC[:, :], [t_a2])
            t_cols2 = cols_ln_emit(b, [t_sum2, t_ss2])
            t_rstd2 = act(lambda e, b=b: e.activation(stat[:, b, 5:6],
                                                      stat[:, b, 5:6], AF.Sqrt),
                          [t_cols2], inc=True)
            t_app2 = vec(lambda e, b=b, gi=gi:
                         e.tensor_scalar(xn2s[:, b, :], aout[:, gi, :],
                                         stat[:, b, 2:3], stat[:, b, 5:6],
                                         ALU.subtract, ALU.mult),
                         [t_rstd2,
                          p6_xn2_evac[-1] if p6_xn2_evac else None],
                         inc=True)
            t_sp = dma("sync",
                       x2sp.rearrange("(tt p) dd -> p tt dd", p=128)[:, gt, :],
                       aout[:, gi, :], [t_app2])
            p6_spill[gt] = t_sp
            aout_free[gi] = t_sp
            for half in range(2):
                bank = tbk[half]
                tdeps = [t_app2, p6_xn2_evac[-1] if p6_xn2_evac else None]
                tok_t = None
                for c8 in range(8):
                    c = half * 8 + c8
                    tok_t = tr(bank[:, c8 * 128:(c8 + 1) * 128],
                               xn2s[:, b, c * 128:(c + 1) * 128],
                               tdeps if c8 == 0 else (), inc=(c8 == 7))
                t_ev = act(lambda e, bank=bank, gt=gt, half=half:
                           e.activation(xn2T[:, half * 8:(half + 1) * 8,
                                             gt * 128:(gt + 1) * 128],
                                        bank[:, :], AF.Copy), [tok_t], inc=True)
                p6_xn2_evac.append(t_ev)
    t_xn2T = p6_xn2_evac[-1]
    p6_mm_last = p6_last_mm_tok

    ring8 = Ring(dw + [rw8], init_tok=p6_mm_last)
    x2r_free = [None, None]
    obuf_free = [None, None]
    sg_free = [None, None]
    out_toks = []
    down_last_prev = None
    for e_i in range(E):
        t_int = None
        for s in range(8):
            ss, slab, tok_ld = ring8.load(gup_r[:, e_i, s, :])
            tokU = None
            for pr in range(2):
                i_blk = 2 * s + pr
                biG, bfreeG = next_bank()
                biU, bfreeU = next_bank()
                tokG = None
                for kc in range(C):
                    dG = ([tok_ld, bfreeG, t_xn2T, dbg_t.get("q"),
                           dbg_t.get("gp")] if kc == 0 else ())
                    tokG = mm(fb[biG][:, 0:256], slab[:, kc, 256 * pr:256 * pr + 128],
                              xn2T[:, kc, e_i * 256:(e_i + 1) * 256],
                              start=(kc == 0), stop=(kc == C - 1), deps=dG,
                              inc=(kc == C - 1))
                for kc in range(C):
                    dU = [bfreeU] if kc == 0 else ()
                    tokU = mm(fb[biU][:, 0:256],
                              slab[:, kc, 256 * pr + 128:256 * (pr + 1)],
                              xn2T[:, kc, e_i * 256:(e_i + 1) * 256],
                              start=(kc == 0), stop=(kc == C - 1), deps=dU,
                              inc=(kc == C - 1))
                sb = i_blk % 2
                t_sil = act(lambda e, biG=biG, sb=sb:
                            e.activation(sgate[:, sb, :], fb[biG][:, 0:256], AF.Silu),
                            [tokG, sg_free[sb]], inc=True)
                t_int = vec(lambda e, biU=biU, sb=sb, i_blk=i_blk:
                            e.tensor_tensor(inter[:, i_blk, :], fb[biU][:, 0:256],
                                            sgate[:, sb, :], ALU.mult),
                            [tokU, t_sil, down_last_prev,
                             pass2_last if (e_i == 0 and i_blk == 0) else None],
                            inc=True)
                bank_free[biG] = t_int
                bank_free[biU] = t_int
                sg_free[sb] = t_int
            ring8.done[ss] = tokU
        inter_last = t_int
        ev_toks = [[], []]
        for s in range(4):
            ss, slab, tok_ld = ring8.load(dwn_r[:, e_i, s, :])
            tok_mm = None
            for gt in range(2):
                bi, bfree = next_bank()
                for kc in range(C):
                    deps = ([tok_ld, bfree, inter_last,
                             obuf_free[gt]] if kc == 0 else ())
                    tok_mm = mm(fb[bi][:, :],
                                inter[:, kc, gt * 128:(gt + 1) * 128],
                                slab[:, kc, :], start=(kc == 0), stop=(kc == C - 1),
                                deps=deps, inc=(kc == C - 1))
                t_ev = vec(lambda e, bi=bi, gt=gt, s=s:
                           e.tensor_copy(obuf[:, gt, s * 512:(s + 1) * 512],
                                         fb[bi][:, :]), [tok_mm], inc=True)
                bank_free[bi] = t_ev
                ev_toks[gt].append(t_ev)
            ring8.done[ss] = tok_mm
            down_last_prev = tok_mm
        for gt in range(2):
            g_glob = e_i * 2 + gt
            b = gt
            t_x2 = dma("sync", x2r[:, b, :],
                       x2sp.rearrange("(tt p) dd -> p tt dd", p=128)[:, g_glob, :],
                       [p6_spill[g_glob], x2r_free[b], pass2_last, p6_mm_last])
            t_sum4, t_ss4 = ln_stats(obuf[:, gt, :], b, sqtC[:, :], ev_toks[gt])
            t_cols4 = cols_ln_emit(b, [t_sum4, t_ss4])
            t_rstd4 = act(lambda e, b=b: e.activation(stat[:, b, 5:6],
                                                      stat[:, b, 5:6], AF.Sqrt),
                          [t_cols4], inc=True)
            t_a = vec(lambda e, b=b, gt=gt:
                      e.tensor_scalar(obuf[:, gt, :], obuf[:, gt, :], stat[:, b, 2:3],
                                      stat[:, b, 5:6], ALU.subtract, ALU.mult),
                      [t_rstd4], inc=True)
            t_a2 = vec(lambda e, b=b, gt=gt:
                       e.tensor_tensor(obuf[:, gt, :], obuf[:, gt, :], x2r[:, b, :],
                                       ALU.add), [t_a, t_x2], inc=True)
            t_out = dma("sync",
                        out_d.rearrange("(tt p) dd -> p tt dd", p=128)[:, g_glob, :],
                        obuf[:, gt, :], [t_a2])
            out_toks.append(t_out)
            obuf_free[gt] = t_out
            x2r_free[b] = t_out

    if _DEBUG:
        out_toks.append(dbg_dump("gz", gz[:, :, :], [t_gz_all]))
        out_toks.append(dbg_dump("rrow", rrow[0:1, :], [t_rs]))
        out_toks.append(dbg_dump("xn2T", xn2T[:, :, :], [t_xn2T]))
        out_toks.append(dbg_dump("x2", x2sp[:, :], [p6_spill[gt2] for gt2 in range(8)]))
        out_toks.append(dbg_dump("kvl", kvl_d[:, :], [kvst_dma[hh] for hh in range(NH)]))
        out_toks.append(dbg_dump("ccin", cc_in.ap()[:, :],
                                 [cc_dma[hh] for hh in range(NH)]))
        out_toks.append(dbg_dump("inter", inter[:, :, :], out_toks[-1:]))
        out_toks.extend([dbg_t[k] for k in ("xn1T", "k", "v", "q", "gp", "kvinit", "o")])
    P.emit("sync", lambda e: e.nop(), out_toks)
    P.emit("gpsimd", lambda e: e.nop(), out_toks)

    from contextlib import ExitStack
    with ExitStack() as es:
        sems = {s: es.enter_context(nc.semaphore(s))
                for s in ("S", "G", "T", "V", "A", "CC", "R0", "R1", "R2",
                          "VB0", "VB1", "K0", "K1")}
        with nc.Block() as block:
            block.sync(lambda e: P.run("sync", e, sems))
            block.gpsimd(lambda e: P.run("gpsimd", e, sems))
            block.tensor(lambda e: P.run("tensor", e, sems))
            block.vector(lambda e: P.run("vector", e, sems))
            block.scalar(lambda e: P.run("scalar", e, sems))
    return nc


_PERM = 4 * (np.arange(1024) % 256) + np.arange(1024) // 256
_IPERM = np.argsort(_PERM)
_NC_CACHE = [None]
_LAST_RES = [None]
_DEBUG = bool(os.environ.get("KDEBUG"))


def kernel(L0_attn_hidden_states, L0_attn_slope_rate, L0_attn_w_qkv, L0_attn_w_gate,
           L0_attn_w_out, L0_attn_norm_w, L0_ffn_gate_up_proj, L0_ffn_down_proj):
    x = np.asarray(L0_attn_hidden_states, np.float32)
    slope = np.asarray(L0_attn_slope_rate, np.float32).reshape(1, NH)
    w_qkv = np.asarray(L0_attn_w_qkv, np.float32)
    w_gate = np.asarray(L0_attn_w_gate, np.float32)
    w_out = np.asarray(L0_attn_w_out, np.float32)
    norm_w = np.asarray(L0_attn_norm_w, np.float32)
    gup = np.asarray(L0_ffn_gate_up_proj, np.float32)
    down = np.asarray(L0_ffn_down_proj, np.float32)

    def slabify(wT, w=256):
        nsl = wT.shape[1] // w
        a = wT.reshape(C, 128, nsl, w)
        return np.ascontiguousarray(a.transpose(2, 1, 0, 3).reshape(nsl * 128, C * w))

    k_cols = np.concatenate([h * 384 + 128 + np.arange(128) for h in range(NH)])
    v_cols = np.concatenate([h * 384 + 256 + np.arange(128) for h in range(NH)])
    q_cols = np.concatenate([h * 384 + np.arange(128) for h in range(NH)])
    wqkvT = slabify(w_qkv.T[:, np.concatenate([k_cols, q_cols])])
    wv_h = slabify(w_qkv.T[:, v_cols], 512)
    wgT = slabify(w_gate.T)
    woT = slabify(w_out.T, 512)
    gup_cols = np.concatenate([np.concatenate([i * 128 + np.arange(128),
                                               2048 + i * 128 + np.arange(128)])
                               for i in range(16)])
    gup_h = np.concatenate([slabify(gup[e][:, gup_cols], 512) for e in range(E)])
    down_h = np.concatenate([slabify(down[e], 512) for e in range(E)])
    slope_bc = np.ascontiguousarray(np.broadcast_to(slope, (128, NH)))
    normw_h = np.ascontiguousarray(norm_w.reshape(C, 128).T)

    if _NC_CACHE[0] is None:
        _NC_CACHE[0] = build_nc()
    nc = _NC_CACHE[0]

    in_maps = []
    for c in range(8):
        b, hf = divmod(c, 2)
        x_sh = np.ascontiguousarray(x[b, hf * T:(hf + 1) * T][_PERM])
        in_maps.append({
            "x_sh": x_sh, "wqkvT": wqkvT, "wv": wv_h, "wgT": wgT,
            "woT": woT,
            "gup_w": gup_h, "down_w": down_h, "slope_b": slope_bc,
            "normw": normw_h,
            "flag": np.full((128, 1), 1.0 if c % 2 else 0.0, np.float32),
        })
    res = run_bass_kernel_spmd(nc, in_maps, core_ids=list(range(8)))
    if _DEBUG:
        _LAST_RES[0] = res
    out = np.empty((4, 2048, D), np.float32)
    for c in range(8):
        b, hf = divmod(c, 2)
        out[b, hf * T:(hf + 1) * T] = np.asarray(res.results[c]["out"])[_IPERM]
    return out


# version 16
# speedup vs baseline: 1.0641x; 1.0641x over previous
import os
import numpy as np
import concourse.bass as bass
import concourse.mybir as mybir
from concourse.bass_utils import run_bass_kernel_spmd

F32 = mybir.dt.float32
BF16 = mybir.dt.bfloat16
AF = mybir.ActivationFunctionType
ALU = mybir.AluOpType
AX = mybir.AxisListType.X

NH, HD, BLK = 16, 128, 256
D = 2048
T = 1024
C = 16
NB = 4
E = 4
EPS_LN = 1e-5
EPS_RMS = 1e-6
KB = 1024

OFF_A = 0 * KB
OFF_B = 32 * KB
OFF_C = 64 * KB
OFF_D = 96 * KB
OFF_E = 128 * KB
OFF_G = 160 * KB
OFF_W = 175 * KB
OFF_S = 191 * KB
ARENA = 196 * KB


class Prog:

    ENGS = ("sync", "gpsimd", "tensor", "vector", "scalar")

    def __init__(self):
        self.ops = {e: [] for e in self.ENGS}
        self.count = {}

    def emit(self, eng, fn, deps=(), inc=None, n=1):
        deps = tuple(d for d in deps if d is not None)
        self.ops[eng].append((fn, deps, inc, n))
        if inc is not None:
            self.count[inc] = self.count.get(inc, 0) + n
            return (inc, self.count[inc])
        return None

    def run(self, engname, eng, sems):
        seen = {}
        for fn, deps, inc, n in self.ops[engname]:
            for s, v in deps:
                if seen.get(s, 0) < v:
                    eng.wait_ge(sems[s], v)
                    seen[s] = v
            ins = fn(eng)
            if inc is not None:
                ins.then_inc(sems[inc], n)


_DEBUG = bool(os.environ.get("KDEBUG"))


def build_nc():
    nc = bass.Bass("TRN2", target_bir_lowering=False)
    dbg_t = {}
    dbg_dram = {}

    x_in = nc.dram_tensor("x_sh", [T, D], F32, kind="ExternalInput").ap()
    wqkvT = nc.dram_tensor("wqkvT", [16 * 128, C * 256], F32, kind="ExternalInput").ap()
    wv = nc.dram_tensor("wv", [4 * 128, C * 512], F32, kind="ExternalInput").ap()
    wgT = nc.dram_tensor("wgT", [8 * 128, C * 256], F32, kind="ExternalInput").ap()
    woT = nc.dram_tensor("woT", [4 * 128, C * 512], F32, kind="ExternalInput").ap()
    gup_w = nc.dram_tensor("gup_w", [E * 8 * 128, C * 512], F32, kind="ExternalInput").ap()
    down_w = nc.dram_tensor("down_w", [E * 4 * 128, C * 512], F32, kind="ExternalInput").ap()
    slope_b = nc.dram_tensor("slope_b", [128, NH], F32, kind="ExternalInput").ap()
    normw_in = nc.dram_tensor("normw", [128, C], F32, kind="ExternalInput").ap()
    flag_in = nc.dram_tensor("flag", [128, 1], F32, kind="ExternalInput").ap()
    out_d = nc.dram_tensor("out", [T, D], F32, kind="ExternalOutput").ap()

    cc_in = nc.dram_tensor("cc_in", [128, NH * HD], F32, kind="Internal")
    cc_out = nc.dram_tensor("cc_out", [256, NH * HD], F32, kind="Internal")
    x2sp = nc.dram_tensor("x2spill", [T, D], F32, kind="Internal").ap()
    rstd_d = nc.dram_tensor("rstd_d", [1, T], F32, kind="Internal").ap()
    kvl_d = nc.dram_tensor("kvloc_d", [NH * 128, 3 * HD], F32, kind="Internal").ap()

    idxt_np = np.empty((128, 2, BLK), np.float32)
    for j in range(2):
        n_idx = (j * 128 + np.arange(128))[:, None]
        m_idx = np.arange(BLK)[None, :]
        idxt_np[:, j, :] = np.where(m_idx >= n_idx,
                                    (m_idx - n_idx).astype(np.float32), 1e9)
    arrq_np = np.arange(1, BLK + 1, dtype=np.float32).reshape(1, BLK)
    arrk_np = np.empty((128, 2), np.float32)
    for j in range(2):
        arrk_np[:, j] = 255.0 - (j * 128 + np.arange(128))
    idxt_d = nc.inline_tensor(idxt_np, "idxt_c").ap()
    arrq_d = nc.inline_tensor(arrq_np, "arrq_c").ap()
    arrk_d = nc.inline_tensor(arrk_np, "arrk_c").ap()
    ident_d = nc.inline_tensor(np.eye(128, dtype=np.float32), "ident_c").ap()
    ones_c_d = nc.inline_tensor(np.ones((128, 1), np.float32), "ones_c_c").ap()
    ones_r_d = nc.inline_tensor(np.ones((1, 128), np.float32), "ones_r_c").ap()

    nc.alloc_sbuf_tensor("arena", [128, ARENA], mybir.dt.uint8)
    base = nc.sbuf_base - ARENA

    def place(name, shape, dtype, off):
        return nc.alloc_sbuf_tensor_at(name, shape, dtype, offset=base + off).ap()

    xn1T = place("xn1T", [128, C, T], BF16, OFF_A)
    kvinit = place("kvinit", [128, NH * HD], F32, OFF_A)
    xn2T = place("xn2T", [128, C, T], BF16, OFF_A)
    qf = place("qf", [128, NH, T], BF16, OFF_B)
    vbuf = [place(f"vb{i}", [128, C, 512], BF16, OFF_B + i * 16 * KB)
            for i in range(2)]
    woB = place("woB", [128, 2, C, 512], BF16, OFF_B)
    inter = place("inter", [128, C, 256], BF16, OFF_B)
    x2r = place("x2r", [128, 2, D], F32, OFF_B + 8 * KB)
    kf = place("kf", [128, NH, T], BF16, OFF_C)
    xin = place("xin", [128, 2, D], F32, OFF_C)
    aout = place("aout", [128, 2, D], F32, OFF_C + 8 * KB)
    xn2s = place("xn2s", [128, 2, D], BF16, OFF_C + 24 * KB)
    sqtC = place("sqtC", [128, 512], F32, OFF_G + 7 * KB)
    v_tm = place("v_tm", [128, 8, D], BF16, OFF_D)
    gz = place("gz", [128, C, T], BF16, OFF_D)
    gp = place("gp", [128, C, T], BF16, OFF_E)
    woE = place("woE", [128, 2, C, 512], BF16, OFF_E)
    obuf = place("obuf", [128, 2, D], F32, OFF_E)
    sgate = place("sgate", [128, 2, 256], F32, OFF_B + 24 * KB)
    kvrun = place("kvrun", [128, 2, HD], F32, OFF_G)
    kvst = place("kvst", [128, 2, 3, HD], F32, OFF_G + 1 * KB)
    kvld = place("kvld", [128, 2, 3, HD], F32, OFF_G + 4 * KB)
    diagT = place("diagT", [128, 2, BLK], F32, OFF_G + 7 * KB)
    qdecb = place("qdecb", [128, BLK], F32, OFF_G + 9 * KB)
    kdec = place("kdec", [128, 2], F32, OFF_G + 10 * KB)
    lamt = place("lamt", [128, NB], F32, OFF_G + 10 * KB + 32)
    qtil = place("qtil", [128, 2, BLK], BF16, OFF_G + 10 * KB + 256)
    qks = place("qks", [128, 2, 2, BLK], BF16, OFF_G + 11 * KB + 256)
    kvt = place("kvt", [128, 2, HD], BF16, OFF_G + 13 * KB + 256)
    ktil = place("ktil", [128, 2, HD], BF16, OFF_G + 13 * KB + 768)
    arrk = place("arrk", [128, 2], F32, OFF_G + 14 * KB + 256)
    sqG = place("sqG", [128, 2, T], BF16, OFF_G)
    rrow = place("rrow", [1, T], F32, OFF_G + 4 * KB)
    wsl = [place(f"wsl{i}", [128, C, 256], BF16, OFF_W + i * 8 * KB) for i in range(2)]
    dw = [place(f"dw{i}", [128, C, 512], BF16, OFF_D + i * 16 * KB) for i in range(2)]
    rw8 = place("rw8", [128, C, 512], BF16, OFF_W)
    ident = place("ident", [128, 128], BF16, OFF_S)
    ones_c = place("ones_c", [128, 1], BF16, OFF_S + 256)
    ones_r = place("ones_r", [1, 128], BF16, OFF_S + 288)
    arrq = place("arrq", [1, BLK], F32, OFF_S + 544)
    qdrow = place("qdrow", [1, BLK], BF16, OFF_S + 1568)
    normw = place("normw", [128, C], F32, OFF_S + 2080)
    slopes = place("slopes", [128, NH], F32, OFF_S + 2144)
    negs = place("negs", [128, NH], F32, OFF_S + 2208)
    flag = place("flag", [128, 1], F32, OFF_S + 2272)
    rstdc = place("rstdc", [128, 8], F32, OFF_S + 2304)
    stat = place("stat", [128, 2, 8], F32, OFF_S + 2336)
    idxt = place("idxt", [128, 2, BLK], F32, OFF_S + 2432)

    fb = [nc.alloc_psum_tensor(f"fb{i}", [128, 512], F32).ap() for i in range(6)]
    tbk = [nc.alloc_psum_tensor(f"tb{i}", [128, 1024], BF16).ap() for i in range(2)]

    P = Prog()

    def dma(q, out, in_, deps=(), sem=None):
        return P.emit(q, lambda e, o=out, i=in_: e.dma_start(out=o, in_=i),
                      deps, sem or ("S" if q == "sync" else "G"), 16)

    def mm(out, lhsT, rhs, start, stop, deps=(), inc=False):
        return P.emit("tensor",
                      lambda e, o=out, l=lhsT, r=rhs, st=start, sp=stop:
                      e.matmul(o, l, r, start=st, stop=sp),
                      deps, "T" if inc else None, 1)

    def tr(out, in_, deps=(), inc=False):
        return P.emit("tensor", lambda e, o=out, i=in_: e.transpose(o, i, ident),
                      deps, "T" if inc else None, 1)

    def act(fn, deps=(), inc=False):
        return P.emit("scalar", fn, deps, "A" if inc else None, 1)

    def vec(fn, deps=(), inc=False):
        return P.emit("vector", fn, deps, "V" if inc else None, 1)

    bank_free = [None] * 6
    grp_i = [0]

    def next_bank():
        i = grp_i[0] % 6
        grp_i[0] += 1
        return i, bank_free[i]

    def dbg_dump(name, src_ap, deps):
        if not _DEBUG:
            return None
        th = nc.dram_tensor("d_" + name, list(src_ap.shape), src_ap.dtype,
                            kind="ExternalOutput")
        dbg_dram[name] = th
        dbg_t[name] = dma("sync", th.ap(), src_ap, deps)
        return dbg_t[name]

    def ln_stats(src, b, sq_scr, deps):
        t_sum = vec(lambda e, src=src, b=b: e.reduce_sum(stat[:, b, 0:1], src, AX),
                    deps, inc=True)
        t_prev = None
        for s in range(4):
            d = list(deps) + [t_prev]
            t_q = act(lambda e, src=src, b=b, s=s, sq=sq_scr:
                      e.activation(sq, src[:, s * 512:(s + 1) * 512], AF.Square,
                                   accum_out=stat[:, b, 1:2] if s == 0
                                   else stat[:, b, 7:8]), d, inc=True)
            if s > 0:
                t_prev = vec(lambda e, b=b: e.tensor_tensor(stat[:, b, 1:2],
                                                            stat[:, b, 1:2],
                                                            stat[:, b, 7:8], ALU.add),
                             [t_q, t_prev], inc=True)
            else:
                t_prev = t_q
        return t_sum, t_prev

    def cols_ln_emit(b, deps, eps=EPS_LN):
        t1 = vec(lambda e: e.tensor_scalar(stat[:, b, 2:3], stat[:, b, 0:1],
                                           1.0 / D, None, ALU.mult), deps, inc=True)
        t2 = vec(lambda e: e.tensor_scalar(stat[:, b, 3:4], stat[:, b, 1:2],
                                           1.0 / D, None, ALU.mult), deps, inc=True)
        t3 = vec(lambda e: e.tensor_tensor(stat[:, b, 4:5], stat[:, b, 2:3],
                                           stat[:, b, 2:3], ALU.mult), [t1], inc=True)
        t4 = vec(lambda e: e.tensor_tensor(stat[:, b, 4:5], stat[:, b, 3:4],
                                           stat[:, b, 4:5], ALU.subtract),
                 [t2, t3], inc=True)
        t5 = vec(lambda e: e.tensor_scalar(stat[:, b, 5:6], stat[:, b, 4:5], eps,
                                           None, ALU.add), [t4], inc=True)
        return vec(lambda e: e.reciprocal(stat[:, b, 5:6], stat[:, b, 5:6]),
                   [t5], inc=True)

    dma("gpsimd", ident[:, :], ident_d[:, :])
    dma("gpsimd", ones_c[:, :], ones_c_d[:, :])
    dma("gpsimd", ones_r[:, :], ones_r_d[:, :])
    dma("gpsimd", idxt[:, :, :], idxt_d[:, :, :])
    dma("gpsimd", arrq[:, :], arrq_d[:, :])
    dma("gpsimd", arrk[:, :], arrk_d[:, :])
    dma("gpsimd", normw[:, :], normw_in[:, :])
    dma("gpsimd", slopes[:, :], slope_b[:, :])
    t_cst = dma("gpsimd", flag[:, :], flag_in[:, :])
    t_negs = vec(lambda e: e.tensor_scalar(negs[:, :], slopes[:, :], -1.0, None,
                                           ALU.mult), [t_cst], inc=True)

    wq_r = wqkvT.rearrange("(s p) w -> p s w", p=128)
    wv_r = wv.rearrange("(s p) w -> p s w", p=128)
    wg_r = wgT.rearrange("(s p) w -> p s w", p=128)
    wo_r = woT.rearrange("(s p) w -> p s w", p=128)
    gup_r = gup_w.rearrange("(ee s p) w -> p ee s w", p=128, ee=E)
    dwn_r = down_w.rearrange("(ee s p) w -> p ee s w", p=128, ee=E)
    slab_done = {}
    slab_loaded = {}
    slab_serial = [0]

    def load_slab(src_ap, extra=()):
        s = slab_serial[0]
        d = list(extra)
        if s >= 2:
            d.append(slab_done.get(s - 2))
        slab_loaded[s] = dma("gpsimd", wsl[s % 2].rearrange("p c w -> p (c w)"),
                             src_ap, d)
        slab_serial[0] = s + 1
        return s, wsl[s % 2]

    class Ring:
        def __init__(self, bufs, init_tok=None, sembase="R"):
            self.bufs = bufs
            self.n = len(bufs)
            self.s = 0
            self.done = {}
            self.loaded = {}
            self.init_tok = init_tok
            self.sembase = sembase

        def load(self, src_ap, extra=()):
            i = self.s
            d = list(extra) + [self.done.get(i - self.n, self.init_tok)]
            buf = self.bufs[i % self.n]
            tok = dma("gpsimd", buf.rearrange("p c w -> p (c w)"), src_ap, d,
                      sem=f"{self.sembase}{i % self.n}")
            self.loaded[i] = tok
            self.s = i + 1
            return i, buf, tok

    x_r = x_in.rearrange("(tt p) dd -> p tt dd", p=128)
    p0_evac = []
    for tt in range(8):
        b = tt % 2
        d = [p0_evac[2 * (tt - 2) + 1]] if tt >= 2 else []
        t_x = dma("sync", xin[:, b, :], x_r[:, tt, :], d)
        t_sum, t_ss = ln_stats(xin[:, b, :], b, sqtC[:, :], [t_x])
        t_cols = cols_ln_emit(b, [t_sum, t_ss])
        t_rstd = act(lambda e, b=b: e.activation(stat[:, b, 5:6], stat[:, b, 5:6],
                                                 AF.Sqrt), [t_cols], inc=True)
        t_app = vec(lambda e, b=b: e.tensor_scalar(xn2s[:, b, :], xin[:, b, :],
                                                   stat[:, b, 2:3], stat[:, b, 5:6],
                                                   ALU.subtract, ALU.mult),
                    [t_rstd], inc=True)
        for half in range(2):
            bank = tbk[half]
            tdeps = [t_app, p0_evac[-1] if p0_evac else None]
            tok_t = None
            for c8 in range(8):
                c = half * 8 + c8
                tok_t = tr(bank[:, c8 * 128:(c8 + 1) * 128],
                           xn2s[:, b, c * 128:(c + 1) * 128],
                           tdeps if c8 == 0 else (), inc=(c8 == 7))
            off = 512 * (tt % 2) + tt // 2
            t_ev = act(lambda e, bank=bank, off=off, half=half:
                       e.activation(xn1T[:, half * 8:(half + 1) * 8,
                                         off:off + 509:4],
                                    bank[:, :], AF.Copy), [tok_t], inc=True)
            p0_evac.append(t_ev)
    t_xn1T = p0_evac[-1]
    dbg_dump("xn1T", xn1T[:, :, :], [t_xn1T])

    silu_evac = {}

    def proj_fm(buf, dst, kind, heads, tok_load):
        tok_mm = None
        for obi, h in enumerate(heads):
            for th in range(2):
                bi, bfree = next_bank()
                for kc in range(C):
                    deps = [tok_load, bfree, t_xn1T] if kc == 0 else ()
                    tok_mm = mm(fb[bi][:, :], buf[:, kc, obi * 128:(obi + 1) * 128],
                                xn1T[:, kc, th * 512:(th + 1) * 512],
                                start=(kc == 0), stop=(kc == C - 1),
                                deps=deps, inc=(kc == C - 1))
                t_ev = act(lambda e, bi=bi, dst=dst, h=h, th=th:
                           e.activation(dst[:, h, th * 512:(th + 1) * 512],
                                        fb[bi][:, :], AF.Silu), [tok_mm], inc=True)
                bank_free[bi] = t_ev
                silu_evac[(kind, h, th)] = t_ev
        return tok_mm

    for s in range(8):
        ss, buf = load_slab(wq_r[:, s, :])
        slab_done[ss] = proj_fm(buf, kf, "k", [2 * s, 2 * s + 1], slab_loaded[ss])
    dbg_dump("k", kf[:, :, :], [silu_evac[("k", NH - 1, 1)]])
    tb_guard = t_xn1T
    vring = Ring(vbuf, sembase="VB")
    for s in range(4):
        vi, buf, tok_ld = vring.load(wv_r[:, s, :])
        tok_mm = None
        for tt in range(8):
            bi, bfree = next_bank()
            for kc in range(C):
                deps = [tok_ld, bfree, t_xn1T] if kc == 0 else ()
                tok_mm = mm(fb[bi][:, :], xn1T[:, kc, tt * 128:(tt + 1) * 128],
                            buf[:, kc, :], start=(kc == 0), stop=(kc == C - 1),
                            deps=deps, inc=(kc == C - 1))
            t_ev = act(lambda e, bi=bi, tt=tt, s=s:
                       e.activation(v_tm[:, tt, 512 * s:512 * (s + 1)],
                                    fb[bi][:, :], AF.Silu), [tok_mm], inc=True)
            bank_free[bi] = t_ev
            silu_evac[("v", tt, s)] = t_ev
        vring.done[vi] = tok_mm
    v_last_mm = tok_mm
    dbg_dump("v", v_tm[:, :, :], [silu_evac[("v", 7, 3)]])

    cc_dma = {}
    kvst_dma = {}
    pass1_prev = None
    for h in range(NH):
        hb = h % 2
        t_lam = act(lambda e, h=h: e.activation(lamt[:, 0:1], negs[:, h:h + 1],
                                                AF.Exp, scale=256.0),
                    [t_negs, pass1_prev], inc=True)
        t_kd = act(lambda e, h=h: e.activation(kdec[:, :], arrk[:, :], AF.Exp,
                                               scale=negs[:, h:h + 1]),
                   [t_negs, t_cst], inc=True)
        tok_kvup = None
        tok_st = None
        for t in range(NB):
            ktoks = []
            for j in range(2):
                bank = tbk[j]
                deps = [silu_evac[("k", h, (t * 256 + j * 128) // 512)], tb_guard]
                tok_t = tr(bank[:, 0:128],
                           kf[:, h, t * 256 + j * 128:t * 256 + (j + 1) * 128],
                           deps, inc=True)
                t_ke = vec(lambda e, j=j, bank=bank:
                           e.tensor_scalar(ktil[:, j, :], bank[:, 0:128],
                                           kdec[:, j:j + 1], None, ALU.mult),
                           [tok_t, t_kd], inc=True)
                tb_guard = t_ke
                ktoks.append(t_ke)
            if t >= 1:
                tok_st = vec(lambda e, t=t, hb=hb:
                             e.tensor_copy(kvst[:, hb, t - 1, :], kvrun[:, hb, :]),
                             [tok_kvup, kvst_dma.get(h - 2)], inc=True)
            bi, bfree = next_bank()
            tok_c = None
            for j in range(2):
                deps = [ktoks[j], silu_evac[("v", 2 * t + j, h // 4)]]
                if j == 0:
                    deps.append(bfree)
                tok_c = mm(fb[bi][:, 0:128], ktil[:, j, :],
                           v_tm[:, 2 * t + j, h * 128:(h + 1) * 128],
                           start=(j == 0), stop=(j == 1), deps=deps, inc=(j == 1))
            if t == 0:
                tok_kvup = vec(lambda e, hb=hb, bi=bi:
                               e.tensor_copy(kvrun[:, hb, :], fb[bi][:, 0:128]),
                               [tok_c, cc_dma.get(h - 2)], inc=True)
            else:
                tok_kvup = vec(lambda e, hb=hb, bi=bi:
                               e.scalar_tensor_tensor(kvrun[:, hb, :], kvrun[:, hb, :],
                                                      lamt[:, 0:1], fb[bi][:, 0:128],
                                                      ALU.mult, ALU.add),
                               [tok_c, t_lam, tok_kvup], inc=True)
            bank_free[bi] = tok_kvup
        kvst_dma[h] = dma("sync", kvl_d[h * 128:(h + 1) * 128, :],
                          kvst[:, hb, :, :].rearrange("p a w -> p (a w)"), [tok_st])
        cc_dma[h] = dma("sync", cc_in.ap()[:, h * 128:(h + 1) * 128],
                        kvrun[:, hb, :], [tok_kvup])
        pass1_prev = tok_kvup

    for s in range(8):
        ss, buf = load_slab(wq_r[:, 8 + s, :])
        slab_done[ss] = proj_fm(buf, qf, "q", [2 * s, 2 * s + 1], slab_loaded[ss])
    dbg_dump("q", qf[:, :, :], [silu_evac[("q", NH - 1, 1)]])

    for s in range(8):
        ss, buf = load_slab(wg_r[:, s, :])
        tok_mm = None
        for obi in range(2):
            c = 2 * s + obi
            for th in range(2):
                bi, bfree = next_bank()
                for kc in range(C):
                    deps = [slab_loaded[ss], bfree, t_xn1T] if kc == 0 else ()
                    tok_mm = mm(fb[bi][:, :], buf[:, kc, obi * 128:(obi + 1) * 128],
                                xn1T[:, kc, th * 512:(th + 1) * 512],
                                start=(kc == 0), stop=(kc == C - 1),
                                deps=deps, inc=(kc == C - 1))
                t_sg = act(lambda e, bi=bi, c=c, th=th:
                           e.activation(gp[:, c, th * 512:(th + 1) * 512],
                                        fb[bi][:, :], AF.Sigmoid), [tok_mm], inc=True)
                t_nw = vec(lambda e, c=c, th=th:
                           e.tensor_scalar(gp[:, c, th * 512:(th + 1) * 512],
                                           gp[:, c, th * 512:(th + 1) * 512],
                                           normw[:, c:c + 1], None, ALU.mult),
                           [t_sg, t_cst], inc=True)
                bank_free[bi] = t_nw
        slab_done[ss] = tok_mm
        t_gp_last = t_nw
    t_gate_mm_last = slab_done[23]
    dbg_dump("gp", gp[:, :, :], [t_gp_last])

    t_cc = P.emit("gpsimd",
                  lambda e: e.collective_compute(
                      "AllGather", ALU.bypass,
                      replica_groups=[[0, 1], [2, 3], [4, 5], [6, 7]],
                      ins=[cc_in.ap()[:, :]], outs=[cc_out.ap()[:, :]]),
                  [cc_dma[h] for h in range(NH)], "CC", 1)

    t_kvld = dma("sync", kvinit[:, :], cc_out.ap()[0:128, :],
                 [t_cc, t_gate_mm_last])
    t_kvfl = vec(lambda e: e.tensor_scalar(kvinit[:, :], kvinit[:, :], flag[:, 0:1],
                                           None, ALU.mult), [t_kvld, t_cst], inc=True)
    dbg_dump("kvinit", kvinit[:, :], [t_kvfl])
    t_qkz = vec(lambda e: e.memset(qks[:, 0, 1, 0:128], 0.0), [], inc=True)
    t_qkz2 = vec(lambda e: e.memset(qks[:, 1, 1, 0:128], 0.0), [], inc=True)

    o_evac = {}
    pass2_last = None
    kvt_free = [None, None]
    qtil_free = [None, None]
    qks_free = [t_qkz, t_qkz2]
    kvld_dma = {}
    kvld_free = [None, None]
    pend_o = None
    last_qke1 = None
    hstate = {}
    for i in range(NH * NB + 1):
        if pend_o is not None:
            (ph, pt, ptb2, pqke0, pqke1, pkvt, pqt, pbio, pbfree_o) = pend_o
            mm(fb[pbio][:, 0:256], v_tm[:, 2 * pt, ph * 128:(ph + 1) * 128],
               qks[:, ptb2, 0, :], start=True, stop=False,
               deps=[pqke0, pbfree_o, silu_evac[("v", 2 * pt, ph // 4)]])
            mm(fb[pbio][:, 128:256], v_tm[:, 2 * pt + 1, ph * 128:(ph + 1) * 128],
               qks[:, ptb2, 1, 128:256], start=False, stop=False,
               deps=[pqke1, silu_evac[("v", 2 * pt + 1, ph // 4)]])
            tok_o = mm(fb[pbio][:, 0:256], kvt[:, ptb2, :], qtil[:, ptb2, :],
                       start=False, stop=True, deps=[pkvt, pqt], inc=True)
            kvt_free[ptb2] = tok_o
            qtil_free[ptb2] = tok_o
            qks_free[ptb2] = tok_o
            t_oe = act(lambda e, pbio=pbio, ph=ph, pt=pt:
                       e.activation(kf[:, ph, pt * 256:(pt + 1) * 256],
                                    fb[pbio][:, 0:256], AF.Copy), [tok_o], inc=True)
            o_evac[(ph, pt)] = t_oe
            bank_free[pbio] = t_oe
            pass2_last = t_oe
            if pt == 3:
                kvld_free[ph % 2] = t_oe
        if i == NH * NB:
            break
        h, t = divmod(i, NB)
        if t == 0:
            kvld_dma[h] = dma("sync",
                              kvld[:, h % 2, :, :].rearrange("p a w -> p (a w)"),
                              kvl_d[h * 128:(h + 1) * 128, :],
                              [kvst_dma[NH - 1] if h == 0 else None,
                               kvld_free[h % 2]], sem=f"K{h % 2}")
            t_dg = act(lambda e, h=h: e.activation(diagT[:, :, :], idxt[:, :, :],
                                                   AF.Exp, scale=negs[:, h:h + 1]),
                       [t_negs, t_cst, last_qke1], inc=True)
            t_qr = act(lambda e, h=h: e.activation(qdrow[:, :], arrq[:, :], AF.Exp,
                                                   scale=negs[0:1, h:h + 1]),
                       [t_cst, last_qke1], inc=True)
            bi_q, bfree_q = next_bank()
            t_qb = mm(fb[bi_q][:, 0:256], ones_r[:, :], qdrow[:, :], True, True,
                      [t_qr, bfree_q], inc=True)
            t_qbe = act(lambda e, bi_q=bi_q:
                        e.activation(qdecb[:, :], fb[bi_q][:, 0:256], AF.Copy),
                        [t_qb], inc=True)
            bank_free[bi_q] = t_qbe
            lam_toks = [act(lambda e, h=h, t2=t2:
                            e.activation(lamt[:, t2:t2 + 1], negs[:, h:h + 1],
                                         AF.Exp, scale=256.0 * t2),
                            [t_negs, last_qke1], inc=True)
                        for t2 in range(1, NB)]
            hstate = {"qbe": t_qbe, "dg": t_dg, "lam": lam_toks}
        tb2 = t % 2
        if t == 0:
            t_kvt = vec(lambda e, h=h, tb2=tb2:
                        e.tensor_copy(kvt[:, tb2, :],
                                      kvinit[:, h * 128:(h + 1) * 128]),
                        [t_kvfl, kvt_free[tb2]], inc=True)
        else:
            t_kvt = vec(lambda e, h=h, t=t, tb2=tb2:
                        e.scalar_tensor_tensor(kvt[:, tb2, :],
                                               kvinit[:, h * 128:(h + 1) * 128],
                                               lamt[:, t:t + 1],
                                               kvld[:, h % 2, t - 1, :],
                                               ALU.mult, ALU.add),
                        [t_kvfl, hstate["lam"][t - 1], kvld_dma[h], kvt_free[tb2]],
                        inc=True)
        th = t // 2
        t_qt = vec(lambda e, h=h, t=t, tb2=tb2:
                   e.tensor_tensor(qtil[:, tb2, :],
                                   qf[:, h, t * 256:(t + 1) * 256],
                                   qdecb[:, :], ALU.mult),
                   [hstate["qbe"], silu_evac[("q", h, th)], qtil_free[tb2]],
                   inc=True)
        bi_qk, bfree_qk = next_bank()
        qk_mm = []
        for j in range(2):
            deps = [silu_evac[("k", h, th)], silu_evac[("q", h, th)]]
            if j == 0:
                deps.append(bfree_qk)
            qk_mm.append(mm(fb[bi_qk][:, j * 256:(j + 1) * 256],
                            kf[:, h, t * 256 + j * 128:t * 256 + (j + 1) * 128],
                            qf[:, h, t * 256:(t + 1) * 256],
                            start=True, stop=True, deps=deps, inc=True))
        t_qke0 = vec(lambda e, bi_qk=bi_qk, tb2=tb2:
                     e.tensor_tensor(qks[:, tb2, 0, :], fb[bi_qk][:, 0:256],
                                     diagT[:, 0, :], ALU.mult),
                     [qk_mm[0], hstate["dg"], qks_free[tb2]], inc=True)
        t_qke1 = vec(lambda e, bi_qk=bi_qk, tb2=tb2:
                     e.tensor_tensor(qks[:, tb2, 1, 128:256],
                                     fb[bi_qk][:, 384:512],
                                     diagT[:, 1, 128:256], ALU.mult),
                     [qk_mm[1], hstate["dg"]], inc=True)
        bank_free[bi_qk] = t_qke1
        last_qke1 = t_qke1
        bi_o, bfree_o = next_bank()
        pend_o = (h, t, tb2, t_qke0, t_qke1, t_kvt, t_qt, bi_o, bfree_o)

    dbg_dump("o", kf[:, :, :], [pass2_last])
    bi_r0, bfree_r0 = next_bank()
    bi_r1, bfree_r1 = next_bank()
    mm_hist = []
    for c in range(C):
        b = c % 2
        deps = [o_evac[(c, t)] for t in range(NB)] + [pass2_last]
        if len(mm_hist) >= 2:
            deps += mm_hist[-2]
        t_sq = vec(lambda e, c=c, b=b: e.tensor_tensor(sqG[:, b, :], kf[:, c, :],
                                                       kf[:, c, :], ALU.mult),
                   deps, inc=True)
        cur = []
        for th in range(2):
            bi = bi_r0 if th == 0 else bi_r1
            rhs = sqG[:, b, :].rearrange("p (tq t4) -> p t4 tq",
                                         t4=4)[:, 2 * th:2 * th + 2, :]
            deps2 = [t_sq]
            if c == 0:
                deps2.append(bfree_r0 if th == 0 else bfree_r1)
            cur.append(mm(fb[bi][0:1, :], ones_c[:, :], rhs,
                          start=(c == 0), stop=(c == C - 1), deps=deps2, inc=True))
        mm_hist.append(cur)

    r1 = vec(lambda e: e.tensor_scalar(rrow[0:1, 0:512], fb[bi_r0][0:1, :],
                                       1.0 / D, EPS_RMS, ALU.mult, ALU.add),
             mm_hist[-1], inc=True)
    r2 = vec(lambda e: e.tensor_scalar(rrow[0:1, 512:1024], fb[bi_r1][0:1, :],
                                       1.0 / D, EPS_RMS, ALU.mult, ALU.add),
             mm_hist[-1], inc=True)
    r3 = vec(lambda e: e.reciprocal(rrow[0:1, 0:512], rrow[0:1, 0:512]),
             [r1], inc=True)
    t_rm = vec(lambda e: e.reciprocal(rrow[0:1, 512:1024], rrow[0:1, 512:1024]),
               [r2, r3], inc=True)
    t_rs = act(lambda e: e.activation(rrow[0:1, :], rrow[0:1, :], AF.Sqrt), [t_rm],
               inc=True)
    bank_free[bi_r0] = t_rs
    bank_free[bi_r1] = t_rs
    t_rw = dma("sync", rstd_d[0:1, :], rrow[0:1, :], [t_rs])
    t_rc = []
    for gt in range(8):
        t_rc.append(dma("sync", rstdc[:, gt:gt + 1],
                        rstd_d.rearrange("o (gt p) -> p (o gt)", p=128)[:, gt:gt + 1],
                        [t_rw]))

    gz_tok = {}
    for c in range(C):
        for e4 in range(4):
            deps = [o_evac[(c, t)] for t in range(NB)] + [pass2_last]
            gz_tok[(c, e4)] = vec(
                lambda e, c=c, e4=e4: e.tensor_tensor(
                    gz[:, c, e4 * 256:(e4 + 1) * 256],
                    gp[:, c, e4:T:4], kf[:, c, e4:T:4], ALU.mult),
                deps, inc=True)
    t_gz_all = gz_tok[(C - 1, 3)]

    wo_tok = {}
    for s in range(2):
        wo_tok[s] = dma("gpsimd", woB[:, s, :, :].rearrange("p c w -> p (c w)"),
                        wo_r[:, s, :], [pass2_last, dbg_t.get("q")])
    for s in range(2, 4):
        wo_tok[s] = dma("gpsimd", woE[:, s - 2, :, :].rearrange("p c w -> p (c w)"),
                        wo_r[:, s, :], [t_gz_all])
    wo_all = wo_tok[3]

    def wo_res(s):
        return woB[:, s, :, :] if s < 2 else woE[:, s - 2, :, :]

    xin_free = None
    aout_free = [None, None]
    p6_xn2_evac = []
    p6_spill = {}
    pend_tr = []
    p6_last_mm_tok = None
    for gt in range(8):
        gi = gt % 2
        ev_toks2 = []
        for s in range(4):
            bi, bfree = next_bank()
            for kc in range(C):
                deps = ([wo_all, bfree, t_gz_all, aout_free[gi],
                         dbg_t.get("o"), dbg_t.get("kvinit")]
                        if kc == 0 else ())
                tok_mm = mm(fb[bi][:, :], gz[:, kc, gt * 128:(gt + 1) * 128],
                            wo_res(s)[:, kc, :], start=(kc == 0), stop=(kc == C - 1),
                            deps=deps, inc=(kc == C - 1))
            t_ev = vec(lambda e, bi=bi, gi=gi, s=s:
                       e.tensor_copy(aout[:, gi, s * 512:(s + 1) * 512],
                                     fb[bi][:, :]), [tok_mm], inc=True)
            bank_free[bi] = t_ev
            ev_toks2.append(t_ev)
            p6_last_mm_tok = tok_mm
        if True:
            ev_toks = {gi: ev_toks2}
            b = gt % 2
            t_sum, t_ss = ln_stats(aout[:, gi, :], b, sqtC[:, :], ev_toks[gi])

            u1 = vec(lambda e, b=b: e.tensor_scalar(stat[:, b, 2:3],
                                                    stat[:, b, 0:1], 1.0 / D, None,
                                                    ALU.mult), [t_sum], inc=True)
            u2 = vec(lambda e, b=b: e.tensor_scalar(stat[:, b, 3:4],
                                                    stat[:, b, 1:2], 1.0 / D, None,
                                                    ALU.mult), [t_ss], inc=True)
            u3 = vec(lambda e, b=b: e.tensor_tensor(stat[:, b, 4:5],
                                                    stat[:, b, 2:3],
                                                    stat[:, b, 2:3], ALU.mult),
                     [u1], inc=True)
            u4 = vec(lambda e, b=b: e.tensor_tensor(stat[:, b, 4:5],
                                                    stat[:, b, 3:4],
                                                    stat[:, b, 4:5], ALU.subtract),
                     [u2, u3], inc=True)
            u5 = vec(lambda e, b=b, gt=gt: e.tensor_tensor(stat[:, b, 5:6],
                                                           rstdc[:, gt:gt + 1],
                                                           rstdc[:, gt:gt + 1],
                                                           ALU.mult),
                     t_rc + [u2], inc=True)
            u6 = vec(lambda e, b=b: e.tensor_tensor(stat[:, b, 5:6],
                                                    stat[:, b, 4:5],
                                                    stat[:, b, 5:6], ALU.mult),
                     [u4, u5], inc=True)
            u7 = vec(lambda e, b=b: e.tensor_scalar(stat[:, b, 5:6],
                                                    stat[:, b, 5:6], EPS_LN, None,
                                                    ALU.add), [u6], inc=True)
            t_c1 = vec(lambda e, b=b: e.reciprocal(stat[:, b, 5:6],
                                                   stat[:, b, 5:6]), [u7], inc=True)
            t_c2 = act(lambda e, b=b: e.activation(stat[:, b, 5:6], stat[:, b, 5:6],
                                                   AF.Sqrt), [t_c1], inc=True)
            t_c3 = vec(lambda e, b=b, gt=gt:
                       e.tensor_tensor(stat[:, b, 6:7], stat[:, b, 5:6],
                                       rstdc[:, gt:gt + 1], ALU.mult), [t_c2],
                       inc=True)
            t_x = dma("sync", xin[:, 0, :], x_r[:, gt, :], [xin_free, t_gz_all])
            t_a = vec(lambda e, b=b, gi=gi:
                      e.tensor_scalar(aout[:, gi, :], aout[:, gi, :], stat[:, b, 2:3],
                                      stat[:, b, 6:7], ALU.subtract, ALU.mult),
                      [t_c3], inc=True)
            t_a2 = vec(lambda e, gi=gi:
                       e.tensor_tensor(aout[:, gi, :], aout[:, gi, :], xin[:, 0, :],
                                       ALU.add), [t_a, t_x], inc=True)
            xin_free = t_a2
            t_sum2, t_ss2 = ln_stats(aout[:, gi, :], b, sqtC[:, :], [t_a2])
            t_cols2 = cols_ln_emit(b, [t_sum2, t_ss2])
            t_rstd2 = act(lambda e, b=b: e.activation(stat[:, b, 5:6],
                                                      stat[:, b, 5:6], AF.Sqrt),
                          [t_cols2], inc=True)
            t_app2 = vec(lambda e, b=b, gi=gi:
                         e.tensor_scalar(xn2s[:, b, :], aout[:, gi, :],
                                         stat[:, b, 2:3], stat[:, b, 5:6],
                                         ALU.subtract, ALU.mult),
                         [t_rstd2,
                          p6_xn2_evac[-1] if p6_xn2_evac else None],
                         inc=True)
            t_sp = dma("sync",
                       x2sp.rearrange("(tt p) dd -> p tt dd", p=128)[:, gt, :],
                       aout[:, gi, :], [t_app2])
            p6_spill[gt] = t_sp
            aout_free[gi] = t_sp
            for half in range(2):
                bank = tbk[half]
                tdeps = [t_app2, p6_xn2_evac[-1] if p6_xn2_evac else None]
                tok_t = None
                for c8 in range(8):
                    c = half * 8 + c8
                    tok_t = tr(bank[:, c8 * 128:(c8 + 1) * 128],
                               xn2s[:, b, c * 128:(c + 1) * 128],
                               tdeps if c8 == 0 else (), inc=(c8 == 7))
                t_ev = act(lambda e, bank=bank, gt=gt, half=half:
                           e.activation(xn2T[:, half * 8:(half + 1) * 8,
                                             gt * 128:(gt + 1) * 128],
                                        bank[:, :], AF.Copy), [tok_t], inc=True)
                p6_xn2_evac.append(t_ev)
    t_xn2T = p6_xn2_evac[-1]
    p6_mm_last = p6_last_mm_tok

    ring8 = Ring(dw + [rw8], init_tok=p6_mm_last)
    x2r_free = [None, None]
    obuf_free = [None, None]
    sg_free = [None, None]
    out_toks = []
    down_last_prev = None
    for e_i in range(E):
        t_int = None
        for s in range(8):
            ss, slab, tok_ld = ring8.load(gup_r[:, e_i, s, :])
            tokU = None
            for pr in range(2):
                i_blk = 2 * s + pr
                biG, bfreeG = next_bank()
                biU, bfreeU = next_bank()
                tokG = None
                for kc in range(C):
                    dG = ([tok_ld, bfreeG, t_xn2T, dbg_t.get("q"),
                           dbg_t.get("gp")] if kc == 0 else ())
                    tokG = mm(fb[biG][:, 0:256], slab[:, kc, 256 * pr:256 * pr + 128],
                              xn2T[:, kc, e_i * 256:(e_i + 1) * 256],
                              start=(kc == 0), stop=(kc == C - 1), deps=dG,
                              inc=(kc == C - 1))
                for kc in range(C):
                    dU = [bfreeU] if kc == 0 else ()
                    tokU = mm(fb[biU][:, 0:256],
                              slab[:, kc, 256 * pr + 128:256 * (pr + 1)],
                              xn2T[:, kc, e_i * 256:(e_i + 1) * 256],
                              start=(kc == 0), stop=(kc == C - 1), deps=dU,
                              inc=(kc == C - 1))
                sb = i_blk % 2
                t_sil = act(lambda e, biG=biG, sb=sb:
                            e.activation(sgate[:, sb, :], fb[biG][:, 0:256], AF.Silu),
                            [tokG, sg_free[sb]], inc=True)
                t_int = vec(lambda e, biU=biU, sb=sb, i_blk=i_blk:
                            e.tensor_tensor(inter[:, i_blk, :], fb[biU][:, 0:256],
                                            sgate[:, sb, :], ALU.mult),
                            [tokU, t_sil, down_last_prev,
                             pass2_last if (e_i == 0 and i_blk == 0) else None],
                            inc=True)
                bank_free[biG] = t_int
                bank_free[biU] = t_int
                sg_free[sb] = t_int
            ring8.done[ss] = tokU
        inter_last = t_int
        ev_toks = [[], []]
        for s in range(4):
            ss, slab, tok_ld = ring8.load(dwn_r[:, e_i, s, :])
            tok_mm = None
            for gt in range(2):
                bi, bfree = next_bank()
                for kc in range(C):
                    deps = ([tok_ld, bfree, inter_last,
                             obuf_free[gt]] if kc == 0 else ())
                    tok_mm = mm(fb[bi][:, :],
                                inter[:, kc, gt * 128:(gt + 1) * 128],
                                slab[:, kc, :], start=(kc == 0), stop=(kc == C - 1),
                                deps=deps, inc=(kc == C - 1))
                t_ev = vec(lambda e, bi=bi, gt=gt, s=s:
                           e.tensor_copy(obuf[:, gt, s * 512:(s + 1) * 512],
                                         fb[bi][:, :]), [tok_mm], inc=True)
                bank_free[bi] = t_ev
                ev_toks[gt].append(t_ev)
            ring8.done[ss] = tok_mm
            down_last_prev = tok_mm
        for gt in range(2):
            g_glob = e_i * 2 + gt
            b = gt
            t_x2 = dma("sync", x2r[:, b, :],
                       x2sp.rearrange("(tt p) dd -> p tt dd", p=128)[:, g_glob, :],
                       [p6_spill[g_glob], x2r_free[b], pass2_last, p6_mm_last])
            t_sum4, t_ss4 = ln_stats(obuf[:, gt, :], b, sqtC[:, :], ev_toks[gt])
            t_cols4 = cols_ln_emit(b, [t_sum4, t_ss4])
            t_rstd4 = act(lambda e, b=b: e.activation(stat[:, b, 5:6],
                                                      stat[:, b, 5:6], AF.Sqrt),
                          [t_cols4], inc=True)
            t_a = vec(lambda e, b=b, gt=gt:
                      e.tensor_scalar(obuf[:, gt, :], obuf[:, gt, :], stat[:, b, 2:3],
                                      stat[:, b, 5:6], ALU.subtract, ALU.mult),
                      [t_rstd4], inc=True)
            t_a2 = vec(lambda e, b=b, gt=gt:
                       e.tensor_tensor(obuf[:, gt, :], obuf[:, gt, :], x2r[:, b, :],
                                       ALU.add), [t_a, t_x2], inc=True)
            t_out = dma("sync",
                        out_d.rearrange("(tt p) dd -> p tt dd", p=128)[:, g_glob, :],
                        obuf[:, gt, :], [t_a2])
            out_toks.append(t_out)
            obuf_free[gt] = t_out
            x2r_free[b] = t_out

    if _DEBUG:
        out_toks.append(dbg_dump("gz", gz[:, :, :], [t_gz_all]))
        out_toks.append(dbg_dump("rrow", rrow[0:1, :], [t_rs]))
        out_toks.append(dbg_dump("xn2T", xn2T[:, :, :], [t_xn2T]))
        out_toks.append(dbg_dump("x2", x2sp[:, :], [p6_spill[gt2] for gt2 in range(8)]))
        out_toks.append(dbg_dump("kvl", kvl_d[:, :], [kvst_dma[hh] for hh in range(NH)]))
        out_toks.append(dbg_dump("ccin", cc_in.ap()[:, :],
                                 [cc_dma[hh] for hh in range(NH)]))
        out_toks.append(dbg_dump("inter", inter[:, :, :], out_toks[-1:]))
        out_toks.extend([dbg_t[k] for k in ("xn1T", "k", "v", "q", "gp", "kvinit", "o")])
    P.emit("sync", lambda e: e.nop(), out_toks)
    P.emit("gpsimd", lambda e: e.nop(), out_toks)

    from contextlib import ExitStack
    with ExitStack() as es:
        sems = {s: es.enter_context(nc.semaphore(s))
                for s in ("S", "G", "T", "V", "A", "CC", "R0", "R1", "R2",
                          "VB0", "VB1", "K0", "K1")}
        with nc.Block() as block:
            block.sync(lambda e: P.run("sync", e, sems))
            block.gpsimd(lambda e: P.run("gpsimd", e, sems))
            block.tensor(lambda e: P.run("tensor", e, sems))
            block.vector(lambda e: P.run("vector", e, sems))
            block.scalar(lambda e: P.run("scalar", e, sems))
    return nc


_PERM = 4 * (np.arange(1024) % 256) + np.arange(1024) // 256
_IPERM = np.argsort(_PERM)
_NC_CACHE = [None]
_LAST_RES = [None]
_DEBUG = bool(os.environ.get("KDEBUG"))


def kernel(L0_attn_hidden_states, L0_attn_slope_rate, L0_attn_w_qkv, L0_attn_w_gate,
           L0_attn_w_out, L0_attn_norm_w, L0_ffn_gate_up_proj, L0_ffn_down_proj):
    x = np.asarray(L0_attn_hidden_states, np.float32)
    slope = np.asarray(L0_attn_slope_rate, np.float32).reshape(1, NH)
    w_qkv = np.asarray(L0_attn_w_qkv, np.float32)
    w_gate = np.asarray(L0_attn_w_gate, np.float32)
    w_out = np.asarray(L0_attn_w_out, np.float32)
    norm_w = np.asarray(L0_attn_norm_w, np.float32)
    gup = np.asarray(L0_ffn_gate_up_proj, np.float32)
    down = np.asarray(L0_ffn_down_proj, np.float32)

    def slabify(wT, w=256):
        nsl = wT.shape[1] // w
        a = wT.reshape(C, 128, nsl, w)
        return np.ascontiguousarray(a.transpose(2, 1, 0, 3).reshape(nsl * 128, C * w))

    k_cols = np.concatenate([h * 384 + 128 + np.arange(128) for h in range(NH)])
    v_cols = np.concatenate([h * 384 + 256 + np.arange(128) for h in range(NH)])
    q_cols = np.concatenate([h * 384 + np.arange(128) for h in range(NH)])
    wqkvT = slabify(w_qkv.T[:, np.concatenate([k_cols, q_cols])])
    wv_h = slabify(w_qkv.T[:, v_cols], 512)
    wgT = slabify(w_gate.T)
    woT = slabify(w_out.T, 512)
    gup_cols = np.concatenate([np.concatenate([i * 128 + np.arange(128),
                                               2048 + i * 128 + np.arange(128)])
                               for i in range(16)])
    gup_h = np.concatenate([slabify(gup[e][:, gup_cols], 512) for e in range(E)])
    down_h = np.concatenate([slabify(down[e], 512) for e in range(E)])
    slope_bc = np.ascontiguousarray(np.broadcast_to(slope, (128, NH)))
    normw_h = np.ascontiguousarray(norm_w.reshape(C, 128).T)

    if _NC_CACHE[0] is None:
        _NC_CACHE[0] = build_nc()
    nc = _NC_CACHE[0]

    in_maps = []
    for c in range(8):
        b, hf = divmod(c, 2)
        x_sh = np.ascontiguousarray(x[b, hf * T:(hf + 1) * T][_PERM])
        in_maps.append({
            "x_sh": x_sh, "wqkvT": wqkvT, "wv": wv_h, "wgT": wgT,
            "woT": woT,
            "gup_w": gup_h, "down_w": down_h, "slope_b": slope_bc,
            "normw": normw_h,
            "flag": np.full((128, 1), 1.0 if c % 2 else 0.0, np.float32),
        })
    res = run_bass_kernel_spmd(nc, in_maps, core_ids=list(range(8)))
    if _DEBUG:
        _LAST_RES[0] = res
    out = np.empty((4, 2048, D), np.float32)
    for c in range(8):
        b, hf = divmod(c, 2)
        out[b, hf * T:(hf + 1) * T] = np.asarray(res.results[c]["out"])[_IPERM]
    return out


# version 18
# speedup vs baseline: 1.0683x; 1.0040x over previous
import os
import numpy as np
import concourse.bass as bass
import concourse.mybir as mybir
from concourse.bass_utils import run_bass_kernel_spmd

F32 = mybir.dt.float32
BF16 = mybir.dt.bfloat16
AF = mybir.ActivationFunctionType
ALU = mybir.AluOpType
AX = mybir.AxisListType.X

NH, HD, BLK = 16, 128, 256
D = 2048
T = 1024
C = 16
NB = 4
E = 4
EPS_LN = 1e-5
EPS_RMS = 1e-6
KB = 1024

OFF_A = 0 * KB
OFF_B = 32 * KB
OFF_C = 64 * KB
OFF_D = 96 * KB
OFF_E = 128 * KB
OFF_G = 160 * KB
OFF_W = 175 * KB
OFF_S = 191 * KB
ARENA = 196 * KB


class Prog:

    ENGS = ("sync", "gpsimd", "tensor", "vector", "scalar")

    def __init__(self):
        self.ops = {e: [] for e in self.ENGS}
        self.count = {}

    def emit(self, eng, fn, deps=(), inc=None, n=1):
        deps = tuple(d for d in deps if d is not None)
        self.ops[eng].append((fn, deps, inc, n))
        if inc is not None:
            self.count[inc] = self.count.get(inc, 0) + n
            return (inc, self.count[inc])
        return None

    def run(self, engname, eng, sems):
        seen = {}
        for fn, deps, inc, n in self.ops[engname]:
            for s, v in deps:
                if seen.get(s, 0) < v:
                    eng.wait_ge(sems[s], v)
                    seen[s] = v
            ins = fn(eng)
            if inc is not None:
                ins.then_inc(sems[inc], n)


_DEBUG = bool(os.environ.get("KDEBUG"))


def build_nc():
    nc = bass.Bass("TRN2", target_bir_lowering=False)
    dbg_t = {}
    dbg_dram = {}

    x_in = nc.dram_tensor("x_sh", [T, D], F32, kind="ExternalInput").ap()
    wqkvT = nc.dram_tensor("wqkvT", [16 * 128, C * 256], F32, kind="ExternalInput").ap()
    wv = nc.dram_tensor("wv", [4 * 128, C * 512], F32, kind="ExternalInput").ap()
    wgT = nc.dram_tensor("wgT", [8 * 128, C * 256], F32, kind="ExternalInput").ap()
    woT = nc.dram_tensor("woT", [4 * 128, C * 512], F32, kind="ExternalInput").ap()
    gup_w = nc.dram_tensor("gup_w", [E * 8 * 128, C * 512], F32, kind="ExternalInput").ap()
    down_w = nc.dram_tensor("down_w", [E * 4 * 128, C * 512], F32, kind="ExternalInput").ap()
    slope_b = nc.dram_tensor("slope_b", [128, NH], F32, kind="ExternalInput").ap()
    normw_in = nc.dram_tensor("normw", [128, C], F32, kind="ExternalInput").ap()
    flag_in = nc.dram_tensor("flag", [128, 1], F32, kind="ExternalInput").ap()
    out_d = nc.dram_tensor("out", [T, D], F32, kind="ExternalOutput").ap()

    cc_in = nc.dram_tensor("cc_in", [128, NH * HD], F32, kind="Internal")
    cc_out = nc.dram_tensor("cc_out", [256, NH * HD], F32, kind="Internal")
    x2sp = nc.dram_tensor("x2spill", [T, D], F32, kind="Internal").ap()
    rstd_d = nc.dram_tensor("rstd_d", [1, T], F32, kind="Internal").ap()
    kvl_d = nc.dram_tensor("kvloc_d", [NH * 128, 3 * HD], F32, kind="Internal").ap()

    idxt_np = np.empty((128, 2, BLK), np.float32)
    for j in range(2):
        n_idx = (j * 128 + np.arange(128))[:, None]
        m_idx = np.arange(BLK)[None, :]
        idxt_np[:, j, :] = np.where(m_idx >= n_idx,
                                    (m_idx - n_idx).astype(np.float32), 1e9)
    arrq_np = np.arange(1, BLK + 1, dtype=np.float32).reshape(1, BLK)
    arrk_np = np.empty((128, 2), np.float32)
    for j in range(2):
        arrk_np[:, j] = 255.0 - (j * 128 + np.arange(128))
    idxt_d = nc.inline_tensor(idxt_np, "idxt_c").ap()
    arrq_d = nc.inline_tensor(arrq_np, "arrq_c").ap()
    arrk_d = nc.inline_tensor(arrk_np, "arrk_c").ap()
    ident_d = nc.inline_tensor(np.eye(128, dtype=np.float32), "ident_c").ap()
    ones_c_d = nc.inline_tensor(np.ones((128, 1), np.float32), "ones_c_c").ap()
    ones_r_d = nc.inline_tensor(np.ones((1, 128), np.float32), "ones_r_c").ap()

    nc.alloc_sbuf_tensor("arena", [128, ARENA], mybir.dt.uint8)
    base = nc.sbuf_base - ARENA

    def place(name, shape, dtype, off):
        return nc.alloc_sbuf_tensor_at(name, shape, dtype, offset=base + off).ap()

    xn1T = place("xn1T", [128, C, T], BF16, OFF_A)
    kvinit = place("kvinit", [128, NH * HD], F32, OFF_A)
    xn2T = place("xn2T", [128, C, T], BF16, OFF_A)
    qf = place("qf", [128, NH, T], BF16, OFF_B)
    vbuf = [place(f"vb{i}", [128, C, 512], BF16, OFF_B + i * 16 * KB)
            for i in range(2)]
    woB = place("woB", [128, 2, C, 512], BF16, OFF_B)
    inter = place("inter", [128, C, 256], BF16, OFF_B)
    x2r = place("x2r", [128, 2, D], F32, OFF_B + 8 * KB)
    kf = place("kf", [128, NH, T], BF16, OFF_C)
    xin = place("xin", [128, 2, D], F32, OFF_C)
    aout = place("aout", [128, 2, D], F32, OFF_C + 8 * KB)
    xn2s = place("xn2s", [128, 2, D], BF16, OFF_C + 24 * KB)
    sqtC = place("sqtC", [128, 512], F32, OFF_G + 7 * KB)
    v_tm = place("v_tm", [128, 8, D], BF16, OFF_D)
    gz = place("gz", [128, C, T], BF16, OFF_D)
    gp = place("gp", [128, C, T], BF16, OFF_E)
    woE = place("woE", [128, 2, C, 512], BF16, OFF_E)
    obuf = place("obuf", [128, 2, D], F32, OFF_E)
    sgate = place("sgate", [128, 2, 256], F32, OFF_B + 24 * KB)
    kvrun = place("kvrun", [128, 2, HD], F32, OFF_G)
    kvst = place("kvst", [128, 2, 3, HD], F32, OFF_G + 1 * KB)
    kvld = place("kvld", [128, 2, 3, HD], F32, OFF_G + 4 * KB)
    diagT = place("diagT", [128, 2, BLK], F32, OFF_G + 7 * KB)
    qdecb = place("qdecb", [128, BLK], F32, OFF_G + 9 * KB)
    kdec = place("kdec", [128, 2], F32, OFF_G + 10 * KB)
    lamt = place("lamt", [128, NB], F32, OFF_G + 10 * KB + 32)
    qtil = place("qtil", [128, 2, BLK], BF16, OFF_G + 10 * KB + 256)
    qks = place("qks", [128, 2, 2, BLK], BF16, OFF_G + 11 * KB + 256)
    kvt = place("kvt", [128, 2, HD], BF16, OFF_G + 13 * KB + 256)
    ktil = place("ktil", [128, 2, HD], BF16, OFF_G + 13 * KB + 768)
    arrk = place("arrk", [128, 2], F32, OFF_G + 14 * KB + 256)
    sqG = place("sqG", [128, 2, T], BF16, OFF_G)
    rrow = place("rrow", [1, T], F32, OFF_G + 4 * KB)
    wsl = [place(f"wsl{i}", [128, C, 256], BF16, OFF_W + i * 8 * KB) for i in range(2)]
    dw = [place(f"dw{i}", [128, C, 512], BF16, OFF_D + i * 16 * KB) for i in range(2)]
    rw8 = place("rw8", [128, C, 512], BF16, OFF_W)
    ident = place("ident", [128, 128], BF16, OFF_S)
    ones_c = place("ones_c", [128, 1], BF16, OFF_S + 256)
    ones_r = place("ones_r", [1, 128], BF16, OFF_S + 288)
    arrq = place("arrq", [1, BLK], F32, OFF_S + 544)
    qdrow = place("qdrow", [1, BLK], BF16, OFF_S + 1568)
    normw = place("normw", [128, C], F32, OFF_S + 2080)
    slopes = place("slopes", [128, NH], F32, OFF_S + 2144)
    negs = place("negs", [128, NH], F32, OFF_S + 2208)
    flag = place("flag", [128, 1], F32, OFF_S + 2272)
    rstdc = place("rstdc", [128, 8], F32, OFF_S + 2304)
    stat = place("stat", [128, 2, 8], F32, OFF_S + 2336)
    idxt = place("idxt", [128, 2, BLK], F32, OFF_S + 2432)

    fb = [nc.alloc_psum_tensor(f"fb{i}", [128, 512], F32).ap() for i in range(6)]
    tbk = [nc.alloc_psum_tensor(f"tb{i}", [128, 1024], BF16).ap() for i in range(2)]

    P = Prog()

    def dma(q, out, in_, deps=(), sem=None):
        return P.emit(q, lambda e, o=out, i=in_: e.dma_start(out=o, in_=i),
                      deps, sem or ("S" if q == "sync" else "G"), 16)

    def mm(out, lhsT, rhs, start, stop, deps=(), inc=False):
        return P.emit("tensor",
                      lambda e, o=out, l=lhsT, r=rhs, st=start, sp=stop:
                      e.matmul(o, l, r, start=st, stop=sp),
                      deps, "T" if inc else None, 1)

    def tr(out, in_, deps=(), inc=False):
        return P.emit("tensor", lambda e, o=out, i=in_: e.transpose(o, i, ident),
                      deps, "T" if inc else None, 1)

    def act(fn, deps=(), inc=False):
        return P.emit("scalar", fn, deps, "A" if inc else None, 1)

    def vec(fn, deps=(), inc=False):
        return P.emit("vector", fn, deps, "V" if inc else None, 1)

    bank_free = [None] * 6
    grp_i = [0]

    def next_bank():
        i = grp_i[0] % 6
        grp_i[0] += 1
        return i, bank_free[i]

    def dbg_dump(name, src_ap, deps):
        if not _DEBUG:
            return None
        th = nc.dram_tensor("d_" + name, list(src_ap.shape), src_ap.dtype,
                            kind="ExternalOutput")
        dbg_dram[name] = th
        dbg_t[name] = dma("sync", th.ap(), src_ap, deps)
        return dbg_t[name]

    def ln_stats(src, b, sq_scr, deps):
        t_sum = vec(lambda e, src=src, b=b: e.reduce_sum(stat[:, b, 0:1], src, AX),
                    deps, inc=True)
        t_prev = None
        for s in range(4):
            d = list(deps) + [t_prev]
            t_q = act(lambda e, src=src, b=b, s=s, sq=sq_scr:
                      e.activation(sq, src[:, s * 512:(s + 1) * 512], AF.Square,
                                   accum_out=stat[:, b, 1:2] if s == 0
                                   else stat[:, b, 7:8]), d, inc=True)
            if s > 0:
                t_prev = vec(lambda e, b=b: e.tensor_tensor(stat[:, b, 1:2],
                                                            stat[:, b, 1:2],
                                                            stat[:, b, 7:8], ALU.add),
                             [t_q, t_prev], inc=True)
            else:
                t_prev = t_q
        return t_sum, t_prev

    def cols_ln_emit(b, deps, eps=EPS_LN):
        t1 = vec(lambda e: e.tensor_scalar(stat[:, b, 2:3], stat[:, b, 0:1],
                                           1.0 / D, None, ALU.mult), deps, inc=True)
        t2 = vec(lambda e: e.tensor_scalar(stat[:, b, 3:4], stat[:, b, 1:2],
                                           1.0 / D, None, ALU.mult), deps, inc=True)
        t3 = vec(lambda e: e.tensor_tensor(stat[:, b, 4:5], stat[:, b, 2:3],
                                           stat[:, b, 2:3], ALU.mult), [t1], inc=True)
        t4 = vec(lambda e: e.tensor_tensor(stat[:, b, 4:5], stat[:, b, 3:4],
                                           stat[:, b, 4:5], ALU.subtract),
                 [t2, t3], inc=True)
        t5 = vec(lambda e: e.tensor_scalar(stat[:, b, 5:6], stat[:, b, 4:5], eps,
                                           None, ALU.add), [t4], inc=True)
        return vec(lambda e: e.reciprocal(stat[:, b, 5:6], stat[:, b, 5:6]),
                   [t5], inc=True)

    dma("gpsimd", ident[:, :], ident_d[:, :])
    dma("gpsimd", ones_c[:, :], ones_c_d[:, :])
    dma("gpsimd", ones_r[:, :], ones_r_d[:, :])
    dma("gpsimd", idxt[:, :, :], idxt_d[:, :, :])
    dma("gpsimd", arrq[:, :], arrq_d[:, :])
    dma("gpsimd", arrk[:, :], arrk_d[:, :])
    dma("gpsimd", normw[:, :], normw_in[:, :])
    dma("gpsimd", slopes[:, :], slope_b[:, :])
    t_cst = dma("gpsimd", flag[:, :], flag_in[:, :])
    t_negs = vec(lambda e: e.tensor_scalar(negs[:, :], slopes[:, :], -1.0, None,
                                           ALU.mult), [t_cst], inc=True)

    wq_r = wqkvT.rearrange("(s p) w -> p s w", p=128)
    wv_r = wv.rearrange("(s p) w -> p s w", p=128)
    wg_r = wgT.rearrange("(s p) w -> p s w", p=128)
    wo_r = woT.rearrange("(s p) w -> p s w", p=128)
    gup_r = gup_w.rearrange("(ee s p) w -> p ee s w", p=128, ee=E)
    dwn_r = down_w.rearrange("(ee s p) w -> p ee s w", p=128, ee=E)
    slab_done = {}
    slab_loaded = {}
    slab_serial = [0]

    def load_slab(src_ap, extra=()):
        s = slab_serial[0]
        d = list(extra)
        if s >= 2:
            d.append(slab_done.get(s - 2))
        slab_loaded[s] = dma("gpsimd", wsl[s % 2].rearrange("p c w -> p (c w)"),
                             src_ap, d)
        slab_serial[0] = s + 1
        return s, wsl[s % 2]

    class Ring:
        def __init__(self, bufs, init_tok=None, sembase="R"):
            self.bufs = bufs
            self.n = len(bufs)
            self.s = 0
            self.done = {}
            self.loaded = {}
            self.init_tok = init_tok
            self.sembase = sembase

        def load(self, src_ap, extra=()):
            i = self.s
            d = list(extra) + [self.done.get(i - self.n, self.init_tok)]
            buf = self.bufs[i % self.n]
            tok = dma("gpsimd", buf.rearrange("p c w -> p (c w)"), src_ap, d,
                      sem=f"{self.sembase}{i % self.n}")
            self.loaded[i] = tok
            self.s = i + 1
            return i, buf, tok

    x_r = x_in.rearrange("(tt p) dd -> p tt dd", p=128)
    p0_evac = []
    for tt in range(8):
        b = tt % 2
        d = [p0_evac[2 * (tt - 2) + 1]] if tt >= 2 else []
        t_x = dma("sync", xin[:, b, :], x_r[:, tt, :], d)
        t_sum, t_ss = ln_stats(xin[:, b, :], b, sqtC[:, :], [t_x])
        t_cols = cols_ln_emit(b, [t_sum, t_ss])
        t_rstd = act(lambda e, b=b: e.activation(stat[:, b, 5:6], stat[:, b, 5:6],
                                                 AF.Sqrt), [t_cols], inc=True)
        t_app = vec(lambda e, b=b: e.tensor_scalar(xn2s[:, b, :], xin[:, b, :],
                                                   stat[:, b, 2:3], stat[:, b, 5:6],
                                                   ALU.subtract, ALU.mult),
                    [t_rstd], inc=True)
        for half in range(2):
            bank = tbk[half]
            tdeps = [t_app, p0_evac[-1] if p0_evac else None]
            tok_t = None
            for c8 in range(8):
                c = half * 8 + c8
                tok_t = tr(bank[:, c8 * 128:(c8 + 1) * 128],
                           xn2s[:, b, c * 128:(c + 1) * 128],
                           tdeps if c8 == 0 else (), inc=(c8 == 7))
            off = 512 * (tt % 2) + tt // 2
            t_ev = act(lambda e, bank=bank, off=off, half=half:
                       e.activation(xn1T[:, half * 8:(half + 1) * 8,
                                         off:off + 509:4],
                                    bank[:, :], AF.Copy), [tok_t], inc=True)
            p0_evac.append(t_ev)
    t_xn1T = p0_evac[-1]
    dbg_dump("xn1T", xn1T[:, :, :], [t_xn1T])

    silu_evac = {}

    def proj_fm(buf, dst, kind, heads, tok_load):
        tok_mm = None
        for obi, h in enumerate(heads):
            for th in range(2):
                bi, bfree = next_bank()
                for kc in range(C):
                    deps = [tok_load, bfree, t_xn1T] if kc == 0 else ()
                    tok_mm = mm(fb[bi][:, :], buf[:, kc, obi * 128:(obi + 1) * 128],
                                xn1T[:, kc, th * 512:(th + 1) * 512],
                                start=(kc == 0), stop=(kc == C - 1),
                                deps=deps, inc=(kc == C - 1))
                t_ev = act(lambda e, bi=bi, dst=dst, h=h, th=th:
                           e.activation(dst[:, h, th * 512:(th + 1) * 512],
                                        fb[bi][:, :], AF.Silu), [tok_mm], inc=True)
                bank_free[bi] = t_ev
                silu_evac[(kind, h, th)] = t_ev
        return tok_mm

    for s in range(8):
        ss, buf = load_slab(wq_r[:, s, :])
        slab_done[ss] = proj_fm(buf, kf, "k", [2 * s, 2 * s + 1], slab_loaded[ss])
    dbg_dump("k", kf[:, :, :], [silu_evac[("k", NH - 1, 1)]])
    tb_guard = t_xn1T
    vring = Ring(vbuf, sembase="VB")
    for s in range(4):
        vi, buf, tok_ld = vring.load(wv_r[:, s, :])
        tok_mm = None
        for tt in range(8):
            bi, bfree = next_bank()
            for kc in range(C):
                deps = [tok_ld, bfree, t_xn1T] if kc == 0 else ()
                tok_mm = mm(fb[bi][:, :], xn1T[:, kc, tt * 128:(tt + 1) * 128],
                            buf[:, kc, :], start=(kc == 0), stop=(kc == C - 1),
                            deps=deps, inc=(kc == C - 1))
            t_ev = act(lambda e, bi=bi, tt=tt, s=s:
                       e.activation(v_tm[:, tt, 512 * s:512 * (s + 1)],
                                    fb[bi][:, :], AF.Silu), [tok_mm], inc=True)
            bank_free[bi] = t_ev
            silu_evac[("v", tt, s)] = t_ev
        vring.done[vi] = tok_mm
    v_last_mm = tok_mm
    dbg_dump("v", v_tm[:, :, :], [silu_evac[("v", 7, 3)]])

    cc_dma = {}
    kvst_dma = {}
    pass1_prev = None
    for h in range(NH):
        hb = h % 2
        t_lam = act(lambda e, h=h: e.activation(lamt[:, 0:1], negs[:, h:h + 1],
                                                AF.Exp, scale=256.0),
                    [t_negs, pass1_prev], inc=True)
        t_kd = act(lambda e, h=h: e.activation(kdec[:, :], arrk[:, :], AF.Exp,
                                               scale=negs[:, h:h + 1]),
                   [t_negs, t_cst], inc=True)
        tok_kvup = None
        tok_st = None
        for t in range(NB):
            ktoks = []
            for j in range(2):
                bank = tbk[j]
                deps = [silu_evac[("k", h, (t * 256 + j * 128) // 512)], tb_guard]
                tok_t = tr(bank[:, 0:128],
                           kf[:, h, t * 256 + j * 128:t * 256 + (j + 1) * 128],
                           deps, inc=True)
                t_ke = vec(lambda e, j=j, bank=bank:
                           e.tensor_scalar(ktil[:, j, :], bank[:, 0:128],
                                           kdec[:, j:j + 1], None, ALU.mult),
                           [tok_t, t_kd], inc=True)
                tb_guard = t_ke
                ktoks.append(t_ke)
            if t >= 1:
                tok_st = vec(lambda e, t=t, hb=hb:
                             e.tensor_copy(kvst[:, hb, t - 1, :], kvrun[:, hb, :]),
                             [tok_kvup, kvst_dma.get(h - 2)], inc=True)
            bi, bfree = next_bank()
            tok_c = None
            for j in range(2):
                deps = [ktoks[j], silu_evac[("v", 2 * t + j, h // 4)]]
                if j == 0:
                    deps.append(bfree)
                tok_c = mm(fb[bi][:, 0:128], ktil[:, j, :],
                           v_tm[:, 2 * t + j, h * 128:(h + 1) * 128],
                           start=(j == 0), stop=(j == 1), deps=deps, inc=(j == 1))
            if t == 0:
                tok_kvup = vec(lambda e, hb=hb, bi=bi:
                               e.tensor_copy(kvrun[:, hb, :], fb[bi][:, 0:128]),
                               [tok_c, cc_dma.get(h - 2)], inc=True)
            else:
                tok_kvup = vec(lambda e, hb=hb, bi=bi:
                               e.scalar_tensor_tensor(kvrun[:, hb, :], kvrun[:, hb, :],
                                                      lamt[:, 0:1], fb[bi][:, 0:128],
                                                      ALU.mult, ALU.add),
                               [tok_c, t_lam, tok_kvup], inc=True)
            bank_free[bi] = tok_kvup
        kvst_dma[h] = dma("sync", kvl_d[h * 128:(h + 1) * 128, :],
                          kvst[:, hb, :, :].rearrange("p a w -> p (a w)"), [tok_st])
        cc_dma[h] = dma("sync", cc_in.ap()[:, h * 128:(h + 1) * 128],
                        kvrun[:, hb, :], [tok_kvup])
        pass1_prev = tok_kvup

    for s in range(8):
        ss, buf = load_slab(wq_r[:, 8 + s, :])
        slab_done[ss] = proj_fm(buf, qf, "q", [2 * s, 2 * s + 1], slab_loaded[ss])
    dbg_dump("q", qf[:, :, :], [silu_evac[("q", NH - 1, 1)]])

    for s in range(8):
        ss, buf = load_slab(wg_r[:, s, :])
        tok_mm = None
        for obi in range(2):
            c = 2 * s + obi
            for th in range(2):
                bi, bfree = next_bank()
                for kc in range(C):
                    deps = [slab_loaded[ss], bfree, t_xn1T] if kc == 0 else ()
                    tok_mm = mm(fb[bi][:, :], buf[:, kc, obi * 128:(obi + 1) * 128],
                                xn1T[:, kc, th * 512:(th + 1) * 512],
                                start=(kc == 0), stop=(kc == C - 1),
                                deps=deps, inc=(kc == C - 1))
                t_sg = act(lambda e, bi=bi, c=c, th=th:
                           e.activation(gp[:, c, th * 512:(th + 1) * 512],
                                        fb[bi][:, :], AF.Sigmoid), [tok_mm], inc=True)
                t_nw = vec(lambda e, c=c, th=th:
                           e.tensor_scalar(gp[:, c, th * 512:(th + 1) * 512],
                                           gp[:, c, th * 512:(th + 1) * 512],
                                           normw[:, c:c + 1], None, ALU.mult),
                           [t_sg, t_cst], inc=True)
                bank_free[bi] = t_nw
        slab_done[ss] = tok_mm
        t_gp_last = t_nw
    t_gate_mm_last = slab_done[23]
    dbg_dump("gp", gp[:, :, :], [t_gp_last])

    t_cc = P.emit("gpsimd",
                  lambda e: e.collective_compute(
                      "AllGather", ALU.bypass,
                      replica_groups=[[0, 1], [2, 3], [4, 5], [6, 7]],
                      ins=[cc_in.ap()[:, :]], outs=[cc_out.ap()[:, :]]),
                  [cc_dma[h] for h in range(NH)], "CC", 1)

    t_kvld = dma("sync", kvinit[:, :], cc_out.ap()[0:128, :],
                 [t_cc, t_gate_mm_last])
    t_kvfl = vec(lambda e: e.tensor_scalar(kvinit[:, :], kvinit[:, :], flag[:, 0:1],
                                           None, ALU.mult), [t_kvld, t_cst], inc=True)
    dbg_dump("kvinit", kvinit[:, :], [t_kvfl])
    t_qkz = vec(lambda e: e.memset(qks[:, 0, 1, 0:128], 0.0), [], inc=True)
    t_qkz2 = vec(lambda e: e.memset(qks[:, 1, 1, 0:128], 0.0), [], inc=True)

    o_evac = {}
    pass2_last = None
    kvt_free = [None, None]
    qtil_free = [None, None]
    qks_free = [t_qkz, t_qkz2]
    kvld_dma = {}
    kvld_free = [None, None]
    pend_o = None
    last_qke1 = None
    hstate = {}
    for i in range(NH * NB + 1):
        if pend_o is not None:
            (ph, pt, ptb2, pqke0, pqke1, pkvt, pqt, pbio, pbfree_o) = pend_o
            mm(fb[pbio][:, 0:256], v_tm[:, 2 * pt, ph * 128:(ph + 1) * 128],
               qks[:, ptb2, 0, :], start=True, stop=False,
               deps=[pqke0, pbfree_o, silu_evac[("v", 2 * pt, ph // 4)]])
            mm(fb[pbio][:, 128:256], v_tm[:, 2 * pt + 1, ph * 128:(ph + 1) * 128],
               qks[:, ptb2, 1, 128:256], start=False, stop=False,
               deps=[pqke1, silu_evac[("v", 2 * pt + 1, ph // 4)]])
            tok_o = mm(fb[pbio][:, 0:256], kvt[:, ptb2, :], qtil[:, ptb2, :],
                       start=False, stop=True, deps=[pkvt, pqt], inc=True)
            kvt_free[ptb2] = tok_o
            qtil_free[ptb2] = tok_o
            qks_free[ptb2] = tok_o
            t_oe = act(lambda e, pbio=pbio, ph=ph, pt=pt:
                       e.activation(kf[:, ph, pt * 256:(pt + 1) * 256],
                                    fb[pbio][:, 0:256], AF.Copy), [tok_o], inc=True)
            o_evac[(ph, pt)] = t_oe
            bank_free[pbio] = t_oe
            pass2_last = t_oe
            if pt == 3:
                kvld_free[ph % 2] = t_oe
        if i == NH * NB:
            break
        h, t = divmod(i, NB)
        if t == 0:
            kvld_dma[h] = dma("sync",
                              kvld[:, h % 2, :, :].rearrange("p a w -> p (a w)"),
                              kvl_d[h * 128:(h + 1) * 128, :],
                              [kvst_dma[NH - 1] if h == 0 else None,
                               kvld_free[h % 2]], sem=f"K{h % 2}")
            t_dg = act(lambda e, h=h: e.activation(diagT[:, :, :], idxt[:, :, :],
                                                   AF.Exp, scale=negs[:, h:h + 1]),
                       [t_negs, t_cst, last_qke1], inc=True)
            t_qr = act(lambda e, h=h: e.activation(qdrow[:, :], arrq[:, :], AF.Exp,
                                                   scale=negs[0:1, h:h + 1]),
                       [t_cst, last_qke1], inc=True)
            bi_q, bfree_q = next_bank()
            t_qb = mm(fb[bi_q][:, 0:256], ones_r[:, :], qdrow[:, :], True, True,
                      [t_qr, bfree_q], inc=True)
            t_qbe = act(lambda e, bi_q=bi_q:
                        e.activation(qdecb[:, :], fb[bi_q][:, 0:256], AF.Copy),
                        [t_qb], inc=True)
            bank_free[bi_q] = t_qbe
            lam_toks = [act(lambda e, h=h, t2=t2:
                            e.activation(lamt[:, t2:t2 + 1], negs[:, h:h + 1],
                                         AF.Exp, scale=256.0 * t2),
                            [t_negs, last_qke1], inc=True)
                        for t2 in range(1, NB)]
            hstate = {"qbe": t_qbe, "dg": t_dg, "lam": lam_toks}
        tb2 = t % 2
        if t == 0:
            t_kvt = P.emit("gpsimd",
                           lambda e, h=h, tb2=tb2:
                           e.tensor_copy(kvt[:, tb2, :],
                                         kvinit[:, h * 128:(h + 1) * 128]),
                           [t_kvfl, kvt_free[tb2]], "PL", 1)
        else:
            t_kvt = vec(lambda e, h=h, t=t, tb2=tb2:
                        e.scalar_tensor_tensor(kvt[:, tb2, :],
                                               kvinit[:, h * 128:(h + 1) * 128],
                                               lamt[:, t:t + 1],
                                               kvld[:, h % 2, t - 1, :],
                                               ALU.mult, ALU.add),
                        [t_kvfl, hstate["lam"][t - 1], kvld_dma[h],
                         kvt_free[tb2]], inc=True)
        th = t // 2
        t_qt = P.emit("gpsimd",
                      lambda e, h=h, t=t, tb2=tb2:
                      e.tensor_tensor(qtil[:, tb2, :],
                                      qf[:, h, t * 256:(t + 1) * 256],
                                      qdecb[:, :], ALU.mult),
                      [hstate["qbe"], silu_evac[("q", h, th)], qtil_free[tb2]],
                      "PL", 1)
        bi_qk, bfree_qk = next_bank()
        qk_mm = []
        for j in range(2):
            deps = [silu_evac[("k", h, th)], silu_evac[("q", h, th)]]
            if j == 0:
                deps.append(bfree_qk)
            qk_mm.append(mm(fb[bi_qk][:, j * 256:(j + 1) * 256],
                            kf[:, h, t * 256 + j * 128:t * 256 + (j + 1) * 128],
                            qf[:, h, t * 256:(t + 1) * 256],
                            start=True, stop=True, deps=deps, inc=True))
        t_qke0 = vec(lambda e, bi_qk=bi_qk, tb2=tb2:
                     e.tensor_tensor(qks[:, tb2, 0, :], fb[bi_qk][:, 0:256],
                                     diagT[:, 0, :], ALU.mult),
                     [qk_mm[0], hstate["dg"], qks_free[tb2]], inc=True)
        t_qke1 = vec(lambda e, bi_qk=bi_qk, tb2=tb2:
                     e.tensor_tensor(qks[:, tb2, 1, 128:256],
                                     fb[bi_qk][:, 384:512],
                                     diagT[:, 1, 128:256], ALU.mult),
                     [qk_mm[1], hstate["dg"]], inc=True)
        bank_free[bi_qk] = t_qke1
        last_qke1 = t_qke1
        bi_o, bfree_o = next_bank()
        pend_o = (h, t, tb2, t_qke0, t_qke1, t_kvt, t_qt, bi_o, bfree_o)

    dbg_dump("o", kf[:, :, :], [pass2_last])
    bi_r0, bfree_r0 = next_bank()
    bi_r1, bfree_r1 = next_bank()
    mm_hist = []
    for c in range(C):
        b = c % 2
        deps = [o_evac[(c, t)] for t in range(NB)] + [pass2_last]
        if len(mm_hist) >= 2:
            deps += mm_hist[-2]
        t_sq = vec(lambda e, c=c, b=b: e.tensor_tensor(sqG[:, b, :], kf[:, c, :],
                                                       kf[:, c, :], ALU.mult),
                   deps, inc=True)
        cur = []
        for th in range(2):
            bi = bi_r0 if th == 0 else bi_r1
            rhs = sqG[:, b, :].rearrange("p (tq t4) -> p t4 tq",
                                         t4=4)[:, 2 * th:2 * th + 2, :]
            deps2 = [t_sq]
            if c == 0:
                deps2.append(bfree_r0 if th == 0 else bfree_r1)
            cur.append(mm(fb[bi][0:1, :], ones_c[:, :], rhs,
                          start=(c == 0), stop=(c == C - 1), deps=deps2, inc=True))
        mm_hist.append(cur)

    r1 = vec(lambda e: e.tensor_scalar(rrow[0:1, 0:512], fb[bi_r0][0:1, :],
                                       1.0 / D, EPS_RMS, ALU.mult, ALU.add),
             mm_hist[-1], inc=True)
    r2 = vec(lambda e: e.tensor_scalar(rrow[0:1, 512:1024], fb[bi_r1][0:1, :],
                                       1.0 / D, EPS_RMS, ALU.mult, ALU.add),
             mm_hist[-1], inc=True)
    r3 = vec(lambda e: e.reciprocal(rrow[0:1, 0:512], rrow[0:1, 0:512]),
             [r1], inc=True)
    t_rm = vec(lambda e: e.reciprocal(rrow[0:1, 512:1024], rrow[0:1, 512:1024]),
               [r2, r3], inc=True)
    t_rs = act(lambda e: e.activation(rrow[0:1, :], rrow[0:1, :], AF.Sqrt), [t_rm],
               inc=True)
    bank_free[bi_r0] = t_rs
    bank_free[bi_r1] = t_rs
    t_rw = dma("sync", rstd_d[0:1, :], rrow[0:1, :], [t_rs])
    t_rc = []
    for gt in range(8):
        t_rc.append(dma("sync", rstdc[:, gt:gt + 1],
                        rstd_d.rearrange("o (gt p) -> p (o gt)", p=128)[:, gt:gt + 1],
                        [t_rw]))

    gz_tok = {}
    for c in range(C):
        for e4 in range(4):
            deps = [o_evac[(c, t)] for t in range(NB)] + [pass2_last]
            gz_tok[(c, e4)] = vec(
                lambda e, c=c, e4=e4: e.tensor_tensor(
                    gz[:, c, e4 * 256:(e4 + 1) * 256],
                    gp[:, c, e4:T:4], kf[:, c, e4:T:4], ALU.mult),
                deps, inc=True)
    t_gz_all = gz_tok[(C - 1, 3)]

    wo_tok = {}
    for s in range(2):
        wo_tok[s] = dma("gpsimd", woB[:, s, :, :].rearrange("p c w -> p (c w)"),
                        wo_r[:, s, :], [pass2_last, dbg_t.get("q")])
    for s in range(2, 4):
        wo_tok[s] = dma("gpsimd", woE[:, s - 2, :, :].rearrange("p c w -> p (c w)"),
                        wo_r[:, s, :], [t_gz_all])
    wo_all = wo_tok[3]

    def wo_res(s):
        return woB[:, s, :, :] if s < 2 else woE[:, s - 2, :, :]

    xin_free = None
    aout_free = [None, None]
    p6_xn2_evac = []
    p6_spill = {}
    pend_tr = []
    p6_last_mm_tok = None
    for gt in range(8):
        gi = gt % 2
        ev_toks2 = []
        for s in range(4):
            bi, bfree = next_bank()
            for kc in range(C):
                deps = ([wo_all, bfree, t_gz_all, aout_free[gi],
                         dbg_t.get("o"), dbg_t.get("kvinit")]
                        if kc == 0 else ())
                tok_mm = mm(fb[bi][:, :], gz[:, kc, gt * 128:(gt + 1) * 128],
                            wo_res(s)[:, kc, :], start=(kc == 0), stop=(kc == C - 1),
                            deps=deps, inc=(kc == C - 1))
            t_ev = vec(lambda e, bi=bi, gi=gi, s=s:
                       e.tensor_copy(aout[:, gi, s * 512:(s + 1) * 512],
                                     fb[bi][:, :]), [tok_mm], inc=True)
            bank_free[bi] = t_ev
            ev_toks2.append(t_ev)
            p6_last_mm_tok = tok_mm
        if True:
            ev_toks = {gi: ev_toks2}
            b = gt % 2
            t_sum, t_ss = ln_stats(aout[:, gi, :], b, sqtC[:, :], ev_toks[gi])

            u1 = vec(lambda e, b=b: e.tensor_scalar(stat[:, b, 2:3],
                                                    stat[:, b, 0:1], 1.0 / D, None,
                                                    ALU.mult), [t_sum], inc=True)
            u2 = vec(lambda e, b=b: e.tensor_scalar(stat[:, b, 3:4],
                                                    stat[:, b, 1:2], 1.0 / D, None,
                                                    ALU.mult), [t_ss], inc=True)
            u3 = vec(lambda e, b=b: e.tensor_tensor(stat[:, b, 4:5],
                                                    stat[:, b, 2:3],
                                                    stat[:, b, 2:3], ALU.mult),
                     [u1], inc=True)
            u4 = vec(lambda e, b=b: e.tensor_tensor(stat[:, b, 4:5],
                                                    stat[:, b, 3:4],
                                                    stat[:, b, 4:5], ALU.subtract),
                     [u2, u3], inc=True)
            u5 = vec(lambda e, b=b, gt=gt: e.tensor_tensor(stat[:, b, 5:6],
                                                           rstdc[:, gt:gt + 1],
                                                           rstdc[:, gt:gt + 1],
                                                           ALU.mult),
                     t_rc + [u2], inc=True)
            u6 = vec(lambda e, b=b: e.tensor_tensor(stat[:, b, 5:6],
                                                    stat[:, b, 4:5],
                                                    stat[:, b, 5:6], ALU.mult),
                     [u4, u5], inc=True)
            u7 = vec(lambda e, b=b: e.tensor_scalar(stat[:, b, 5:6],
                                                    stat[:, b, 5:6], EPS_LN, None,
                                                    ALU.add), [u6], inc=True)
            t_c1 = vec(lambda e, b=b: e.reciprocal(stat[:, b, 5:6],
                                                   stat[:, b, 5:6]), [u7], inc=True)
            t_c2 = act(lambda e, b=b: e.activation(stat[:, b, 5:6], stat[:, b, 5:6],
                                                   AF.Sqrt), [t_c1], inc=True)
            t_c3 = vec(lambda e, b=b, gt=gt:
                       e.tensor_tensor(stat[:, b, 6:7], stat[:, b, 5:6],
                                       rstdc[:, gt:gt + 1], ALU.mult), [t_c2],
                       inc=True)
            t_x = dma("sync", xin[:, 0, :], x_r[:, gt, :], [xin_free, t_gz_all])
            t_a = vec(lambda e, b=b, gi=gi:
                      e.tensor_scalar(aout[:, gi, :], aout[:, gi, :], stat[:, b, 2:3],
                                      stat[:, b, 6:7], ALU.subtract, ALU.mult),
                      [t_c3], inc=True)
            t_a2 = vec(lambda e, gi=gi:
                       e.tensor_tensor(aout[:, gi, :], aout[:, gi, :], xin[:, 0, :],
                                       ALU.add), [t_a, t_x], inc=True)
            xin_free = t_a2
            t_sum2, t_ss2 = ln_stats(aout[:, gi, :], b, sqtC[:, :], [t_a2])
            t_cols2 = cols_ln_emit(b, [t_sum2, t_ss2])
            t_rstd2 = act(lambda e, b=b: e.activation(stat[:, b, 5:6],
                                                      stat[:, b, 5:6], AF.Sqrt),
                          [t_cols2], inc=True)
            t_app2 = vec(lambda e, b=b, gi=gi:
                         e.tensor_scalar(xn2s[:, b, :], aout[:, gi, :],
                                         stat[:, b, 2:3], stat[:, b, 5:6],
                                         ALU.subtract, ALU.mult),
                         [t_rstd2,
                          p6_xn2_evac[-1] if p6_xn2_evac else None],
                         inc=True)
            t_sp = dma("sync",
                       x2sp.rearrange("(tt p) dd -> p tt dd", p=128)[:, gt, :],
                       aout[:, gi, :], [t_app2])
            p6_spill[gt] = t_sp
            aout_free[gi] = t_sp
            for half in range(2):
                bank = tbk[half]
                tdeps = [t_app2, p6_xn2_evac[-1] if p6_xn2_evac else None]
                tok_t = None
                for c8 in range(8):
                    c = half * 8 + c8
                    tok_t = tr(bank[:, c8 * 128:(c8 + 1) * 128],
                               xn2s[:, b, c * 128:(c + 1) * 128],
                               tdeps if c8 == 0 else (), inc=(c8 == 7))
                t_ev = act(lambda e, bank=bank, gt=gt, half=half:
                           e.activation(xn2T[:, half * 8:(half + 1) * 8,
                                             gt * 128:(gt + 1) * 128],
                                        bank[:, :], AF.Copy), [tok_t], inc=True)
                p6_xn2_evac.append(t_ev)
    t_xn2T = p6_xn2_evac[-1]
    p6_mm_last = p6_last_mm_tok

    ring8 = Ring(dw + [rw8], init_tok=p6_mm_last)
    x2r_free = [None, None]
    obuf_free = [None, None]
    sg_free = [None, None]
    out_toks = []
    down_last_prev = None
    for e_i in range(E):
        t_int = None
        for s in range(8):
            ss, slab, tok_ld = ring8.load(gup_r[:, e_i, s, :])
            tokU = None
            for pr in range(2):
                i_blk = 2 * s + pr
                biG, bfreeG = next_bank()
                biU, bfreeU = next_bank()
                tokG = None
                for kc in range(C):
                    dG = ([tok_ld, bfreeG, t_xn2T, dbg_t.get("q"),
                           dbg_t.get("gp")] if kc == 0 else ())
                    tokG = mm(fb[biG][:, 0:256], slab[:, kc, 256 * pr:256 * pr + 128],
                              xn2T[:, kc, e_i * 256:(e_i + 1) * 256],
                              start=(kc == 0), stop=(kc == C - 1), deps=dG,
                              inc=(kc == C - 1))
                for kc in range(C):
                    dU = [bfreeU] if kc == 0 else ()
                    tokU = mm(fb[biU][:, 0:256],
                              slab[:, kc, 256 * pr + 128:256 * (pr + 1)],
                              xn2T[:, kc, e_i * 256:(e_i + 1) * 256],
                              start=(kc == 0), stop=(kc == C - 1), deps=dU,
                              inc=(kc == C - 1))
                sb = i_blk % 2
                t_sil = act(lambda e, biG=biG, sb=sb:
                            e.activation(sgate[:, sb, :], fb[biG][:, 0:256], AF.Silu),
                            [tokG, sg_free[sb]], inc=True)
                t_int = vec(lambda e, biU=biU, sb=sb, i_blk=i_blk:
                            e.tensor_tensor(inter[:, i_blk, :], fb[biU][:, 0:256],
                                            sgate[:, sb, :], ALU.mult),
                            [tokU, t_sil, down_last_prev,
                             pass2_last if (e_i == 0 and i_blk == 0) else None],
                            inc=True)
                bank_free[biG] = t_int
                bank_free[biU] = t_int
                sg_free[sb] = t_int
            ring8.done[ss] = tokU
        inter_last = t_int
        ev_toks = [[], []]
        for s in range(4):
            ss, slab, tok_ld = ring8.load(dwn_r[:, e_i, s, :])
            tok_mm = None
            for gt in range(2):
                bi, bfree = next_bank()
                for kc in range(C):
                    deps = ([tok_ld, bfree, inter_last,
                             obuf_free[gt]] if kc == 0 else ())
                    tok_mm = mm(fb[bi][:, :],
                                inter[:, kc, gt * 128:(gt + 1) * 128],
                                slab[:, kc, :], start=(kc == 0), stop=(kc == C - 1),
                                deps=deps, inc=(kc == C - 1))
                t_ev = vec(lambda e, bi=bi, gt=gt, s=s:
                           e.tensor_copy(obuf[:, gt, s * 512:(s + 1) * 512],
                                         fb[bi][:, :]), [tok_mm], inc=True)
                bank_free[bi] = t_ev
                ev_toks[gt].append(t_ev)
            ring8.done[ss] = tok_mm
            down_last_prev = tok_mm
        for gt in range(2):
            g_glob = e_i * 2 + gt
            b = gt
            t_x2 = dma("sync", x2r[:, b, :],
                       x2sp.rearrange("(tt p) dd -> p tt dd", p=128)[:, g_glob, :],
                       [p6_spill[g_glob], x2r_free[b], pass2_last, p6_mm_last])
            t_sum4, t_ss4 = ln_stats(obuf[:, gt, :], b, sqtC[:, :], ev_toks[gt])
            t_cols4 = cols_ln_emit(b, [t_sum4, t_ss4])
            t_rstd4 = act(lambda e, b=b: e.activation(stat[:, b, 5:6],
                                                      stat[:, b, 5:6], AF.Sqrt),
                          [t_cols4], inc=True)
            t_a = vec(lambda e, b=b, gt=gt:
                      e.tensor_scalar(obuf[:, gt, :], obuf[:, gt, :], stat[:, b, 2:3],
                                      stat[:, b, 5:6], ALU.subtract, ALU.mult),
                      [t_rstd4], inc=True)
            t_a2 = vec(lambda e, b=b, gt=gt:
                       e.tensor_tensor(obuf[:, gt, :], obuf[:, gt, :], x2r[:, b, :],
                                       ALU.add), [t_a, t_x2], inc=True)
            t_out = dma("sync",
                        out_d.rearrange("(tt p) dd -> p tt dd", p=128)[:, g_glob, :],
                        obuf[:, gt, :], [t_a2])
            out_toks.append(t_out)
            obuf_free[gt] = t_out
            x2r_free[b] = t_out

    if _DEBUG:
        out_toks.append(dbg_dump("gz", gz[:, :, :], [t_gz_all]))
        out_toks.append(dbg_dump("rrow", rrow[0:1, :], [t_rs]))
        out_toks.append(dbg_dump("xn2T", xn2T[:, :, :], [t_xn2T]))
        out_toks.append(dbg_dump("x2", x2sp[:, :], [p6_spill[gt2] for gt2 in range(8)]))
        out_toks.append(dbg_dump("kvl", kvl_d[:, :], [kvst_dma[hh] for hh in range(NH)]))
        out_toks.append(dbg_dump("ccin", cc_in.ap()[:, :],
                                 [cc_dma[hh] for hh in range(NH)]))
        out_toks.append(dbg_dump("inter", inter[:, :, :], out_toks[-1:]))
        out_toks.extend([dbg_t[k] for k in ("xn1T", "k", "v", "q", "gp", "kvinit", "o")])
    P.emit("sync", lambda e: e.nop(), out_toks)
    P.emit("gpsimd", lambda e: e.nop(), out_toks)

    from contextlib import ExitStack
    with ExitStack() as es:
        sems = {s: es.enter_context(nc.semaphore(s))
                for s in ("S", "G", "T", "V", "A", "CC", "R0", "R1", "R2",
                          "VB0", "VB1", "K0", "K1", "PL")}
        with nc.Block() as block:
            block.sync(lambda e: P.run("sync", e, sems))
            block.gpsimd(lambda e: P.run("gpsimd", e, sems))
            block.tensor(lambda e: P.run("tensor", e, sems))
            block.vector(lambda e: P.run("vector", e, sems))
            block.scalar(lambda e: P.run("scalar", e, sems))
    return nc


_PERM = 4 * (np.arange(1024) % 256) + np.arange(1024) // 256
_IPERM = np.argsort(_PERM)
_NC_CACHE = [None]
_LAST_RES = [None]
_DEBUG = bool(os.environ.get("KDEBUG"))


def kernel(L0_attn_hidden_states, L0_attn_slope_rate, L0_attn_w_qkv, L0_attn_w_gate,
           L0_attn_w_out, L0_attn_norm_w, L0_ffn_gate_up_proj, L0_ffn_down_proj):
    x = np.asarray(L0_attn_hidden_states, np.float32)
    slope = np.asarray(L0_attn_slope_rate, np.float32).reshape(1, NH)
    w_qkv = np.asarray(L0_attn_w_qkv, np.float32)
    w_gate = np.asarray(L0_attn_w_gate, np.float32)
    w_out = np.asarray(L0_attn_w_out, np.float32)
    norm_w = np.asarray(L0_attn_norm_w, np.float32)
    gup = np.asarray(L0_ffn_gate_up_proj, np.float32)
    down = np.asarray(L0_ffn_down_proj, np.float32)

    def slabify(wT, w=256):
        nsl = wT.shape[1] // w
        a = wT.reshape(C, 128, nsl, w)
        return np.ascontiguousarray(a.transpose(2, 1, 0, 3).reshape(nsl * 128, C * w))

    k_cols = np.concatenate([h * 384 + 128 + np.arange(128) for h in range(NH)])
    v_cols = np.concatenate([h * 384 + 256 + np.arange(128) for h in range(NH)])
    q_cols = np.concatenate([h * 384 + np.arange(128) for h in range(NH)])
    wqkvT = slabify(w_qkv.T[:, np.concatenate([k_cols, q_cols])])
    wv_h = slabify(w_qkv.T[:, v_cols], 512)
    wgT = slabify(w_gate.T)
    woT = slabify(w_out.T, 512)
    gup_cols = np.concatenate([np.concatenate([i * 128 + np.arange(128),
                                               2048 + i * 128 + np.arange(128)])
                               for i in range(16)])
    gup_h = np.concatenate([slabify(gup[e][:, gup_cols], 512) for e in range(E)])
    down_h = np.concatenate([slabify(down[e], 512) for e in range(E)])
    slope_bc = np.ascontiguousarray(np.broadcast_to(slope, (128, NH)))
    normw_h = np.ascontiguousarray(norm_w.reshape(C, 128).T)

    if _NC_CACHE[0] is None:
        _NC_CACHE[0] = build_nc()
    nc = _NC_CACHE[0]

    in_maps = []
    for c in range(8):
        b, hf = divmod(c, 2)
        x_sh = np.ascontiguousarray(x[b, hf * T:(hf + 1) * T][_PERM])
        in_maps.append({
            "x_sh": x_sh, "wqkvT": wqkvT, "wv": wv_h, "wgT": wgT,
            "woT": woT,
            "gup_w": gup_h, "down_w": down_h, "slope_b": slope_bc,
            "normw": normw_h,
            "flag": np.full((128, 1), 1.0 if c % 2 else 0.0, np.float32),
        })
    res = run_bass_kernel_spmd(nc, in_maps, core_ids=list(range(8)))
    if _DEBUG:
        _LAST_RES[0] = res
    out = np.empty((4, 2048, D), np.float32)
    for c in range(8):
        b, hf = divmod(c, 2)
        out[b, hf * T:(hf + 1) * T] = np.asarray(res.results[c]["out"])[_IPERM]
    return out


# version 19
# speedup vs baseline: 1.2316x; 1.1529x over previous
import os
import numpy as np
import ml_dtypes
import concourse.bass as bass
import concourse.mybir as mybir
from concourse.bass_utils import run_bass_kernel_spmd

F32 = mybir.dt.float32
BF16 = mybir.dt.bfloat16
AF = mybir.ActivationFunctionType
ALU = mybir.AluOpType
AX = mybir.AxisListType.X

NH, HD, BLK = 16, 128, 256
D = 2048
T = 1024
C = 16
NB = 4
E = 4
EPS_LN = 1e-5
EPS_RMS = 1e-6
KB = 1024

OFF_A = 0 * KB
OFF_B = 32 * KB
OFF_C = 64 * KB
OFF_D = 96 * KB
OFF_E = 128 * KB
OFF_G = 160 * KB
OFF_W = 175 * KB
OFF_S = 191 * KB
ARENA = 196 * KB


class Prog:

    ENGS = ("sync", "gpsimd", "tensor", "vector", "scalar")

    def __init__(self):
        self.ops = {e: [] for e in self.ENGS}
        self.count = {}

    def emit(self, eng, fn, deps=(), inc=None, n=1):
        deps = tuple(d for d in deps if d is not None)
        self.ops[eng].append((fn, deps, inc, n))
        if inc is not None:
            self.count[inc] = self.count.get(inc, 0) + n
            return (inc, self.count[inc])
        return None

    def run(self, engname, eng, sems):
        seen = {}
        for fn, deps, inc, n in self.ops[engname]:
            for s, v in deps:
                if seen.get(s, 0) < v:
                    eng.wait_ge(sems[s], v)
                    seen[s] = v
            ins = fn(eng)
            if inc is not None:
                ins.then_inc(sems[inc], n)


_DEBUG = bool(os.environ.get("KDEBUG"))


def build_nc():
    nc = bass.Bass("TRN2", target_bir_lowering=False)
    dbg_t = {}
    dbg_dram = {}

    x_in = nc.dram_tensor("x_sh", [T, D], F32, kind="ExternalInput").ap()
    wqkvT = nc.dram_tensor("wqkvT", [16 * 128, C * 256], BF16, kind="ExternalInput").ap()
    wv = nc.dram_tensor("wv", [4 * 128, C * 512], BF16, kind="ExternalInput").ap()
    wgT = nc.dram_tensor("wgT", [8 * 128, C * 256], BF16, kind="ExternalInput").ap()
    woT = nc.dram_tensor("woT", [4 * 128, C * 512], BF16, kind="ExternalInput").ap()
    gup_w = nc.dram_tensor("gup_w", [E * 8 * 128, C * 512], BF16, kind="ExternalInput").ap()
    down_w = nc.dram_tensor("down_w", [E * 4 * 128, C * 512], BF16, kind="ExternalInput").ap()
    slope_b = nc.dram_tensor("slope_b", [128, NH], F32, kind="ExternalInput").ap()
    normw_in = nc.dram_tensor("normw", [128, C], F32, kind="ExternalInput").ap()
    flag_in = nc.dram_tensor("flag", [128, 1], F32, kind="ExternalInput").ap()
    out_d = nc.dram_tensor("out", [T, D], F32, kind="ExternalOutput").ap()

    cc_in = nc.dram_tensor("cc_in", [128, NH * HD], F32, kind="Internal")
    cc_out = nc.dram_tensor("cc_out", [256, NH * HD], F32, kind="Internal")
    x2sp = nc.dram_tensor("x2spill", [T, D], F32, kind="Internal").ap()
    rstd_d = nc.dram_tensor("rstd_d", [1, T], F32, kind="Internal").ap()
    kvl_d = nc.dram_tensor("kvloc_d", [NH * 128, 3 * HD], F32, kind="Internal").ap()

    idxt_np = np.empty((128, 2, BLK), np.float32)
    for j in range(2):
        n_idx = (j * 128 + np.arange(128))[:, None]
        m_idx = np.arange(BLK)[None, :]
        idxt_np[:, j, :] = np.where(m_idx >= n_idx,
                                    (m_idx - n_idx).astype(np.float32), 1e9)
    arrq_np = np.arange(1, BLK + 1, dtype=np.float32).reshape(1, BLK)
    arrk_np = np.empty((128, 2), np.float32)
    for j in range(2):
        arrk_np[:, j] = 255.0 - (j * 128 + np.arange(128))
    idxt_d = nc.inline_tensor(idxt_np, "idxt_c").ap()
    arrq_d = nc.inline_tensor(arrq_np, "arrq_c").ap()
    arrk_d = nc.inline_tensor(arrk_np, "arrk_c").ap()
    ident_d = nc.inline_tensor(np.eye(128, dtype=np.float32), "ident_c").ap()
    ones_c_d = nc.inline_tensor(np.ones((128, 1), np.float32), "ones_c_c").ap()
    ones_r_d = nc.inline_tensor(np.ones((1, 128), np.float32), "ones_r_c").ap()

    nc.alloc_sbuf_tensor("arena", [128, ARENA], mybir.dt.uint8)
    base = nc.sbuf_base - ARENA

    def place(name, shape, dtype, off):
        return nc.alloc_sbuf_tensor_at(name, shape, dtype, offset=base + off).ap()

    xn1T = place("xn1T", [128, C, T], BF16, OFF_A)
    kvinit = place("kvinit", [128, NH * HD], F32, OFF_A)
    xn2T = place("xn2T", [128, C, T], BF16, OFF_A)
    qf = place("qf", [128, NH, T], BF16, OFF_B)
    vbuf = [place(f"vb{i}", [128, C, 512], BF16, OFF_B + i * 16 * KB)
            for i in range(2)]
    woB = place("woB", [128, 2, C, 512], BF16, OFF_B)
    inter = place("inter", [128, C, 256], BF16, OFF_B)
    x2r = place("x2r", [128, 2, D], F32, OFF_B + 8 * KB)
    kf = place("kf", [128, NH, T], BF16, OFF_C)
    xin = place("xin", [128, 2, D], F32, OFF_C)
    aout = place("aout", [128, 2, D], F32, OFF_C + 8 * KB)
    xn2s = place("xn2s", [128, 2, D], BF16, OFF_C + 24 * KB)
    sqtC = place("sqtC", [128, 512], F32, OFF_G + 7 * KB)
    v_tm = place("v_tm", [128, 8, D], BF16, OFF_D)
    gz = place("gz", [128, C, T], BF16, OFF_D)
    gp = place("gp", [128, C, T], BF16, OFF_E)
    woE = place("woE", [128, 2, C, 512], BF16, OFF_E)
    obuf = place("obuf", [128, 2, D], F32, OFF_E)
    sgate = place("sgate", [128, 2, 256], F32, OFF_B + 24 * KB)
    kvrun = place("kvrun", [128, 2, HD], F32, OFF_G)
    kvst = place("kvst", [128, 2, 3, HD], F32, OFF_G + 1 * KB)
    kvld = place("kvld", [128, 2, 3, HD], F32, OFF_G + 4 * KB)
    diagT = place("diagT", [128, 2, BLK], F32, OFF_G + 7 * KB)
    qdecb = place("qdecb", [128, BLK], F32, OFF_G + 9 * KB)
    kdec = place("kdec", [128, 2], F32, OFF_G + 10 * KB)
    lamt = place("lamt", [128, NB], F32, OFF_G + 10 * KB + 32)
    qtil = place("qtil", [128, 2, BLK], BF16, OFF_G + 10 * KB + 256)
    qks = place("qks", [128, 2, 2, BLK], BF16, OFF_G + 11 * KB + 256)
    kvt = place("kvt", [128, 2, HD], BF16, OFF_G + 13 * KB + 256)
    ktil = place("ktil", [128, 2, HD], BF16, OFF_G + 13 * KB + 768)
    arrk = place("arrk", [128, 2], F32, OFF_G + 14 * KB + 256)
    sqG = place("sqG", [128, 2, T], BF16, OFF_G)
    rrow = place("rrow", [1, T], F32, OFF_G + 4 * KB)
    wsl = [place(f"wsl{i}", [128, C, 256], BF16, OFF_W + i * 8 * KB) for i in range(2)]
    dw = [place(f"dw{i}", [128, C, 512], BF16, OFF_D + i * 16 * KB) for i in range(2)]
    rw8 = place("rw8", [128, C, 512], BF16, OFF_W)
    ident = place("ident", [128, 128], BF16, OFF_S)
    ones_c = place("ones_c", [128, 1], BF16, OFF_S + 256)
    ones_r = place("ones_r", [1, 128], BF16, OFF_S + 288)
    arrq = place("arrq", [1, BLK], F32, OFF_S + 544)
    qdrow = place("qdrow", [1, BLK], BF16, OFF_S + 1568)
    normw = place("normw", [128, C], F32, OFF_S + 2080)
    slopes = place("slopes", [128, NH], F32, OFF_S + 2144)
    negs = place("negs", [128, NH], F32, OFF_S + 2208)
    flag = place("flag", [128, 1], F32, OFF_S + 2272)
    rstdc = place("rstdc", [128, 8], F32, OFF_S + 2304)
    stat = place("stat", [128, 2, 8], F32, OFF_S + 2336)
    idxt = place("idxt", [128, 2, BLK], F32, OFF_S + 2432)

    fb = [nc.alloc_psum_tensor(f"fb{i}", [128, 512], F32).ap() for i in range(6)]
    tbk = [nc.alloc_psum_tensor(f"tb{i}", [128, 1024], BF16).ap() for i in range(2)]

    P = Prog()

    def dma(q, out, in_, deps=(), sem=None):
        return P.emit(q, lambda e, o=out, i=in_: e.dma_start(out=o, in_=i),
                      deps, sem or ("S" if q == "sync" else "G"), 16)

    def mm(out, lhsT, rhs, start, stop, deps=(), inc=False):
        return P.emit("tensor",
                      lambda e, o=out, l=lhsT, r=rhs, st=start, sp=stop:
                      e.matmul(o, l, r, start=st, stop=sp),
                      deps, "T" if inc else None, 1)

    def tr(out, in_, deps=(), inc=False):
        return P.emit("tensor", lambda e, o=out, i=in_: e.transpose(o, i, ident),
                      deps, "T" if inc else None, 1)

    def act(fn, deps=(), inc=False):
        return P.emit("scalar", fn, deps, "A" if inc else None, 1)

    def vec(fn, deps=(), inc=False):
        return P.emit("vector", fn, deps, "V" if inc else None, 1)

    bank_free = [None] * 6
    grp_i = [0]

    def next_bank():
        i = grp_i[0] % 6
        grp_i[0] += 1
        return i, bank_free[i]

    def dbg_dump(name, src_ap, deps):
        if not _DEBUG:
            return None
        th = nc.dram_tensor("d_" + name, list(src_ap.shape), src_ap.dtype,
                            kind="ExternalOutput")
        dbg_dram[name] = th
        dbg_t[name] = dma("sync", th.ap(), src_ap, deps)
        return dbg_t[name]

    def ln_stats(src, b, sq_scr, deps):
        t_sum = vec(lambda e, src=src, b=b: e.reduce_sum(stat[:, b, 0:1], src, AX),
                    deps, inc=True)
        t_prev = None
        for s in range(4):
            d = list(deps) + [t_prev]
            t_q = act(lambda e, src=src, b=b, s=s, sq=sq_scr:
                      e.activation(sq, src[:, s * 512:(s + 1) * 512], AF.Square,
                                   accum_out=stat[:, b, 1:2] if s == 0
                                   else stat[:, b, 7:8]), d, inc=True)
            if s > 0:
                t_prev = vec(lambda e, b=b: e.tensor_tensor(stat[:, b, 1:2],
                                                            stat[:, b, 1:2],
                                                            stat[:, b, 7:8], ALU.add),
                             [t_q, t_prev], inc=True)
            else:
                t_prev = t_q
        return t_sum, t_prev

    def cols_ln_emit(b, deps, eps=EPS_LN):
        t1 = vec(lambda e: e.tensor_scalar(stat[:, b, 2:3], stat[:, b, 0:1],
                                           1.0 / D, None, ALU.mult), deps, inc=True)
        t2 = vec(lambda e: e.tensor_scalar(stat[:, b, 3:4], stat[:, b, 1:2],
                                           1.0 / D, None, ALU.mult), deps, inc=True)
        t3 = vec(lambda e: e.tensor_tensor(stat[:, b, 4:5], stat[:, b, 2:3],
                                           stat[:, b, 2:3], ALU.mult), [t1], inc=True)
        t4 = vec(lambda e: e.tensor_tensor(stat[:, b, 4:5], stat[:, b, 3:4],
                                           stat[:, b, 4:5], ALU.subtract),
                 [t2, t3], inc=True)
        t5 = vec(lambda e: e.tensor_scalar(stat[:, b, 5:6], stat[:, b, 4:5], eps,
                                           None, ALU.add), [t4], inc=True)
        return vec(lambda e: e.reciprocal(stat[:, b, 5:6], stat[:, b, 5:6]),
                   [t5], inc=True)

    dma("gpsimd", ident[:, :], ident_d[:, :])
    dma("gpsimd", ones_c[:, :], ones_c_d[:, :])
    dma("gpsimd", ones_r[:, :], ones_r_d[:, :])
    dma("gpsimd", idxt[:, :, :], idxt_d[:, :, :])
    dma("gpsimd", arrq[:, :], arrq_d[:, :])
    dma("gpsimd", arrk[:, :], arrk_d[:, :])
    dma("gpsimd", normw[:, :], normw_in[:, :])
    dma("gpsimd", slopes[:, :], slope_b[:, :])
    t_cst = dma("gpsimd", flag[:, :], flag_in[:, :])
    t_negs = vec(lambda e: e.tensor_scalar(negs[:, :], slopes[:, :], -1.0, None,
                                           ALU.mult), [t_cst], inc=True)

    wq_r = wqkvT.rearrange("(s p) w -> p s w", p=128)
    wv_r = wv.rearrange("(s p) w -> p s w", p=128)
    wg_r = wgT.rearrange("(s p) w -> p s w", p=128)
    wo_r = woT.rearrange("(s p) w -> p s w", p=128)
    gup_r = gup_w.rearrange("(ee s p) w -> p ee s w", p=128, ee=E)
    dwn_r = down_w.rearrange("(ee s p) w -> p ee s w", p=128, ee=E)
    slab_done = {}
    slab_loaded = {}
    slab_serial = [0]

    def load_slab(src_ap, extra=()):
        s = slab_serial[0]
        d = list(extra)
        if s >= 2:
            d.append(slab_done.get(s - 2))
        slab_loaded[s] = dma("gpsimd", wsl[s % 2].rearrange("p c w -> p (c w)"),
                             src_ap, d)
        slab_serial[0] = s + 1
        return s, wsl[s % 2]

    class Ring:
        def __init__(self, bufs, init_tok=None, sembase="R"):
            self.bufs = bufs
            self.n = len(bufs)
            self.s = 0
            self.done = {}
            self.loaded = {}
            self.init_tok = init_tok
            self.sembase = sembase

        def load(self, src_ap, extra=()):
            i = self.s
            d = list(extra) + [self.done.get(i - self.n, self.init_tok)]
            buf = self.bufs[i % self.n]
            tok = dma("gpsimd", buf.rearrange("p c w -> p (c w)"), src_ap, d,
                      sem=f"{self.sembase}{i % self.n}")
            self.loaded[i] = tok
            self.s = i + 1
            return i, buf, tok

    x_r = x_in.rearrange("(tt p) dd -> p tt dd", p=128)
    p0_evac = []
    for tt in range(8):
        b = tt % 2
        d = [p0_evac[2 * (tt - 2) + 1]] if tt >= 2 else []
        t_x = dma("sync", xin[:, b, :], x_r[:, tt, :], d)
        t_sum, t_ss = ln_stats(xin[:, b, :], b, sqtC[:, :], [t_x])
        t_cols = cols_ln_emit(b, [t_sum, t_ss])
        t_rstd = act(lambda e, b=b: e.activation(stat[:, b, 5:6], stat[:, b, 5:6],
                                                 AF.Sqrt), [t_cols], inc=True)
        t_app = vec(lambda e, b=b: e.tensor_scalar(xn2s[:, b, :], xin[:, b, :],
                                                   stat[:, b, 2:3], stat[:, b, 5:6],
                                                   ALU.subtract, ALU.mult),
                    [t_rstd], inc=True)
        for half in range(2):
            bank = tbk[half]
            tdeps = [t_app, p0_evac[-1] if p0_evac else None]
            tok_t = None
            for c8 in range(8):
                c = half * 8 + c8
                tok_t = tr(bank[:, c8 * 128:(c8 + 1) * 128],
                           xn2s[:, b, c * 128:(c + 1) * 128],
                           tdeps if c8 == 0 else (), inc=(c8 == 7))
            off = 512 * (tt % 2) + tt // 2
            t_ev = act(lambda e, bank=bank, off=off, half=half:
                       e.activation(xn1T[:, half * 8:(half + 1) * 8,
                                         off:off + 509:4],
                                    bank[:, :], AF.Copy), [tok_t], inc=True)
            p0_evac.append(t_ev)
    t_xn1T = p0_evac[-1]
    dbg_dump("xn1T", xn1T[:, :, :], [t_xn1T])

    silu_evac = {}

    def proj_fm(buf, dst, kind, heads, tok_load):
        tok_mm = None
        for obi, h in enumerate(heads):
            for th in range(2):
                bi, bfree = next_bank()
                for kc in range(C):
                    deps = [tok_load, bfree, t_xn1T] if kc == 0 else ()
                    tok_mm = mm(fb[bi][:, :], buf[:, kc, obi * 128:(obi + 1) * 128],
                                xn1T[:, kc, th * 512:(th + 1) * 512],
                                start=(kc == 0), stop=(kc == C - 1),
                                deps=deps, inc=(kc == C - 1))
                t_ev = act(lambda e, bi=bi, dst=dst, h=h, th=th:
                           e.activation(dst[:, h, th * 512:(th + 1) * 512],
                                        fb[bi][:, :], AF.Silu), [tok_mm], inc=True)
                bank_free[bi] = t_ev
                silu_evac[(kind, h, th)] = t_ev
        return tok_mm

    for s in range(8):
        ss, buf = load_slab(wq_r[:, s, :])
        slab_done[ss] = proj_fm(buf, kf, "k", [2 * s, 2 * s + 1], slab_loaded[ss])
    dbg_dump("k", kf[:, :, :], [silu_evac[("k", NH - 1, 1)]])
    tb_guard = t_xn1T
    vring = Ring(vbuf, sembase="VB")
    for s in range(4):
        vi, buf, tok_ld = vring.load(wv_r[:, s, :])
        tok_mm = None
        for tt in range(8):
            bi, bfree = next_bank()
            for kc in range(C):
                deps = [tok_ld, bfree, t_xn1T] if kc == 0 else ()
                tok_mm = mm(fb[bi][:, :], xn1T[:, kc, tt * 128:(tt + 1) * 128],
                            buf[:, kc, :], start=(kc == 0), stop=(kc == C - 1),
                            deps=deps, inc=(kc == C - 1))
            t_ev = act(lambda e, bi=bi, tt=tt, s=s:
                       e.activation(v_tm[:, tt, 512 * s:512 * (s + 1)],
                                    fb[bi][:, :], AF.Silu), [tok_mm], inc=True)
            bank_free[bi] = t_ev
            silu_evac[("v", tt, s)] = t_ev
        vring.done[vi] = tok_mm
    v_last_mm = tok_mm
    dbg_dump("v", v_tm[:, :, :], [silu_evac[("v", 7, 3)]])

    cc_dma = {}
    kvst_dma = {}
    pass1_prev = None
    for h in range(NH):
        hb = h % 2
        t_lam = act(lambda e, h=h: e.activation(lamt[:, 0:1], negs[:, h:h + 1],
                                                AF.Exp, scale=256.0),
                    [t_negs, pass1_prev], inc=True)
        t_kd = act(lambda e, h=h: e.activation(kdec[:, :], arrk[:, :], AF.Exp,
                                               scale=negs[:, h:h + 1]),
                   [t_negs, t_cst], inc=True)
        tok_kvup = None
        tok_st = None
        for t in range(NB):
            ktoks = []
            for j in range(2):
                bank = tbk[j]
                deps = [silu_evac[("k", h, (t * 256 + j * 128) // 512)], tb_guard]
                tok_t = tr(bank[:, 0:128],
                           kf[:, h, t * 256 + j * 128:t * 256 + (j + 1) * 128],
                           deps, inc=True)
                t_ke = vec(lambda e, j=j, bank=bank:
                           e.tensor_scalar(ktil[:, j, :], bank[:, 0:128],
                                           kdec[:, j:j + 1], None, ALU.mult),
                           [tok_t, t_kd], inc=True)
                tb_guard = t_ke
                ktoks.append(t_ke)
            if t >= 1:
                tok_st = vec(lambda e, t=t, hb=hb:
                             e.tensor_copy(kvst[:, hb, t - 1, :], kvrun[:, hb, :]),
                             [tok_kvup, kvst_dma.get(h - 2)], inc=True)
            bi, bfree = next_bank()
            tok_c = None
            for j in range(2):
                deps = [ktoks[j], silu_evac[("v", 2 * t + j, h // 4)]]
                if j == 0:
                    deps.append(bfree)
                tok_c = mm(fb[bi][:, 0:128], ktil[:, j, :],
                           v_tm[:, 2 * t + j, h * 128:(h + 1) * 128],
                           start=(j == 0), stop=(j == 1), deps=deps, inc=(j == 1))
            if t == 0:
                tok_kvup = vec(lambda e, hb=hb, bi=bi:
                               e.tensor_copy(kvrun[:, hb, :], fb[bi][:, 0:128]),
                               [tok_c, cc_dma.get(h - 2)], inc=True)
            else:
                tok_kvup = vec(lambda e, hb=hb, bi=bi:
                               e.scalar_tensor_tensor(kvrun[:, hb, :], kvrun[:, hb, :],
                                                      lamt[:, 0:1], fb[bi][:, 0:128],
                                                      ALU.mult, ALU.add),
                               [tok_c, t_lam, tok_kvup], inc=True)
            bank_free[bi] = tok_kvup
        kvst_dma[h] = dma("sync", kvl_d[h * 128:(h + 1) * 128, :],
                          kvst[:, hb, :, :].rearrange("p a w -> p (a w)"), [tok_st])
        cc_dma[h] = dma("sync", cc_in.ap()[:, h * 128:(h + 1) * 128],
                        kvrun[:, hb, :], [tok_kvup])
        pass1_prev = tok_kvup

    for s in range(8):
        ss, buf = load_slab(wq_r[:, 8 + s, :])
        slab_done[ss] = proj_fm(buf, qf, "q", [2 * s, 2 * s + 1], slab_loaded[ss])
    dbg_dump("q", qf[:, :, :], [silu_evac[("q", NH - 1, 1)]])

    for s in range(8):
        ss, buf = load_slab(wg_r[:, s, :])
        tok_mm = None
        for obi in range(2):
            c = 2 * s + obi
            for th in range(2):
                bi, bfree = next_bank()
                for kc in range(C):
                    deps = [slab_loaded[ss], bfree, t_xn1T] if kc == 0 else ()
                    tok_mm = mm(fb[bi][:, :], buf[:, kc, obi * 128:(obi + 1) * 128],
                                xn1T[:, kc, th * 512:(th + 1) * 512],
                                start=(kc == 0), stop=(kc == C - 1),
                                deps=deps, inc=(kc == C - 1))
                t_sg = act(lambda e, bi=bi, c=c, th=th:
                           e.activation(gp[:, c, th * 512:(th + 1) * 512],
                                        fb[bi][:, :], AF.Sigmoid), [tok_mm], inc=True)
                t_nw = vec(lambda e, c=c, th=th:
                           e.tensor_scalar(gp[:, c, th * 512:(th + 1) * 512],
                                           gp[:, c, th * 512:(th + 1) * 512],
                                           normw[:, c:c + 1], None, ALU.mult),
                           [t_sg, t_cst], inc=True)
                bank_free[bi] = t_nw
        slab_done[ss] = tok_mm
        t_gp_last = t_nw
    t_gate_mm_last = slab_done[23]
    dbg_dump("gp", gp[:, :, :], [t_gp_last])

    t_cc = P.emit("gpsimd",
                  lambda e: e.collective_compute(
                      "AllGather", ALU.bypass,
                      replica_groups=[[0, 1], [2, 3], [4, 5], [6, 7]],
                      ins=[cc_in.ap()[:, :]], outs=[cc_out.ap()[:, :]]),
                  [cc_dma[h] for h in range(NH)], "CC", 1)

    t_kvld = dma("sync", kvinit[:, :], cc_out.ap()[0:128, :],
                 [t_cc, t_gate_mm_last])
    t_kvfl = vec(lambda e: e.tensor_scalar(kvinit[:, :], kvinit[:, :], flag[:, 0:1],
                                           None, ALU.mult), [t_kvld, t_cst], inc=True)
    dbg_dump("kvinit", kvinit[:, :], [t_kvfl])
    t_qkz = vec(lambda e: e.memset(qks[:, 0, 1, 0:128], 0.0), [], inc=True)
    t_qkz2 = vec(lambda e: e.memset(qks[:, 1, 1, 0:128], 0.0), [], inc=True)

    o_evac = {}
    pass2_last = None
    kvt_free = [None, None]
    qtil_free = [None, None]
    qks_free = [t_qkz, t_qkz2]
    kvld_dma = {}
    kvld_free = [None, None]
    pend_o = None
    last_qke1 = None
    hstate = {}
    for i in range(NH * NB + 1):
        if pend_o is not None:
            (ph, pt, ptb2, pqke0, pqke1, pkvt, pqt, pbio, pbfree_o) = pend_o
            mm(fb[pbio][:, 0:256], v_tm[:, 2 * pt, ph * 128:(ph + 1) * 128],
               qks[:, ptb2, 0, :], start=True, stop=False,
               deps=[pqke0, pbfree_o, silu_evac[("v", 2 * pt, ph // 4)]])
            mm(fb[pbio][:, 128:256], v_tm[:, 2 * pt + 1, ph * 128:(ph + 1) * 128],
               qks[:, ptb2, 1, 128:256], start=False, stop=False,
               deps=[pqke1, silu_evac[("v", 2 * pt + 1, ph // 4)]])
            tok_o = mm(fb[pbio][:, 0:256], kvt[:, ptb2, :], qtil[:, ptb2, :],
                       start=False, stop=True, deps=[pkvt, pqt], inc=True)
            kvt_free[ptb2] = tok_o
            qtil_free[ptb2] = tok_o
            qks_free[ptb2] = tok_o
            t_oe = act(lambda e, pbio=pbio, ph=ph, pt=pt:
                       e.activation(kf[:, ph, pt * 256:(pt + 1) * 256],
                                    fb[pbio][:, 0:256], AF.Copy), [tok_o], inc=True)
            o_evac[(ph, pt)] = t_oe
            bank_free[pbio] = t_oe
            pass2_last = t_oe
            if pt == 3:
                kvld_free[ph % 2] = t_oe
        if i == NH * NB:
            break
        h, t = divmod(i, NB)
        if t == 0:
            kvld_dma[h] = dma("sync",
                              kvld[:, h % 2, :, :].rearrange("p a w -> p (a w)"),
                              kvl_d[h * 128:(h + 1) * 128, :],
                              [kvst_dma[NH - 1] if h == 0 else None,
                               kvld_free[h % 2]], sem=f"K{h % 2}")
            t_dg = act(lambda e, h=h: e.activation(diagT[:, :, :], idxt[:, :, :],
                                                   AF.Exp, scale=negs[:, h:h + 1]),
                       [t_negs, t_cst, last_qke1], inc=True)
            t_qr = act(lambda e, h=h: e.activation(qdrow[:, :], arrq[:, :], AF.Exp,
                                                   scale=negs[0:1, h:h + 1]),
                       [t_cst, last_qke1], inc=True)
            bi_q, bfree_q = next_bank()
            t_qb = mm(fb[bi_q][:, 0:256], ones_r[:, :], qdrow[:, :], True, True,
                      [t_qr, bfree_q], inc=True)
            t_qbe = act(lambda e, bi_q=bi_q:
                        e.activation(qdecb[:, :], fb[bi_q][:, 0:256], AF.Copy),
                        [t_qb], inc=True)
            bank_free[bi_q] = t_qbe
            lam_toks = [act(lambda e, h=h, t2=t2:
                            e.activation(lamt[:, t2:t2 + 1], negs[:, h:h + 1],
                                         AF.Exp, scale=256.0 * t2),
                            [t_negs, last_qke1], inc=True)
                        for t2 in range(1, NB)]
            hstate = {"qbe": t_qbe, "dg": t_dg, "lam": lam_toks}
        tb2 = t % 2
        if t == 0:
            t_kvt = P.emit("gpsimd",
                           lambda e, h=h, tb2=tb2:
                           e.tensor_copy(kvt[:, tb2, :],
                                         kvinit[:, h * 128:(h + 1) * 128]),
                           [t_kvfl, kvt_free[tb2]], "PL", 1)
        else:
            t_kvt = vec(lambda e, h=h, t=t, tb2=tb2:
                        e.scalar_tensor_tensor(kvt[:, tb2, :],
                                               kvinit[:, h * 128:(h + 1) * 128],
                                               lamt[:, t:t + 1],
                                               kvld[:, h % 2, t - 1, :],
                                               ALU.mult, ALU.add),
                        [t_kvfl, hstate["lam"][t - 1], kvld_dma[h],
                         kvt_free[tb2]], inc=True)
        th = t // 2
        t_qt = P.emit("gpsimd",
                      lambda e, h=h, t=t, tb2=tb2:
                      e.tensor_tensor(qtil[:, tb2, :],
                                      qf[:, h, t * 256:(t + 1) * 256],
                                      qdecb[:, :], ALU.mult),
                      [hstate["qbe"], silu_evac[("q", h, th)], qtil_free[tb2]],
                      "PL", 1)
        bi_qk, bfree_qk = next_bank()
        qk_mm = []
        for j in range(2):
            deps = [silu_evac[("k", h, th)], silu_evac[("q", h, th)]]
            if j == 0:
                deps.append(bfree_qk)
            qk_mm.append(mm(fb[bi_qk][:, j * 256:(j + 1) * 256],
                            kf[:, h, t * 256 + j * 128:t * 256 + (j + 1) * 128],
                            qf[:, h, t * 256:(t + 1) * 256],
                            start=True, stop=True, deps=deps, inc=True))
        t_qke0 = vec(lambda e, bi_qk=bi_qk, tb2=tb2:
                     e.tensor_tensor(qks[:, tb2, 0, :], fb[bi_qk][:, 0:256],
                                     diagT[:, 0, :], ALU.mult),
                     [qk_mm[0], hstate["dg"], qks_free[tb2]], inc=True)
        t_qke1 = vec(lambda e, bi_qk=bi_qk, tb2=tb2:
                     e.tensor_tensor(qks[:, tb2, 1, 128:256],
                                     fb[bi_qk][:, 384:512],
                                     diagT[:, 1, 128:256], ALU.mult),
                     [qk_mm[1], hstate["dg"]], inc=True)
        bank_free[bi_qk] = t_qke1
        last_qke1 = t_qke1
        bi_o, bfree_o = next_bank()
        pend_o = (h, t, tb2, t_qke0, t_qke1, t_kvt, t_qt, bi_o, bfree_o)

    dbg_dump("o", kf[:, :, :], [pass2_last])
    bi_r0, bfree_r0 = next_bank()
    bi_r1, bfree_r1 = next_bank()
    mm_hist = []
    for c in range(C):
        b = c % 2
        deps = [o_evac[(c, t)] for t in range(NB)] + [pass2_last]
        if len(mm_hist) >= 2:
            deps += mm_hist[-2]
        t_sq = vec(lambda e, c=c, b=b: e.tensor_tensor(sqG[:, b, :], kf[:, c, :],
                                                       kf[:, c, :], ALU.mult),
                   deps, inc=True)
        cur = []
        for th in range(2):
            bi = bi_r0 if th == 0 else bi_r1
            rhs = sqG[:, b, :].rearrange("p (tq t4) -> p t4 tq",
                                         t4=4)[:, 2 * th:2 * th + 2, :]
            deps2 = [t_sq]
            if c == 0:
                deps2.append(bfree_r0 if th == 0 else bfree_r1)
            cur.append(mm(fb[bi][0:1, :], ones_c[:, :], rhs,
                          start=(c == 0), stop=(c == C - 1), deps=deps2, inc=True))
        mm_hist.append(cur)

    r1 = vec(lambda e: e.tensor_scalar(rrow[0:1, 0:512], fb[bi_r0][0:1, :],
                                       1.0 / D, EPS_RMS, ALU.mult, ALU.add),
             mm_hist[-1], inc=True)
    r2 = vec(lambda e: e.tensor_scalar(rrow[0:1, 512:1024], fb[bi_r1][0:1, :],
                                       1.0 / D, EPS_RMS, ALU.mult, ALU.add),
             mm_hist[-1], inc=True)
    r3 = vec(lambda e: e.reciprocal(rrow[0:1, 0:512], rrow[0:1, 0:512]),
             [r1], inc=True)
    t_rm = vec(lambda e: e.reciprocal(rrow[0:1, 512:1024], rrow[0:1, 512:1024]),
               [r2, r3], inc=True)
    t_rs = act(lambda e: e.activation(rrow[0:1, :], rrow[0:1, :], AF.Sqrt), [t_rm],
               inc=True)
    bank_free[bi_r0] = t_rs
    bank_free[bi_r1] = t_rs
    t_rw = dma("sync", rstd_d[0:1, :], rrow[0:1, :], [t_rs])
    t_rc = []
    for gt in range(8):
        t_rc.append(dma("sync", rstdc[:, gt:gt + 1],
                        rstd_d.rearrange("o (gt p) -> p (o gt)", p=128)[:, gt:gt + 1],
                        [t_rw]))

    gz_tok = {}
    for c in range(C):
        for e4 in range(4):
            deps = [o_evac[(c, t)] for t in range(NB)] + [pass2_last]
            gz_tok[(c, e4)] = vec(
                lambda e, c=c, e4=e4: e.tensor_tensor(
                    gz[:, c, e4 * 256:(e4 + 1) * 256],
                    gp[:, c, e4:T:4], kf[:, c, e4:T:4], ALU.mult),
                deps, inc=True)
    t_gz_all = gz_tok[(C - 1, 3)]

    wo_tok = {}
    for s in range(2):
        wo_tok[s] = dma("gpsimd", woB[:, s, :, :].rearrange("p c w -> p (c w)"),
                        wo_r[:, s, :], [pass2_last, dbg_t.get("q")])
    for s in range(2, 4):
        wo_tok[s] = dma("gpsimd", woE[:, s - 2, :, :].rearrange("p c w -> p (c w)"),
                        wo_r[:, s, :], [t_gz_all])
    wo_all = wo_tok[3]

    def wo_res(s):
        return woB[:, s, :, :] if s < 2 else woE[:, s - 2, :, :]

    xin_free = None
    aout_free = [None, None]
    p6_xn2_evac = []
    p6_spill = {}
    pend_tr = []
    p6_last_mm_tok = None
    for gt in range(8):
        gi = gt % 2
        ev_toks2 = []
        for s in range(4):
            bi, bfree = next_bank()
            for kc in range(C):
                deps = ([wo_all, bfree, t_gz_all, aout_free[gi],
                         dbg_t.get("o"), dbg_t.get("kvinit")]
                        if kc == 0 else ())
                tok_mm = mm(fb[bi][:, :], gz[:, kc, gt * 128:(gt + 1) * 128],
                            wo_res(s)[:, kc, :], start=(kc == 0), stop=(kc == C - 1),
                            deps=deps, inc=(kc == C - 1))
            t_ev = vec(lambda e, bi=bi, gi=gi, s=s:
                       e.tensor_copy(aout[:, gi, s * 512:(s + 1) * 512],
                                     fb[bi][:, :]), [tok_mm], inc=True)
            bank_free[bi] = t_ev
            ev_toks2.append(t_ev)
            p6_last_mm_tok = tok_mm
        if True:
            ev_toks = {gi: ev_toks2}
            b = gt % 2
            t_sum, t_ss = ln_stats(aout[:, gi, :], b, sqtC[:, :], ev_toks[gi])

            u1 = vec(lambda e, b=b: e.tensor_scalar(stat[:, b, 2:3],
                                                    stat[:, b, 0:1], 1.0 / D, None,
                                                    ALU.mult), [t_sum], inc=True)
            u2 = vec(lambda e, b=b: e.tensor_scalar(stat[:, b, 3:4],
                                                    stat[:, b, 1:2], 1.0 / D, None,
                                                    ALU.mult), [t_ss], inc=True)
            u3 = vec(lambda e, b=b: e.tensor_tensor(stat[:, b, 4:5],
                                                    stat[:, b, 2:3],
                                                    stat[:, b, 2:3], ALU.mult),
                     [u1], inc=True)
            u4 = vec(lambda e, b=b: e.tensor_tensor(stat[:, b, 4:5],
                                                    stat[:, b, 3:4],
                                                    stat[:, b, 4:5], ALU.subtract),
                     [u2, u3], inc=True)
            u5 = vec(lambda e, b=b, gt=gt: e.tensor_tensor(stat[:, b, 5:6],
                                                           rstdc[:, gt:gt + 1],
                                                           rstdc[:, gt:gt + 1],
                                                           ALU.mult),
                     t_rc + [u2], inc=True)
            u6 = vec(lambda e, b=b: e.tensor_tensor(stat[:, b, 5:6],
                                                    stat[:, b, 4:5],
                                                    stat[:, b, 5:6], ALU.mult),
                     [u4, u5], inc=True)
            u7 = vec(lambda e, b=b: e.tensor_scalar(stat[:, b, 5:6],
                                                    stat[:, b, 5:6], EPS_LN, None,
                                                    ALU.add), [u6], inc=True)
            t_c1 = vec(lambda e, b=b: e.reciprocal(stat[:, b, 5:6],
                                                   stat[:, b, 5:6]), [u7], inc=True)
            t_c2 = act(lambda e, b=b: e.activation(stat[:, b, 5:6], stat[:, b, 5:6],
                                                   AF.Sqrt), [t_c1], inc=True)
            t_c3 = vec(lambda e, b=b, gt=gt:
                       e.tensor_tensor(stat[:, b, 6:7], stat[:, b, 5:6],
                                       rstdc[:, gt:gt + 1], ALU.mult), [t_c2],
                       inc=True)
            t_x = dma("sync", xin[:, 0, :], x_r[:, gt, :], [xin_free, t_gz_all])
            t_a = vec(lambda e, b=b, gi=gi:
                      e.tensor_scalar(aout[:, gi, :], aout[:, gi, :], stat[:, b, 2:3],
                                      stat[:, b, 6:7], ALU.subtract, ALU.mult),
                      [t_c3], inc=True)
            t_a2 = vec(lambda e, gi=gi:
                       e.tensor_tensor(aout[:, gi, :], aout[:, gi, :], xin[:, 0, :],
                                       ALU.add), [t_a, t_x], inc=True)
            xin_free = t_a2
            t_sum2, t_ss2 = ln_stats(aout[:, gi, :], b, sqtC[:, :], [t_a2])
            t_cols2 = cols_ln_emit(b, [t_sum2, t_ss2])
            t_rstd2 = act(lambda e, b=b: e.activation(stat[:, b, 5:6],
                                                      stat[:, b, 5:6], AF.Sqrt),
                          [t_cols2], inc=True)
            t_app2 = vec(lambda e, b=b, gi=gi:
                         e.tensor_scalar(xn2s[:, b, :], aout[:, gi, :],
                                         stat[:, b, 2:3], stat[:, b, 5:6],
                                         ALU.subtract, ALU.mult),
                         [t_rstd2,
                          p6_xn2_evac[-1] if p6_xn2_evac else None],
                         inc=True)
            t_sp = dma("sync",
                       x2sp.rearrange("(tt p) dd -> p tt dd", p=128)[:, gt, :],
                       aout[:, gi, :], [t_app2])
            p6_spill[gt] = t_sp
            aout_free[gi] = t_sp
            for half in range(2):
                bank = tbk[half]
                tdeps = [t_app2, p6_xn2_evac[-1] if p6_xn2_evac else None]
                tok_t = None
                for c8 in range(8):
                    c = half * 8 + c8
                    tok_t = tr(bank[:, c8 * 128:(c8 + 1) * 128],
                               xn2s[:, b, c * 128:(c + 1) * 128],
                               tdeps if c8 == 0 else (), inc=(c8 == 7))
                t_ev = act(lambda e, bank=bank, gt=gt, half=half:
                           e.activation(xn2T[:, half * 8:(half + 1) * 8,
                                             gt * 128:(gt + 1) * 128],
                                        bank[:, :], AF.Copy), [tok_t], inc=True)
                p6_xn2_evac.append(t_ev)
    t_xn2T = p6_xn2_evac[-1]
    p6_mm_last = p6_last_mm_tok

    ring8 = Ring(dw + [rw8], init_tok=p6_mm_last)
    x2r_free = [None, None]
    obuf_free = [None, None]
    sg_free = [None, None]
    out_toks = []
    down_last_prev = None
    for e_i in range(E):
        t_int = None
        for s in range(8):
            ss, slab, tok_ld = ring8.load(gup_r[:, e_i, s, :])
            tokU = None
            for pr in range(2):
                i_blk = 2 * s + pr
                biG, bfreeG = next_bank()
                biU, bfreeU = next_bank()
                tokG = None
                for kc in range(C):
                    dG = ([tok_ld, bfreeG, t_xn2T, dbg_t.get("q"),
                           dbg_t.get("gp")] if kc == 0 else ())
                    tokG = mm(fb[biG][:, 0:256], slab[:, kc, 256 * pr:256 * pr + 128],
                              xn2T[:, kc, e_i * 256:(e_i + 1) * 256],
                              start=(kc == 0), stop=(kc == C - 1), deps=dG,
                              inc=(kc == C - 1))
                for kc in range(C):
                    dU = [bfreeU] if kc == 0 else ()
                    tokU = mm(fb[biU][:, 0:256],
                              slab[:, kc, 256 * pr + 128:256 * (pr + 1)],
                              xn2T[:, kc, e_i * 256:(e_i + 1) * 256],
                              start=(kc == 0), stop=(kc == C - 1), deps=dU,
                              inc=(kc == C - 1))
                sb = i_blk % 2
                t_sil = act(lambda e, biG=biG, sb=sb:
                            e.activation(sgate[:, sb, :], fb[biG][:, 0:256], AF.Silu),
                            [tokG, sg_free[sb]], inc=True)
                t_int = vec(lambda e, biU=biU, sb=sb, i_blk=i_blk:
                            e.tensor_tensor(inter[:, i_blk, :], fb[biU][:, 0:256],
                                            sgate[:, sb, :], ALU.mult),
                            [tokU, t_sil, down_last_prev,
                             pass2_last if (e_i == 0 and i_blk == 0) else None],
                            inc=True)
                bank_free[biG] = t_int
                bank_free[biU] = t_int
                sg_free[sb] = t_int
            ring8.done[ss] = tokU
        inter_last = t_int
        ev_toks = [[], []]
        for s in range(4):
            ss, slab, tok_ld = ring8.load(dwn_r[:, e_i, s, :])
            tok_mm = None
            for gt in range(2):
                bi, bfree = next_bank()
                for kc in range(C):
                    deps = ([tok_ld, bfree, inter_last,
                             obuf_free[gt]] if kc == 0 else ())
                    tok_mm = mm(fb[bi][:, :],
                                inter[:, kc, gt * 128:(gt + 1) * 128],
                                slab[:, kc, :], start=(kc == 0), stop=(kc == C - 1),
                                deps=deps, inc=(kc == C - 1))
                t_ev = vec(lambda e, bi=bi, gt=gt, s=s:
                           e.tensor_copy(obuf[:, gt, s * 512:(s + 1) * 512],
                                         fb[bi][:, :]), [tok_mm], inc=True)
                bank_free[bi] = t_ev
                ev_toks[gt].append(t_ev)
            ring8.done[ss] = tok_mm
            down_last_prev = tok_mm
        for gt in range(2):
            g_glob = e_i * 2 + gt
            b = gt
            t_x2 = dma("sync", x2r[:, b, :],
                       x2sp.rearrange("(tt p) dd -> p tt dd", p=128)[:, g_glob, :],
                       [p6_spill[g_glob], x2r_free[b], pass2_last, p6_mm_last])
            t_sum4, t_ss4 = ln_stats(obuf[:, gt, :], b, sqtC[:, :], ev_toks[gt])
            t_cols4 = cols_ln_emit(b, [t_sum4, t_ss4])
            t_rstd4 = act(lambda e, b=b: e.activation(stat[:, b, 5:6],
                                                      stat[:, b, 5:6], AF.Sqrt),
                          [t_cols4], inc=True)
            t_a = vec(lambda e, b=b, gt=gt:
                      e.tensor_scalar(obuf[:, gt, :], obuf[:, gt, :], stat[:, b, 2:3],
                                      stat[:, b, 5:6], ALU.subtract, ALU.mult),
                      [t_rstd4], inc=True)
            t_a2 = vec(lambda e, b=b, gt=gt:
                       e.tensor_tensor(obuf[:, gt, :], obuf[:, gt, :], x2r[:, b, :],
                                       ALU.add), [t_a, t_x2], inc=True)
            t_out = dma("sync",
                        out_d.rearrange("(tt p) dd -> p tt dd", p=128)[:, g_glob, :],
                        obuf[:, gt, :], [t_a2])
            out_toks.append(t_out)
            obuf_free[gt] = t_out
            x2r_free[b] = t_out

    if _DEBUG:
        out_toks.append(dbg_dump("gz", gz[:, :, :], [t_gz_all]))
        out_toks.append(dbg_dump("rrow", rrow[0:1, :], [t_rs]))
        out_toks.append(dbg_dump("xn2T", xn2T[:, :, :], [t_xn2T]))
        out_toks.append(dbg_dump("x2", x2sp[:, :], [p6_spill[gt2] for gt2 in range(8)]))
        out_toks.append(dbg_dump("kvl", kvl_d[:, :], [kvst_dma[hh] for hh in range(NH)]))
        out_toks.append(dbg_dump("ccin", cc_in.ap()[:, :],
                                 [cc_dma[hh] for hh in range(NH)]))
        out_toks.append(dbg_dump("inter", inter[:, :, :], out_toks[-1:]))
        out_toks.extend([dbg_t[k] for k in ("xn1T", "k", "v", "q", "gp", "kvinit", "o")])
    P.emit("sync", lambda e: e.nop(), out_toks)
    P.emit("gpsimd", lambda e: e.nop(), out_toks)

    from contextlib import ExitStack
    with ExitStack() as es:
        sems = {s: es.enter_context(nc.semaphore(s))
                for s in ("S", "G", "T", "V", "A", "CC", "R0", "R1", "R2",
                          "VB0", "VB1", "K0", "K1", "PL")}
        with nc.Block() as block:
            block.sync(lambda e: P.run("sync", e, sems))
            block.gpsimd(lambda e: P.run("gpsimd", e, sems))
            block.tensor(lambda e: P.run("tensor", e, sems))
            block.vector(lambda e: P.run("vector", e, sems))
            block.scalar(lambda e: P.run("scalar", e, sems))
    return nc


_PERM = 4 * (np.arange(1024) % 256) + np.arange(1024) // 256
_IPERM = np.argsort(_PERM)
_NC_CACHE = [None]
_LAST_RES = [None]
_DEBUG = bool(os.environ.get("KDEBUG"))


def kernel(L0_attn_hidden_states, L0_attn_slope_rate, L0_attn_w_qkv, L0_attn_w_gate,
           L0_attn_w_out, L0_attn_norm_w, L0_ffn_gate_up_proj, L0_ffn_down_proj):
    x = np.asarray(L0_attn_hidden_states, np.float32)
    slope = np.asarray(L0_attn_slope_rate, np.float32).reshape(1, NH)
    w_qkv = np.asarray(L0_attn_w_qkv, np.float32)
    w_gate = np.asarray(L0_attn_w_gate, np.float32)
    w_out = np.asarray(L0_attn_w_out, np.float32)
    norm_w = np.asarray(L0_attn_norm_w, np.float32)
    gup = np.asarray(L0_ffn_gate_up_proj, np.float32)
    down = np.asarray(L0_ffn_down_proj, np.float32)

    def slabify(wT, w=256):
        nsl = wT.shape[1] // w
        a = wT.reshape(C, 128, nsl, w)
        return np.ascontiguousarray(a.transpose(2, 1, 0, 3).reshape(nsl * 128, C * w))

    k_cols = np.concatenate([h * 384 + 128 + np.arange(128) for h in range(NH)])
    v_cols = np.concatenate([h * 384 + 256 + np.arange(128) for h in range(NH)])
    q_cols = np.concatenate([h * 384 + np.arange(128) for h in range(NH)])
    bf = ml_dtypes.bfloat16
    wqkvT = slabify(w_qkv.T[:, np.concatenate([k_cols, q_cols])]).astype(bf)
    wv_h = slabify(w_qkv.T[:, v_cols], 512).astype(bf)
    wgT = slabify(w_gate.T).astype(bf)
    woT = slabify(w_out.T, 512).astype(bf)
    gup_cols = np.concatenate([np.concatenate([i * 128 + np.arange(128),
                                               2048 + i * 128 + np.arange(128)])
                               for i in range(16)])
    gup_h = np.concatenate([slabify(gup[e][:, gup_cols], 512)
                            for e in range(E)]).astype(bf)
    down_h = np.concatenate([slabify(down[e], 512) for e in range(E)]).astype(bf)
    slope_bc = np.ascontiguousarray(np.broadcast_to(slope, (128, NH)))
    normw_h = np.ascontiguousarray(norm_w.reshape(C, 128).T)

    if _NC_CACHE[0] is None:
        _NC_CACHE[0] = build_nc()
    nc = _NC_CACHE[0]

    in_maps = []
    for c in range(8):
        b, hf = divmod(c, 2)
        x_sh = np.ascontiguousarray(x[b, hf * T:(hf + 1) * T][_PERM])
        in_maps.append({
            "x_sh": x_sh, "wqkvT": wqkvT, "wv": wv_h, "wgT": wgT,
            "woT": woT,
            "gup_w": gup_h, "down_w": down_h, "slope_b": slope_bc,
            "normw": normw_h,
            "flag": np.full((128, 1), 1.0 if c % 2 else 0.0, np.float32),
        })
    res = run_bass_kernel_spmd(nc, in_maps, core_ids=list(range(8)))
    if _DEBUG:
        _LAST_RES[0] = res
    out = np.empty((4, 2048, D), np.float32)
    for c in range(8):
        b, hf = divmod(c, 2)
        out[b, hf * T:(hf + 1) * T] = np.asarray(res.results[c]["out"])[_IPERM]
    return out


# version 21
# speedup vs baseline: 1.3642x; 1.1076x over previous
import os
import numpy as np
import ml_dtypes
import concourse.bass as bass
import concourse.mybir as mybir
from concourse.bass_utils import run_bass_kernel_spmd

F32 = mybir.dt.float32
BF16 = mybir.dt.bfloat16
AF = mybir.ActivationFunctionType
ALU = mybir.AluOpType
AX = mybir.AxisListType.X

NH, HD, BLK = 16, 128, 256
D = 2048
T = 1024
C = 16
NB = 4
E = 4
EPS_LN = 1e-5
EPS_RMS = 1e-6
KB = 1024

OFF_A = 0 * KB
OFF_B = 32 * KB
OFF_C = 64 * KB
OFF_D = 96 * KB
OFF_E = 128 * KB
OFF_G = 160 * KB
OFF_W = 175 * KB
OFF_S = 191 * KB
ARENA = 196 * KB


class Prog:

    ENGS = ("sync", "gpsimd", "tensor", "vector", "scalar")

    def __init__(self):
        self.ops = {e: [] for e in self.ENGS}
        self.count = {}

    def emit(self, eng, fn, deps=(), inc=None, n=1):
        deps = tuple(d for d in deps if d is not None)
        self.ops[eng].append((fn, deps, inc, n))
        if inc is not None:
            self.count[inc] = self.count.get(inc, 0) + n
            return (inc, self.count[inc])
        return None

    def run(self, engname, eng, sems):
        seen = {}
        for fn, deps, inc, n in self.ops[engname]:
            for s, v in deps:
                if seen.get(s, 0) < v:
                    eng.wait_ge(sems[s], v)
                    seen[s] = v
            ins = fn(eng)
            if inc is not None:
                ins.then_inc(sems[inc], n)


_DEBUG = bool(os.environ.get("KDEBUG"))


def build_nc():
    nc = bass.Bass("TRN2", target_bir_lowering=False)
    dbg_t = {}
    dbg_dram = {}

    x_in = nc.dram_tensor("x_sh", [T, D], F32, kind="ExternalInput").ap()
    wqkvT = nc.dram_tensor("wqkvT", [16 * 128, C * 256], BF16, kind="ExternalInput").ap()
    wv = nc.dram_tensor("wv", [4 * 128, C * 512], BF16, kind="ExternalInput").ap()
    wgT = nc.dram_tensor("wgT", [8 * 128, C * 256], BF16, kind="ExternalInput").ap()
    woT = nc.dram_tensor("woT", [4 * 128, C * 512], BF16, kind="ExternalInput").ap()
    gup_w = nc.dram_tensor("gup_w", [E * 8 * 128, C * 512], BF16, kind="ExternalInput").ap()
    down_w = nc.dram_tensor("down_w", [E * 4 * 128, C * 512], BF16, kind="ExternalInput").ap()
    slope_b = nc.dram_tensor("slope_b", [128, NH], F32, kind="ExternalInput").ap()
    normw_in = nc.dram_tensor("normw", [128, C], F32, kind="ExternalInput").ap()
    flag_in = nc.dram_tensor("flag", [128, 1], F32, kind="ExternalInput").ap()
    out_d = nc.dram_tensor("out", [T, D], F32, kind="ExternalOutput").ap()

    cc_in = nc.dram_tensor("cc_in", [128, NH * HD], F32, kind="Internal")
    cc_out = nc.dram_tensor("cc_out", [256, NH * HD], F32, kind="Internal")
    x2sp = nc.dram_tensor("x2spill", [T, D], F32, kind="Internal").ap()
    kvl_d = nc.dram_tensor("kvloc_d", [NH * 128, 3 * HD], F32, kind="Internal").ap()

    idxt_np = np.empty((128, 2, BLK), np.float32)
    for j in range(2):
        n_idx = (j * 128 + np.arange(128))[:, None]
        m_idx = np.arange(BLK)[None, :]
        idxt_np[:, j, :] = np.where(m_idx >= n_idx,
                                    (m_idx - n_idx).astype(np.float32), 1e9)
    arrq_np = np.arange(1, BLK + 1, dtype=np.float32).reshape(1, BLK)
    arrk_np = np.empty((128, 2), np.float32)
    for j in range(2):
        arrk_np[:, j] = 255.0 - (j * 128 + np.arange(128))
    idxt_d = nc.inline_tensor(idxt_np, "idxt_c").ap()
    arrq_d = nc.inline_tensor(arrq_np, "arrq_c").ap()
    arrk_d = nc.inline_tensor(arrk_np, "arrk_c").ap()
    ident_d = nc.inline_tensor(np.eye(128, dtype=np.float32), "ident_c").ap()
    ones_c_d = nc.inline_tensor(np.ones((128, 1), np.float32), "ones_c_c").ap()
    ones_r_d = nc.inline_tensor(np.ones((1, 128), np.float32), "ones_r_c").ap()

    nc.alloc_sbuf_tensor("arena", [128, ARENA], mybir.dt.uint8)
    base = nc.sbuf_base - ARENA

    def place(name, shape, dtype, off):
        return nc.alloc_sbuf_tensor_at(name, shape, dtype, offset=base + off).ap()

    xn1T = place("xn1T", [128, C, T], BF16, OFF_A)
    kvinit = place("kvinit", [128, NH * HD], F32, OFF_A)
    xn2T = place("xn2T", [128, C, T], BF16, OFF_A)
    qf = place("qf", [128, NH, T], BF16, OFF_B)
    vbuf = [place(f"vb{i}", [128, C, 512], BF16, OFF_B + i * 16 * KB)
            for i in range(2)]
    woB = place("woB", [128, 2, C, 512], BF16, OFF_B)
    inter = place("inter", [128, C, 256], BF16, OFF_B)
    x2r = place("x2r", [128, 2, D], F32, OFF_B + 8 * KB)
    sgate = place("sgate", [128, 2, 256], F32, OFF_B + 24 * KB)
    kf = place("kf", [128, NH, T], BF16, OFF_C)
    xin = place("xin", [128, 2, D], F32, OFF_C)
    aout = place("aout", [128, 2, D], F32, OFF_C + 8 * KB)
    xn2s = place("xn2s", [128, 2, D], BF16, OFF_C + 24 * KB)
    v_tm = place("v_tm", [128, 8, D], BF16, OFF_D)
    gz = place("gz", [128, C, T], BF16, OFF_D)
    dw = [place(f"dw{i}", [128, C, 512], BF16, OFF_D + i * 16 * KB) for i in range(2)]
    gp = place("gp", [128, C, T], BF16, OFF_E)
    woE3 = place("woE3", [128, C, 512], BF16, OFF_E + 16 * KB)
    obuf = place("obuf", [128, 2, D], F32, OFF_E)
    rrow = place("rrow", [1, T], F32, OFF_E)
    kvrun = place("kvrun", [128, 2, HD], F32, OFF_G)
    kvst = place("kvst", [128, 2, 3, HD], F32, OFF_G + 1 * KB)
    ktil = place("ktil", [128, 2, HD], BF16, OFF_G + 4 * KB)
    kdec = place("kdec", [128, 2], F32, OFF_G + 4 * KB + 512)
    arrk = place("arrk", [128, 2], F32, OFF_G + 4 * KB + 576)
    kvld = place("kvld", [128, 2, 3, HD], F32, OFF_G + 5 * KB)
    diagT = place("diagT", [128, 2, 2, BLK], BF16, OFF_G + 8 * KB)
    qdecb = place("qdecb", [128, 2, BLK], BF16, OFF_G + 10 * KB)
    qtil = place("qtil", [128, 2, BLK], BF16, OFF_G + 11 * KB)
    qks = place("qks", [128, 2, 2, BLK], BF16, OFF_G + 12 * KB)
    kvt = place("kvt", [128, 2, HD], BF16, OFF_G + 14 * KB)
    lamt = place("lamt", [128, 2, NB], F32, OFF_G + 14 * KB + 512)
    sqG = place("sqG", [128, 4, T], BF16, OFF_G)
    wsl = [place(f"wsl{i}", [128, C, 256], BF16, OFF_W + i * 8 * KB) for i in range(2)]
    woW = place("woW", [128, C, 512], BF16, OFF_W)
    rw8 = place("rw8", [128, C, 512], BF16, OFF_W)
    ident = place("ident", [128, 128], BF16, OFF_S)
    ones_c = place("ones_c", [128, 1], BF16, OFF_S + 256)
    ones_r = place("ones_r", [1, 128], BF16, OFF_S + 288)
    arrq = place("arrq", [1, BLK], F32, OFF_S + 544)
    qdrow = place("qdrow", [1, 2, BLK], BF16, OFF_S + 1568)
    normw = place("normw", [128, C], F32, OFF_S + 2592)
    slopes = place("slopes", [128, NH], F32, OFF_S + 2656)
    negs = place("negs", [128, NH], F32, OFF_S + 2720)
    flag = place("flag", [128, 1], F32, OFF_S + 2784)
    rstdc = place("rstdc", [128, 8], F32, OFF_S + 2816)
    stat = place("stat", [128, 4, 8], F32, OFF_S + 2848)
    bnst = place("bnst", [128, 4, 24], F32, OFF_S + 2976)
    identF = place("identF", [128, 128], F32, OFF_S + 3360)
    idxt = place("idxt", [128, 2, BLK], BF16, OFF_S + 3872)

    fb = [nc.alloc_psum_tensor(f"fb{i}", [128, 512], F32).ap() for i in range(6)]
    tbk = [nc.alloc_psum_tensor(f"tb{i}", [128, 1024], BF16).ap() for i in range(2)]

    P = Prog()

    def dma(q, out, in_, deps=(), sem=None):
        return P.emit(q, lambda e, o=out, i=in_: e.dma_start(out=o, in_=i),
                      deps, sem or ("S" if q == "sync" else "G"), 16)

    def mm(out, lhsT, rhs, start, stop, deps=(), inc=False):
        return P.emit("tensor",
                      lambda e, o=out, l=lhsT, r=rhs, st=start, sp=stop:
                      e.matmul(o, l, r, start=st, stop=sp),
                      deps, "T" if inc else None, 1)

    def tr(out, in_, deps=(), inc=False, idt=None):
        idm = ident if idt is None else idt
        return P.emit("tensor", lambda e, o=out, i=in_, m=idm: e.transpose(o, i, m),
                      deps, "T" if inc else None, 1)

    def act(fn, deps=(), inc=False):
        return P.emit("scalar", fn, deps, "A" if inc else None, 1)

    def vec(fn, deps=(), inc=False):
        return P.emit("vector", fn, deps, "V" if inc else None, 1)

    bank_free = [None] * 6
    grp_i = [0]

    def next_bank():
        i = grp_i[0] % 6
        grp_i[0] += 1
        return i, bank_free[i]

    def dbg_dump(name, src_ap, deps):
        if not _DEBUG:
            return None
        th = nc.dram_tensor("d_" + name, list(src_ap.shape), src_ap.dtype,
                            kind="ExternalOutput")
        dbg_dram[name] = th
        dbg_t[name] = dma("sync", th.ap(), src_ap, deps)
        return dbg_t[name]

    def ln_bn(src, b, deps, per_chunk=False, eps=EPS_LN):
        toks = []
        for s4 in range(4):
            d = deps[s4] if per_chunk else (deps if s4 == 0 else ())
            toks.append(vec(lambda e, s4=s4, b=b, src=src:
                            e.bn_stats(bnst[:, b, s4 * 6:(s4 + 1) * 6],
                                       src[:, s4 * 512:(s4 + 1) * 512]),
                            d, inc=True))
        t_ag = vec(lambda e, b=b: e.bn_aggr(stat[:, b, 0:2], bnst[:, b, :]),
                   toks, inc=True)
        if eps is None:
            return t_ag, None
        t5 = vec(lambda e, b=b, eps=eps:
                 e.tensor_scalar(stat[:, b, 2:3], stat[:, b, 1:2],
                                 eps, None, ALU.add), [t_ag], inc=True)
        t6 = vec(lambda e, b=b: e.reciprocal(stat[:, b, 2:3], stat[:, b, 2:3]),
                 [t5], inc=True)
        t7 = act(lambda e, b=b: e.activation(stat[:, b, 2:3], stat[:, b, 2:3],
                                             AF.Sqrt), [t6], inc=True)
        return t_ag, t7

    dma("gpsimd", ident[:, :], ident_d[:, :])
    dma("gpsimd", identF[:, :], ident_d[:, :])
    dma("gpsimd", ones_c[:, :], ones_c_d[:, :])
    dma("gpsimd", ones_r[:, :], ones_r_d[:, :])
    dma("gpsimd", idxt[:, :, :], idxt_d[:, :, :])
    dma("gpsimd", arrq[:, :], arrq_d[:, :])
    dma("gpsimd", arrk[:, :], arrk_d[:, :])
    dma("gpsimd", normw[:, :], normw_in[:, :])
    dma("gpsimd", slopes[:, :], slope_b[:, :])
    t_cst = dma("gpsimd", flag[:, :], flag_in[:, :])
    t_negs = vec(lambda e: e.tensor_scalar(negs[:, :], slopes[:, :], -1.0, None,
                                           ALU.mult), [t_cst], inc=True)

    wq_r = wqkvT.rearrange("(s p) w -> p s w", p=128)
    wv_r = wv.rearrange("(s p) w -> p s w", p=128)
    wg_r = wgT.rearrange("(s p) w -> p s w", p=128)
    wo_r = woT.rearrange("(s p) w -> p s w", p=128)
    gup_r = gup_w.rearrange("(ee s p) w -> p ee s w", p=128, ee=E)
    dwn_r = down_w.rearrange("(ee s p) w -> p ee s w", p=128, ee=E)
    slab_done = {}
    slab_loaded = {}
    slab_serial = [0]

    def load_slab(src_ap, extra=(), q="gpsimd"):
        s = slab_serial[0]
        d = list(extra)
        if s >= 2:
            d.append(slab_done.get(s - 2))
        slab_loaded[s] = dma(q, wsl[s % 2].rearrange("p c w -> p (c w)"),
                             src_ap, d)
        slab_serial[0] = s + 1
        return s, wsl[s % 2]

    class Ring:
        def __init__(self, bufs, init_tok=None, sembase="R"):
            self.bufs = bufs
            self.n = len(bufs)
            self.s = 0
            self.done = {}
            self.loaded = {}
            self.init_tok = init_tok
            self.sembase = sembase

        def load(self, src_ap, extra=()):
            i = self.s
            d = list(extra) + [self.done.get(i - self.n, self.init_tok)]
            buf = self.bufs[i % self.n]
            tok = dma("gpsimd", buf.rearrange("p c w -> p (c w)"), src_ap, d,
                      sem=f"{self.sembase}{i % self.n}")
            self.loaded[i] = tok
            self.s = i + 1
            return i, buf, tok

    x_r = x_in.rearrange("(tt p) dd -> p tt dd", p=128)
    p0_evac = []
    t_half = [None, None]
    for idx, tt in enumerate([0, 2, 4, 6, 1, 3, 5, 7]):
        b = idx % 2
        d = [p0_evac[2 * (idx - 2) + 1]] if idx >= 2 else []
        t_x = dma("sync", xin[:, b, :], x_r[:, tt, :], d)
        _, t_rstd = ln_bn(xin[:, b, :], b, [t_x])
        t_app = vec(lambda e, b=b: e.tensor_scalar(xn2s[:, b, :], xin[:, b, :],
                                                   stat[:, b, 0:1], stat[:, b, 2:3],
                                                   ALU.subtract, ALU.mult),
                    [t_rstd], inc=True)
        for half in range(2):
            bank = tbk[half]
            tdeps = [t_app, p0_evac[-1] if p0_evac else None]
            tok_t = None
            for c8 in range(8):
                c = half * 8 + c8
                tok_t = tr(bank[:, c8 * 128:(c8 + 1) * 128],
                           xn2s[:, b, c * 128:(c + 1) * 128],
                           tdeps if c8 == 0 else (), inc=(c8 == 7))
            off = 512 * (tt % 2) + tt // 2
            t_ev = act(lambda e, bank=bank, off=off, half=half:
                       e.activation(xn1T[:, half * 8:(half + 1) * 8,
                                         off:off + 509:4],
                                    bank[:, :], AF.Copy), [tok_t], inc=True)
            p0_evac.append(t_ev)
        if idx == 3:
            t_half[0] = p0_evac[-1]
    t_half[1] = p0_evac[-1]
    t_xn1T = p0_evac[-1]
    dbg_dump("xn1T", xn1T[:, :, :], [t_xn1T])

    silu_evac = {}

    def proj_fm(buf, dst, kind, heads, tok_load):
        tok_mm = None
        for th in range(2):
            for obi, h in enumerate(heads):
                bi, bfree = next_bank()
                for kc in range(C):
                    deps = [tok_load, bfree, t_half[th]] if kc == 0 else ()
                    tok_mm = mm(fb[bi][:, :], buf[:, kc, obi * 128:(obi + 1) * 128],
                                xn1T[:, kc, th * 512:(th + 1) * 512],
                                start=(kc == 0), stop=(kc == C - 1),
                                deps=deps, inc=(kc == C - 1))
                t_ev = act(lambda e, bi=bi, dst=dst, h=h, th=th:
                           e.activation(dst[:, h, th * 512:(th + 1) * 512],
                                        fb[bi][:, :], AF.Silu), [tok_mm], inc=True)
                bank_free[bi] = t_ev
                silu_evac[(kind, h, th)] = t_ev
        return tok_mm

    for s in range(8):
        ss, buf = load_slab(wq_r[:, s, :])
        slab_done[ss] = proj_fm(buf, kf, "k", [2 * s, 2 * s + 1], slab_loaded[ss])
    dbg_dump("k", kf[:, :, :], [silu_evac[("k", NH - 1, 1)]])
    tb_guard = t_xn1T
    vring = Ring(vbuf, sembase="VB")
    for s in range(4):
        vi, buf, tok_ld = vring.load(wv_r[:, s, :])
        tok_mm = None
        for tt in range(8):
            bi, bfree = next_bank()
            for kc in range(C):
                deps = [tok_ld, bfree, t_half[tt // 4]] if kc == 0 else ()
                tok_mm = mm(fb[bi][:, :], xn1T[:, kc, tt * 128:(tt + 1) * 128],
                            buf[:, kc, :], start=(kc == 0), stop=(kc == C - 1),
                            deps=deps, inc=(kc == C - 1))
            t_ev = act(lambda e, bi=bi, tt=tt, s=s:
                       e.activation(v_tm[:, tt, 512 * s:512 * (s + 1)],
                                    fb[bi][:, :], AF.Silu), [tok_mm], inc=True)
            bank_free[bi] = t_ev
            silu_evac[("v", tt, s)] = t_ev
        vring.done[vi] = tok_mm
    dbg_dump("v", v_tm[:, :, :], [silu_evac[("v", 7, 3)]])

    cc_dma = {}
    kvst_dma = {}
    pass1_prev = None
    for h in range(NH):
        hb = h % 2
        t_lam = act(lambda e, h=h: e.activation(lamt[:, 0, 0:1], negs[:, h:h + 1],
                                                AF.Exp, scale=256.0),
                    [t_negs, pass1_prev], inc=True)
        t_kd = act(lambda e, h=h: e.activation(kdec[:, :], arrk[:, :], AF.Exp,
                                               scale=negs[:, h:h + 1]),
                   [t_negs, t_cst], inc=True)
        tok_kvup = None
        tok_st = None
        for t in range(NB):
            ktoks = []
            for j in range(2):
                bank = tbk[j]
                deps = [silu_evac[("k", h, (t * 256 + j * 128) // 512)], tb_guard]
                tok_t = tr(bank[:, 0:128],
                           kf[:, h, t * 256 + j * 128:t * 256 + (j + 1) * 128],
                           deps, inc=True)
                t_ke = vec(lambda e, j=j, bank=bank:
                           e.tensor_scalar(ktil[:, j, :], bank[:, 0:128],
                                           kdec[:, j:j + 1], None, ALU.mult),
                           [tok_t, t_kd], inc=True)
                tb_guard = t_ke
                ktoks.append(t_ke)
            if t >= 1:
                tok_st = vec(lambda e, t=t, hb=hb:
                             e.tensor_copy(kvst[:, hb, t - 1, :], kvrun[:, hb, :]),
                             [tok_kvup, kvst_dma.get(h - 2)], inc=True)
            bi, bfree = next_bank()
            tok_c = None
            for j in range(2):
                deps = [ktoks[j], silu_evac[("v", 2 * t + j, h // 4)]]
                if j == 0:
                    deps.append(bfree)
                tok_c = mm(fb[bi][:, 0:128], ktil[:, j, :],
                           v_tm[:, 2 * t + j, h * 128:(h + 1) * 128],
                           start=(j == 0), stop=(j == 1), deps=deps, inc=(j == 1))
            if t == 0:
                tok_kvup = vec(lambda e, hb=hb, bi=bi:
                               e.tensor_copy(kvrun[:, hb, :], fb[bi][:, 0:128]),
                               [tok_c, cc_dma.get(h - 2)], inc=True)
            else:
                tok_kvup = vec(lambda e, hb=hb, bi=bi:
                               e.scalar_tensor_tensor(kvrun[:, hb, :], kvrun[:, hb, :],
                                                      lamt[:, 0, 0:1], fb[bi][:, 0:128],
                                                      ALU.mult, ALU.add),
                               [tok_c, t_lam, tok_kvup], inc=True)
            bank_free[bi] = tok_kvup
        kvst_dma[h] = dma("sync", kvl_d[h * 128:(h + 1) * 128, :],
                          kvst[:, hb, :, :].rearrange("p a w -> p (a w)"), [tok_st])
        cc_dma[h] = dma("sync", cc_in.ap()[:, h * 128:(h + 1) * 128],
                        kvrun[:, hb, :], [tok_kvup])
        pass1_prev = tok_kvup

    for s in range(8):
        ss, buf = load_slab(wq_r[:, 8 + s, :])
        slab_done[ss] = proj_fm(buf, qf, "q", [2 * s, 2 * s + 1], slab_loaded[ss])
    dbg_dump("q", qf[:, :, :], [silu_evac[("q", NH - 1, 1)]])

    t_cc = P.emit("gpsimd",
                  lambda e: e.collective_compute(
                      "AllGather", ALU.bypass,
                      replica_groups=[[0, 1], [2, 3], [4, 5], [6, 7]],
                      ins=[cc_in.ap()[:, :]], outs=[cc_out.ap()[:, :]]),
                  [cc_dma[h] for h in range(NH)], "CC", 1)

    for s in range(8):
        ss, buf = load_slab(wg_r[:, s, :], q="sync")
        tok_mm = None
        for obi in range(2):
            c = 2 * s + obi
            for th in range(2):
                bi, bfree = next_bank()
                for kc in range(C):
                    deps = [slab_loaded[ss], bfree, t_half[th]] if kc == 0 else ()
                    tok_mm = mm(fb[bi][:, :], buf[:, kc, obi * 128:(obi + 1) * 128],
                                xn1T[:, kc, th * 512:(th + 1) * 512],
                                start=(kc == 0), stop=(kc == C - 1),
                                deps=deps, inc=(kc == C - 1))
                t_sg = act(lambda e, bi=bi, c=c, th=th:
                           e.activation(gp[:, c, th * 512:(th + 1) * 512],
                                        fb[bi][:, :], AF.Sigmoid), [tok_mm], inc=True)
                t_nw = vec(lambda e, c=c, th=th:
                           e.tensor_scalar(gp[:, c, th * 512:(th + 1) * 512],
                                           gp[:, c, th * 512:(th + 1) * 512],
                                           normw[:, c:c + 1], None, ALU.mult),
                           [t_sg, t_cst], inc=True)
                bank_free[bi] = t_nw
        slab_done[ss] = tok_mm
        t_gp_last = t_nw
    t_gate_mm_last = slab_done[23]
    dbg_dump("gp", gp[:, :, :], [t_gp_last])

    t_kvld = dma("sync", kvinit[:, :], cc_out.ap()[0:128, :],
                 [t_cc, t_gate_mm_last])
    t_kvfl = vec(lambda e: e.tensor_scalar(kvinit[:, :], kvinit[:, :], flag[:, 0:1],
                                           None, ALU.mult), [t_kvld, t_cst], inc=True)
    dbg_dump("kvinit", kvinit[:, :], [t_kvfl])
    t_qkz = vec(lambda e: e.memset(qks[:, 0, 1, 0:128], 0.0), [], inc=True)
    t_qkz2 = vec(lambda e: e.memset(qks[:, 1, 1, 0:128], 0.0), [], inc=True)

    iters = []
    for hp in range(8):
        for t in range(NB):
            for sub in range(2):
                iters.append((2 * hp + sub, t))

    o_evac = {}
    head_drain = {}
    pass2_last = None
    kvt_free = [None, None]
    qtil_free = [None, None]
    qks_free = [t_qkz, t_qkz2]
    kvld_dma = {}
    kvld_free = [None, None]
    hstate = [None, None]
    pend_o = None
    for i in range(len(iters) + 1):
        if pend_o is not None:
            (ph, pt, ptb2, pqke0, pqke1, pkvt, pqt, pbio, pbfree_o) = pend_o
            mm(fb[pbio][:, 0:256], v_tm[:, 2 * pt, ph * 128:(ph + 1) * 128],
               qks[:, ptb2, 0, :], start=True, stop=False,
               deps=[pqke0, pbfree_o, silu_evac[("v", 2 * pt, ph // 4)]])
            mm(fb[pbio][:, 128:256], v_tm[:, 2 * pt + 1, ph * 128:(ph + 1) * 128],
               qks[:, ptb2, 1, 128:256], start=False, stop=False,
               deps=[pqke1, silu_evac[("v", 2 * pt + 1, ph // 4)]])
            tok_o = mm(fb[pbio][:, 0:256], kvt[:, ptb2, :], qtil[:, ptb2, :],
                       start=False, stop=True, deps=[pkvt, pqt], inc=True)
            kvt_free[ptb2] = tok_o
            qtil_free[ptb2] = tok_o
            qks_free[ptb2] = tok_o
            t_oe = act(lambda e, pbio=pbio, ph=ph, pt=pt:
                       e.activation(kf[:, ph, pt * 256:(pt + 1) * 256],
                                    fb[pbio][:, 0:256], AF.Copy), [tok_o], inc=True)
            o_evac[(ph, pt)] = t_oe
            bank_free[pbio] = t_oe
            pass2_last = t_oe
            if pt == 3:
                kvld_free[ph % 2] = t_oe
                head_drain[ph] = t_oe
        if i == len(iters):
            break
        h, t = iters[i]
        par = h % 2
        if t == 0:
            kvld_dma[h] = dma("scalar",
                              kvld[:, par, :, :].rearrange("p a w -> p (a w)"),
                              kvl_d[h * 128:(h + 1) * 128, :],
                              [kvst_dma[h], kvld_free[par]], sem=f"K{par}")
            deps_c = [t_negs, kvld_free[par]]
            t_dg = act(lambda e, h=h, par=par:
                       e.activation(diagT[:, par, :, :], idxt[:, :, :],
                                    AF.Exp, scale=negs[:, h:h + 1]),
                       deps_c, inc=True)
            t_qr = act(lambda e, h=h, par=par:
                       e.activation(qdrow[0:1, par, :], arrq[:, :], AF.Exp,
                                    scale=negs[0:1, h:h + 1]),
                       deps_c, inc=True)
            bi_q, bfree_q = next_bank()
            t_qb = mm(fb[bi_q][:, 0:256], ones_r[:, :], qdrow[0:1, par, :],
                      True, True, [t_qr, bfree_q], inc=True)
            t_qbe = act(lambda e, bi_q=bi_q, par=par:
                        e.activation(qdecb[:, par, :], fb[bi_q][:, 0:256], AF.Copy),
                        [t_qb], inc=True)
            bank_free[bi_q] = t_qbe
            lam_toks = [act(lambda e, h=h, par=par, t2=t2:
                            e.activation(lamt[:, par, t2:t2 + 1], negs[:, h:h + 1],
                                         AF.Exp, scale=256.0 * t2),
                            deps_c, inc=True)
                        for t2 in range(1, NB)]
            hstate[par] = {"qbe": t_qbe, "dg": t_dg, "lam": lam_toks}
        tb2 = i % 2
        if t == 0:
            t_kvt = P.emit("gpsimd",
                           lambda e, h=h, tb2=tb2:
                           e.tensor_copy(kvt[:, tb2, :],
                                         kvinit[:, h * 128:(h + 1) * 128]),
                           [t_kvfl, kvt_free[tb2]], "PL", 1)
        else:
            t_kvt = vec(lambda e, h=h, t=t, tb2=tb2, par=par:
                        e.scalar_tensor_tensor(kvt[:, tb2, :],
                                               kvinit[:, h * 128:(h + 1) * 128],
                                               lamt[:, par, t:t + 1],
                                               kvld[:, par, t - 1, :],
                                               ALU.mult, ALU.add),
                        [t_kvfl, hstate[par]["lam"][t - 1], kvld_dma[h],
                         kvt_free[tb2]], inc=True)
        th = t // 2
        t_qt = P.emit("gpsimd",
                      lambda e, h=h, t=t, tb2=tb2, par=par:
                      e.tensor_tensor(qtil[:, tb2, :],
                                      qf[:, h, t * 256:(t + 1) * 256],
                                      qdecb[:, par, :], ALU.mult),
                      [hstate[par]["qbe"], silu_evac[("q", h, th)], qtil_free[tb2]],
                      "PL", 1)
        bi_qk, bfree_qk = next_bank()
        qk_mm = []
        for j in range(2):
            deps = [silu_evac[("k", h, th)], silu_evac[("q", h, th)]]
            if j == 0:
                deps.append(bfree_qk)
            qk_mm.append(mm(fb[bi_qk][:, j * 256:(j + 1) * 256],
                            kf[:, h, t * 256 + j * 128:t * 256 + (j + 1) * 128],
                            qf[:, h, t * 256:(t + 1) * 256],
                            start=True, stop=True, deps=deps, inc=True))
        t_qke0 = vec(lambda e, bi_qk=bi_qk, tb2=tb2, par=par:
                     e.tensor_tensor(qks[:, tb2, 0, :], fb[bi_qk][:, 0:256],
                                     diagT[:, par, 0, :], ALU.mult),
                     [qk_mm[0], hstate[par]["dg"], qks_free[tb2]], inc=True)
        t_qke1 = vec(lambda e, bi_qk=bi_qk, tb2=tb2, par=par:
                     e.tensor_tensor(qks[:, tb2, 1, 128:256],
                                     fb[bi_qk][:, 384:512],
                                     diagT[:, par, 1, 128:256], ALU.mult),
                     [qk_mm[1], hstate[par]["dg"]], inc=True)
        bank_free[bi_qk] = t_qke1
        bi_o, bfree_o = next_bank()
        pend_o = (h, t, tb2, t_qke0, t_qke1, t_kvt, t_qt, bi_o, bfree_o)

    dbg_dump("o", kf[:, :, :], [pass2_last])

    wo_tok = {}
    wo_tok[2] = dma("sync", woW[:, :, :].rearrange("p c w -> p (c w)"),
                    wo_r[:, 2, :], [t_gate_mm_last])
    wo_tok[0] = dma("sync", woB[:, 0, :, :].rearrange("p c w -> p (c w)"),
                    wo_r[:, 0, :], [head_drain[7], dbg_t.get("q")])
    wo_tok[1] = dma("sync", woB[:, 1, :, :].rearrange("p c w -> p (c w)"),
                    wo_r[:, 1, :], [head_drain[15]])

    def wo_res(s):
        return woB[:, s, :, :] if s < 2 else (woW[:, :, :] if s == 2
                                              else woE3[:, :, :])

    gz_tok = {}
    first_gz = True
    for e4 in range(4):
        for c in range(C):
            deps = [pass2_last] if first_gz else []
            first_gz = False
            gz_tok[(c, e4)] = vec(
                lambda e, c=c, e4=e4: e.tensor_tensor(
                    gz[:, c, e4 * 256:(e4 + 1) * 256],
                    gp[:, c, e4:T:4], kf[:, c, e4:T:4], ALU.mult),
                deps, inc=True)
    t_gz_all = gz_tok[(C - 1, 3)]
    wo_tok[3] = dma("sync", woE3[:, :, :].rearrange("p c w -> p (c w)"),
                    wo_r[:, 3, :], [t_gz_all])
    wo_dep = [wo_tok[0], wo_tok[1], wo_tok[2], wo_tok[3]]

    p6_mm = {}
    p6_bfree = [bank_free[s] for s in range(4)]

    def p6_chains(gt):
        for s in range(4):
            bi = s
            for kc in range(C):
                deps = [wo_dep[s], p6_bfree[s]] if kc == 0 else []
                deps = list(deps) + [gz_tok[(kc, gt // 2)]]
                tok_mm = mm(fb[bi][:, :], gz[:, kc, gt * 128:(gt + 1) * 128],
                            wo_res(s)[:, kc, :], start=(kc == 0), stop=(kc == C - 1),
                            deps=deps, inc=(kc == C - 1))
            p6_mm[(gt, s)] = tok_mm

    ev_tok = {}
    aout_free = [None, None]

    def p6_evacs(gt, guard):
        gi = gt % 2
        for s in range(4):
            t_ev = act(lambda e, s=s, gi=gi, gt=gt:
                       e.activation(aout[:, gi, s * 512:(s + 1) * 512],
                                    fb[s][:, :], AF.Copy),
                       [p6_mm[(gt, s)], guard if s == 0 else None], inc=True)
            ev_tok[(gt, s)] = t_ev
            p6_bfree[s] = t_ev

    p6_chains(0)

    t_sq = {}
    p5mm = {}
    for c in range(C):
        sl = c % 4
        deps = [p5mm.get(c - 4)]
        if c == 0:
            deps.append(pass2_last)
        t_sq[c] = vec(lambda e, c=c, sl=sl:
                      e.tensor_tensor(sqG[:, sl, :], kf[:, c, :], kf[:, c, :],
                                      ALU.mult), deps, inc=True)
        cur = None
        for thh in range(2):
            bi = 4 + thh
            rhs = sqG[:, sl, :].rearrange("p (tq t4) -> p t4 tq",
                                          t4=4)[:, 2 * thh:2 * thh + 2, :]
            deps2 = [t_sq[c]]
            if c == 0:
                deps2.append(bank_free[bi])
            cur = mm(fb[bi][0:1, :], ones_c[:, :], rhs,
                     start=(c == 0), stop=(c == C - 1), deps=deps2, inc=True)
        p5mm[c] = cur
    r1 = vec(lambda e: e.tensor_scalar(rrow[0:1, 0:512], fb[4][0:1, :],
                                       1.0 / D, EPS_RMS, ALU.mult, ALU.add),
             [p5mm[C - 1]], inc=True)
    r2 = vec(lambda e: e.tensor_scalar(rrow[0:1, 512:1024], fb[5][0:1, :],
                                       1.0 / D, EPS_RMS, ALU.mult, ALU.add),
             [p5mm[C - 1]], inc=True)
    r3 = vec(lambda e: e.reciprocal(rrow[0:1, 0:512], rrow[0:1, 0:512]),
             [r1], inc=True)
    t_rm = vec(lambda e: e.reciprocal(rrow[0:1, 512:1024], rrow[0:1, 512:1024]),
               [r2, r3], inc=True)
    t_rs = act(lambda e: e.activation(rrow[0:1, :], rrow[0:1, :], AF.Sqrt), [t_rm],
               inc=True)
    bank_free[4] = r1
    bank_free[5] = r2
    tok_tr = None
    for gt8 in range(8):
        d = [t_rs, r1, r3] if gt8 == 0 else ()
        tok_tr = tr(fb[4][:, gt8:gt8 + 1], rrow[0:1, gt8 * 128:(gt8 + 1) * 128],
                    d, inc=(gt8 == 7), idt=identF[0:1, 0:1])
    t_rc = act(lambda e: e.activation(rstdc[:, :], fb[4][:, 0:8], AF.Copy),
               [tok_tr], inc=True)
    bank_free[4] = t_rc
    dbg_dump("rrow", rrow[0:1, :], [t_rs])

    xin_free = [None]
    p6_xn2_evac = []
    p6_spill = {}
    xn2_done = {}
    pend_tr = []

    def p6_epilogue(gt):
        gi = gt % 2
        b = gt % 2
        ag1, _ = ln_bn(aout[:, gi, :], b,
                       [[ev_tok[(gt, s)]] for s in range(4)],
                       per_chunk=True, eps=None)
        u1 = vec(lambda e, b=b, gt=gt: e.tensor_tensor(stat[:, b, 3:4],
                                                       rstdc[:, gt:gt + 1],
                                                       rstdc[:, gt:gt + 1],
                                                       ALU.mult), [t_rc, ag1],
                 inc=True)
        u2 = vec(lambda e, b=b: e.tensor_tensor(stat[:, b, 3:4], stat[:, b, 1:2],
                                                stat[:, b, 3:4], ALU.mult),
                 [u1], inc=True)
        u3 = vec(lambda e, b=b: e.tensor_scalar(stat[:, b, 3:4], stat[:, b, 3:4],
                                                EPS_LN, None, ALU.add), [u2],
                 inc=True)
        u4 = vec(lambda e, b=b: e.reciprocal(stat[:, b, 3:4], stat[:, b, 3:4]),
                 [u3], inc=True)
        u5 = act(lambda e, b=b: e.activation(stat[:, b, 3:4], stat[:, b, 3:4],
                                             AF.Sqrt), [u4], inc=True)
        u6 = vec(lambda e, b=b, gt=gt: e.tensor_tensor(stat[:, b, 4:5],
                                                       stat[:, b, 3:4],
                                                       rstdc[:, gt:gt + 1],
                                                       ALU.mult), [u5], inc=True)
        t_x = dma("sync", xin[:, 0, :], x_r[:, gt, :],
                  [xin_free[0] if gt > 0 else t_sq[3]])
        t_a = vec(lambda e, b=b, gi=gi:
                  e.tensor_scalar(aout[:, gi, :], aout[:, gi, :], stat[:, b, 0:1],
                                  stat[:, b, 4:5], ALU.subtract, ALU.mult),
                  [u6], inc=True)
        t_a2 = vec(lambda e, gi=gi:
                   e.tensor_tensor(aout[:, gi, :], aout[:, gi, :], xin[:, 0, :],
                                   ALU.add), [t_a, t_x], inc=True)
        xin_free[0] = t_a2
        _, t_rstd2 = ln_bn(aout[:, gi, :], b, [t_a2])
        t_app2 = vec(lambda e, b=b, gi=gi:
                     e.tensor_scalar(xn2s[:, b, :], aout[:, gi, :],
                                     stat[:, b, 0:1], stat[:, b, 2:3],
                                     ALU.subtract, ALU.mult),
                     [t_rstd2,
                      p6_xn2_evac[-1] if p6_xn2_evac else t_sq[15]],
                     inc=True)
        t_sp = dma("sync",
                   x2sp.rearrange("(tt p) dd -> p tt dd", p=128)[:, gt, :],
                   aout[:, gi, :], [t_app2])
        p6_spill[gt] = t_sp
        aout_free[gi] = t_sp
        pend_tr.append((gt, t_app2))

    def p6_transposes(gt, t_app2):
        b = gt % 2
        evs = []
        for half in range(2):
            bank = tbk[half]
            tdeps = [t_app2, p6_xn2_evac[-1] if p6_xn2_evac else None]
            tok_t = None
            for c8 in range(8):
                c = half * 8 + c8
                tok_t = tr(bank[:, c8 * 128:(c8 + 1) * 128],
                           xn2s[:, b, c * 128:(c + 1) * 128],
                           tdeps if c8 == 0 else (), inc=(c8 == 7))
            t_ev = act(lambda e, bank=bank, gt=gt, half=half:
                       e.activation(xn2T[:, half * 8:(half + 1) * 8,
                                         gt * 128:(gt + 1) * 128],
                                    bank[:, :], AF.Copy), [tok_t], inc=True)
            p6_xn2_evac.append(t_ev)
            evs.append(t_ev)
        xn2_done[gt] = evs

    p6_evacs(0, t_sq[7])
    p6_epilogue(0)
    for gt in range(1, 8):
        if len(pend_tr) >= 2:
            g2, ta2 = pend_tr.pop(0)
            p6_transposes(g2, ta2)
        p6_chains(gt)
        guard = t_sq[11] if gt == 1 else aout_free[gt % 2]
        p6_evacs(gt, guard)
        p6_epilogue(gt)
    while pend_tr:
        g2, ta2 = pend_tr.pop(0)
        p6_transposes(g2, ta2)
    t_xn2T = p6_xn2_evac[-1]
    p6_mm_last = p6_mm[(7, 3)]
    for s in range(4):
        bank_free[s] = ev_tok[(7, s)]
    dbg_dump("xn2T", xn2T[:, :, :], [t_xn2T])

    ring8 = Ring(dw + [rw8], init_tok=p6_mm_last)
    x2r_free = [None, None]
    obuf_free = [None, None]
    sg_free = [None, None]
    out_toks = []
    down_last_prev = None
    for e_i in range(E):
        t_int = None
        xdep = [xn2_done[2 * e_i][1], xn2_done[2 * e_i + 1][1]]
        for s in range(8):
            ss, slab, tok_ld = ring8.load(gup_r[:, e_i, s, :])
            tokU = None
            for pr in range(2):
                i_blk = 2 * s + pr
                biG, bfreeG = next_bank()
                biU, bfreeU = next_bank()
                tokG = None
                for kc in range(C):
                    dG = ([tok_ld, bfreeG] + xdep) if kc == 0 else ()
                    tokG = mm(fb[biG][:, 0:256], slab[:, kc, 256 * pr:256 * pr + 128],
                              xn2T[:, kc, e_i * 256:(e_i + 1) * 256],
                              start=(kc == 0), stop=(kc == C - 1), deps=dG,
                              inc=(kc == C - 1))
                for kc in range(C):
                    dU = [bfreeU] if kc == 0 else ()
                    tokU = mm(fb[biU][:, 0:256],
                              slab[:, kc, 256 * pr + 128:256 * (pr + 1)],
                              xn2T[:, kc, e_i * 256:(e_i + 1) * 256],
                              start=(kc == 0), stop=(kc == C - 1), deps=dU,
                              inc=(kc == C - 1))
                sb = i_blk % 2
                t_sil = act(lambda e, biG=biG, sb=sb:
                            e.activation(sgate[:, sb, :], fb[biG][:, 0:256], AF.Silu),
                            [tokG, sg_free[sb]], inc=True)
                t_int = vec(lambda e, biU=biU, sb=sb, i_blk=i_blk:
                            e.tensor_tensor(inter[:, i_blk, :], fb[biU][:, 0:256],
                                            sgate[:, sb, :], ALU.mult),
                            [tokU, t_sil, down_last_prev],
                            inc=True)
                bank_free[biG] = t_int
                bank_free[biU] = t_int
                sg_free[sb] = t_int
            ring8.done[ss] = tokU
        inter_last = t_int
        ev_toks = [[], []]
        for s in range(4):
            ss, slab, tok_ld = ring8.load(dwn_r[:, e_i, s, :])
            tok_mm = None
            for gt in range(2):
                bi, bfree = next_bank()
                for kc in range(C):
                    deps = ([tok_ld, bfree, inter_last,
                             obuf_free[gt]] if kc == 0 else ())
                    tok_mm = mm(fb[bi][:, :],
                                inter[:, kc, gt * 128:(gt + 1) * 128],
                                slab[:, kc, :], start=(kc == 0), stop=(kc == C - 1),
                                deps=deps, inc=(kc == C - 1))
                t_ev = act(lambda e, bi=bi, gt=gt, s=s:
                           e.activation(obuf[:, gt, s * 512:(s + 1) * 512],
                                        fb[bi][:, :], AF.Copy), [tok_mm], inc=True)
                bank_free[bi] = t_ev
                ev_toks[gt].append(t_ev)
            ring8.done[ss] = tok_mm
            down_last_prev = tok_mm
        for gt in range(2):
            g_glob = e_i * 2 + gt
            b = gt
            t_x2 = dma("sync", x2r[:, b, :],
                       x2sp.rearrange("(tt p) dd -> p tt dd", p=128)[:, g_glob, :],
                       [p6_spill[g_glob], x2r_free[b], pass2_last, p6_mm_last])
            _, t_rstd4 = ln_bn(obuf[:, gt, :], b,
                               [[ev_toks[gt][s]] for s in range(4)],
                               per_chunk=True)
            t_a = vec(lambda e, b=b, gt=gt:
                      e.tensor_scalar(obuf[:, gt, :], obuf[:, gt, :],
                                      stat[:, b, 0:1], stat[:, b, 2:3],
                                      ALU.subtract, ALU.mult),
                      [t_rstd4], inc=True)
            t_a2 = vec(lambda e, b=b, gt=gt:
                       e.tensor_tensor(obuf[:, gt, :], obuf[:, gt, :], x2r[:, b, :],
                                       ALU.add), [t_a, t_x2], inc=True)
            t_out = dma("sync",
                        out_d.rearrange("(tt p) dd -> p tt dd", p=128)[:, g_glob, :],
                        obuf[:, gt, :], [t_a2])
            out_toks.append(t_out)
            obuf_free[gt] = t_out
            x2r_free[b] = t_out

    if _DEBUG:
        out_toks.append(dbg_dump("gz", gz[:, :, :], [t_gz_all]))
        out_toks.append(dbg_dump("x2", x2sp[:, :], [p6_spill[gt2] for gt2 in range(8)]))
        out_toks.append(dbg_dump("kvl", kvl_d[:, :], [kvst_dma[hh] for hh in range(NH)]))
        out_toks.append(dbg_dump("ccin", cc_in.ap()[:, :],
                                 [cc_dma[hh] for hh in range(NH)]))
        out_toks.append(dbg_dump("inter", inter[:, :, :], out_toks[-1:]))
        out_toks.extend([dbg_t[k] for k in ("xn1T", "k", "v", "q", "gp", "kvinit",
                                            "o", "rrow", "xn2T")])
    P.emit("sync", lambda e: e.nop(), out_toks)
    P.emit("gpsimd", lambda e: e.nop(), out_toks)

    from contextlib import ExitStack
    with ExitStack() as es:
        sems = {s: es.enter_context(nc.semaphore(s))
                for s in ("S", "G", "T", "V", "A", "CC", "R0", "R1", "R2",
                          "VB0", "VB1", "K0", "K1", "PL")}
        with nc.Block() as block:
            block.sync(lambda e: P.run("sync", e, sems))
            block.gpsimd(lambda e: P.run("gpsimd", e, sems))
            block.tensor(lambda e: P.run("tensor", e, sems))
            block.vector(lambda e: P.run("vector", e, sems))
            block.scalar(lambda e: P.run("scalar", e, sems))
    return nc


_PERM = 4 * (np.arange(1024) % 256) + np.arange(1024) // 256
_IPERM = np.argsort(_PERM)
_NC_CACHE = [None]
_LAST_RES = [None]
_DEBUG = bool(os.environ.get("KDEBUG"))


def kernel(L0_attn_hidden_states, L0_attn_slope_rate, L0_attn_w_qkv, L0_attn_w_gate,
           L0_attn_w_out, L0_attn_norm_w, L0_ffn_gate_up_proj, L0_ffn_down_proj):
    x = np.asarray(L0_attn_hidden_states, np.float32)
    slope = np.asarray(L0_attn_slope_rate, np.float32).reshape(1, NH)
    w_qkv = np.asarray(L0_attn_w_qkv, np.float32)
    w_gate = np.asarray(L0_attn_w_gate, np.float32)
    w_out = np.asarray(L0_attn_w_out, np.float32)
    norm_w = np.asarray(L0_attn_norm_w, np.float32)
    gup = np.asarray(L0_ffn_gate_up_proj, np.float32)
    down = np.asarray(L0_ffn_down_proj, np.float32)

    def slabify(wT, w=256):
        nsl = wT.shape[1] // w
        a = wT.reshape(C, 128, nsl, w)
        return np.ascontiguousarray(a.transpose(2, 1, 0, 3).reshape(nsl * 128, C * w))

    k_cols = np.concatenate([h * 384 + 128 + np.arange(128) for h in range(NH)])
    v_cols = np.concatenate([h * 384 + 256 + np.arange(128) for h in range(NH)])
    q_cols = np.concatenate([h * 384 + np.arange(128) for h in range(NH)])
    bf = ml_dtypes.bfloat16
    wqkvT = slabify(w_qkv.T[:, np.concatenate([k_cols, q_cols])]).astype(bf)
    wv_h = slabify(w_qkv.T[:, v_cols], 512).astype(bf)
    wgT = slabify(w_gate.T).astype(bf)
    woT = slabify(w_out.T, 512).astype(bf)
    gup_cols = np.concatenate([np.concatenate([i * 128 + np.arange(128),
                                               2048 + i * 128 + np.arange(128)])
                               for i in range(16)])
    gup_h = np.concatenate([slabify(gup[e][:, gup_cols], 512)
                            for e in range(E)]).astype(bf)
    down_h = np.concatenate([slabify(down[e], 512) for e in range(E)]).astype(bf)
    slope_bc = np.ascontiguousarray(np.broadcast_to(slope, (128, NH)))
    normw_h = np.ascontiguousarray(norm_w.reshape(C, 128).T)

    if _NC_CACHE[0] is None:
        _NC_CACHE[0] = build_nc()
    nc = _NC_CACHE[0]

    in_maps = []
    for c in range(8):
        b, hf = divmod(c, 2)
        x_sh = np.ascontiguousarray(x[b, hf * T:(hf + 1) * T][_PERM])
        in_maps.append({
            "x_sh": x_sh, "wqkvT": wqkvT, "wv": wv_h, "wgT": wgT,
            "woT": woT,
            "gup_w": gup_h, "down_w": down_h, "slope_b": slope_bc,
            "normw": normw_h,
            "flag": np.full((128, 1), 1.0 if c % 2 else 0.0, np.float32),
        })
    res = run_bass_kernel_spmd(nc, in_maps, core_ids=list(range(8)))
    if _DEBUG:
        _LAST_RES[0] = res
    out = np.empty((4, 2048, D), np.float32)
    for c in range(8):
        b, hf = divmod(c, 2)
        out[b, hf * T:(hf + 1) * T] = np.asarray(res.results[c]["out"])[_IPERM]
    return out


# version 27 (final)
# speedup vs baseline: 1.3643x; 1.0000x over previous
import os
import numpy as np
import ml_dtypes
import concourse.bass as bass
import concourse.mybir as mybir
from concourse.bass_utils import run_bass_kernel_spmd

F32 = mybir.dt.float32
BF16 = mybir.dt.bfloat16
AF = mybir.ActivationFunctionType
ALU = mybir.AluOpType
AX = mybir.AxisListType.X

NH, HD, BLK = 16, 128, 256
D = 2048
T = 1024
C = 16
NB = 4
E = 4
EPS_LN = 1e-5
EPS_RMS = 1e-6
KB = 1024

OFF_A = 0 * KB
OFF_B = 32 * KB
OFF_C = 64 * KB
OFF_D = 96 * KB
OFF_E = 128 * KB
OFF_G = 160 * KB
OFF_W = 175 * KB
OFF_S = 191 * KB
ARENA = 196 * KB


class Prog:

    ENGS = ("sync", "gpsimd", "tensor", "vector", "scalar")

    def __init__(self):
        self.ops = {e: [] for e in self.ENGS}
        self.count = {}

    def emit(self, eng, fn, deps=(), inc=None, n=1):
        deps = tuple(d for d in deps if d is not None)
        self.ops[eng].append((fn, deps, inc, n))
        if inc is not None:
            self.count[inc] = self.count.get(inc, 0) + n
            return (inc, self.count[inc])
        return None

    def run(self, engname, eng, sems):
        seen = {}
        for fn, deps, inc, n in self.ops[engname]:
            for s, v in deps:
                if seen.get(s, 0) < v:
                    eng.wait_ge(sems[s], v)
                    seen[s] = v
            ins = fn(eng)
            if inc is not None:
                ins.then_inc(sems[inc], n)


_DEBUG = bool(os.environ.get("KDEBUG"))


def build_nc():
    nc = bass.Bass("TRN2", target_bir_lowering=False)
    dbg_t = {}
    dbg_dram = {}

    x_in = nc.dram_tensor("x_sh", [T, D], F32, kind="ExternalInput").ap()
    wqkvT = nc.dram_tensor("wqkvT", [16 * 128, C * 256], BF16, kind="ExternalInput").ap()
    wv = nc.dram_tensor("wv", [4 * 128, C * 512], BF16, kind="ExternalInput").ap()
    wgT = nc.dram_tensor("wgT", [8 * 128, C * 256], BF16, kind="ExternalInput").ap()
    woT = nc.dram_tensor("woT", [4 * 128, C * 512], BF16, kind="ExternalInput").ap()
    gup_w = nc.dram_tensor("gup_w", [E * 8 * 128, C * 512], BF16, kind="ExternalInput").ap()
    down_w = nc.dram_tensor("down_w", [E * 4 * 128, C * 512], BF16, kind="ExternalInput").ap()
    slope_b = nc.dram_tensor("slope_b", [128, NH], F32, kind="ExternalInput").ap()
    normw_in = nc.dram_tensor("normw", [128, C], F32, kind="ExternalInput").ap()
    flag_in = nc.dram_tensor("flag", [128, 1], F32, kind="ExternalInput").ap()
    out_d = nc.dram_tensor("out", [T, D], F32, kind="ExternalOutput").ap()

    cc_in = nc.dram_tensor("cc_in", [128, NH * HD], F32, kind="Internal")
    cc_out = nc.dram_tensor("cc_out", [256, NH * HD], F32, kind="Internal")
    x2sp = nc.dram_tensor("x2spill", [T, D], F32, kind="Internal").ap()
    kvl_d = nc.dram_tensor("kvloc_d", [NH * 128, 3 * HD], F32, kind="Internal").ap()

    idxt_np = np.empty((128, 2, BLK), np.float32)
    for j in range(2):
        n_idx = (j * 128 + np.arange(128))[:, None]
        m_idx = np.arange(BLK)[None, :]
        idxt_np[:, j, :] = np.where(m_idx >= n_idx,
                                    (m_idx - n_idx).astype(np.float32), 1e9)
    arrq_np = np.arange(1, BLK + 1, dtype=np.float32).reshape(1, BLK)
    arrk_np = np.empty((128, 2), np.float32)
    for j in range(2):
        arrk_np[:, j] = 255.0 - (j * 128 + np.arange(128))
    idxt_d = nc.inline_tensor(idxt_np, "idxt_c").ap()
    arrq_d = nc.inline_tensor(arrq_np, "arrq_c").ap()
    arrk_d = nc.inline_tensor(arrk_np, "arrk_c").ap()
    ident_d = nc.inline_tensor(np.eye(128, dtype=np.float32), "ident_c").ap()
    ones_c_d = nc.inline_tensor(np.ones((128, 1), np.float32), "ones_c_c").ap()
    ones_r_d = nc.inline_tensor(np.ones((1, 128), np.float32), "ones_r_c").ap()

    nc.alloc_sbuf_tensor("arena", [128, ARENA], mybir.dt.uint8)
    base = nc.sbuf_base - ARENA

    def place(name, shape, dtype, off):
        return nc.alloc_sbuf_tensor_at(name, shape, dtype, offset=base + off).ap()

    xn1T = place("xn1T", [128, C, T], BF16, OFF_A)
    kvinit = place("kvinit", [128, NH * HD], F32, OFF_A)
    xn2T = place("xn2T", [128, C, T], BF16, OFF_A)
    qf = place("qf", [128, NH, T], BF16, OFF_B)
    vbuf = [place(f"vb{i}", [128, C, 512], BF16, OFF_B + i * 16 * KB)
            for i in range(2)]
    woB = place("woB", [128, 2, C, 512], BF16, OFF_B)
    inter = place("inter", [128, C, 256], BF16, OFF_B)
    x2r = place("x2r", [128, 2, D], F32, OFF_B + 8 * KB)
    sgate = place("sgate", [128, 2, 256], F32, OFF_B + 24 * KB)
    kf = place("kf", [128, NH, T], BF16, OFF_C)
    xin = place("xin", [128, 2, D], F32, OFF_C)
    aout = place("aout", [128, 2, D], F32, OFF_C + 8 * KB)
    xn2s = place("xn2s", [128, 2, D], BF16, OFF_C + 24 * KB)
    v_tm = place("v_tm", [128, 8, D], BF16, OFF_D)
    gz = place("gz", [128, C, T], BF16, OFF_D)
    dw = [place(f"dw{i}", [128, C, 512], BF16, OFF_D + i * 16 * KB) for i in range(2)]
    gp = place("gp", [128, C, T], BF16, OFF_E)
    woE3 = place("woE3", [128, C, 512], BF16, OFF_E + 16 * KB)
    obuf = place("obuf", [128, 2, D], F32, OFF_E)
    rrow = place("rrow", [1, T], F32, OFF_E)
    kvrun = place("kvrun", [128, 2, HD], F32, OFF_G)
    kvst = place("kvst", [128, 2, 3, HD], F32, OFF_G + 1 * KB)
    ktil = place("ktil", [128, 2, HD], BF16, OFF_G + 4 * KB)
    kdec = place("kdec", [128, 2], F32, OFF_G + 4 * KB + 512)
    arrk = place("arrk", [128, 2], F32, OFF_G + 4 * KB + 576)
    kvld = place("kvld", [128, 2, 3, HD], F32, OFF_G + 5 * KB)
    diagT = place("diagT", [128, 2, 2, BLK], BF16, OFF_G + 8 * KB)
    qdecb = place("qdecb", [128, 2, BLK], BF16, OFF_G + 10 * KB)
    qtil = place("qtil", [128, 2, BLK], BF16, OFF_G + 11 * KB)
    qks = place("qks", [128, 2, 2, BLK], BF16, OFF_G + 12 * KB)
    kvt = place("kvt", [128, 2, HD], BF16, OFF_G + 14 * KB)
    lamt = place("lamt", [128, 2, NB], F32, OFF_G + 14 * KB + 512)
    sqG = place("sqG", [128, 4, T], BF16, OFF_G)
    wsl = [place(f"wsl{i}", [128, C, 256], BF16, OFF_W + i * 8 * KB) for i in range(2)]
    woW = place("woW", [128, C, 512], BF16, OFF_W)
    rw8 = place("rw8", [128, C, 512], BF16, OFF_W)
    ident = place("ident", [128, 128], BF16, OFF_S)
    ones_c = place("ones_c", [128, 1], BF16, OFF_S + 256)
    ones_r = place("ones_r", [1, 128], BF16, OFF_S + 288)
    arrq = place("arrq", [1, BLK], F32, OFF_S + 544)
    qdrow = place("qdrow", [1, 2, BLK], BF16, OFF_S + 1568)
    normw = place("normw", [128, C], F32, OFF_S + 2592)
    slopes = place("slopes", [128, NH], F32, OFF_S + 2656)
    negs = place("negs", [128, NH], F32, OFF_S + 2720)
    flag = place("flag", [128, 1], F32, OFF_S + 2784)
    rstdc = place("rstdc", [128, 8], F32, OFF_S + 2816)
    stat = place("stat", [128, 4, 8], F32, OFF_S + 2848)
    bnst = place("bnst", [128, 4, 24], F32, OFF_S + 2976)
    identF = place("identF", [128, 128], F32, OFF_S + 3360)
    idxt = place("idxt", [128, 2, BLK], BF16, OFF_S + 3872)

    fb = [nc.alloc_psum_tensor(f"fb{i}", [128, 512], F32).ap() for i in range(6)]
    tbk = [nc.alloc_psum_tensor(f"tb{i}", [128, 1024], BF16).ap() for i in range(2)]

    P = Prog()

    def dma(q, out, in_, deps=(), sem=None):
        return P.emit(q, lambda e, o=out, i=in_: e.dma_start(out=o, in_=i),
                      deps, sem or ("S" if q == "sync" else "G"), 16)

    def mm(out, lhsT, rhs, start, stop, deps=(), inc=False):
        return P.emit("tensor",
                      lambda e, o=out, l=lhsT, r=rhs, st=start, sp=stop:
                      e.matmul(o, l, r, start=st, stop=sp),
                      deps, "T" if inc else None, 1)

    def tr(out, in_, deps=(), inc=False, idt=None):
        idm = ident if idt is None else idt
        return P.emit("tensor", lambda e, o=out, i=in_, m=idm: e.transpose(o, i, m),
                      deps, "T" if inc else None, 1)

    def act(fn, deps=(), inc=False):
        return P.emit("scalar", fn, deps, "A" if inc else None, 1)

    def vec(fn, deps=(), inc=False):
        return P.emit("vector", fn, deps, "V" if inc else None, 1)

    bank_free = [None] * 6
    grp_i = [0]

    def next_bank():
        i = grp_i[0] % 6
        grp_i[0] += 1
        return i, bank_free[i]

    def dbg_dump(name, src_ap, deps):
        if not _DEBUG:
            return None
        th = nc.dram_tensor("d_" + name, list(src_ap.shape), src_ap.dtype,
                            kind="ExternalOutput")
        dbg_dram[name] = th
        dbg_t[name] = dma("sync", th.ap(), src_ap, deps)
        return dbg_t[name]

    def ln_bn(src, b, deps, per_chunk=False, eps=EPS_LN):
        toks = []
        for s4 in range(4):
            d = deps[s4] if per_chunk else (deps if s4 == 0 else ())
            toks.append(vec(lambda e, s4=s4, b=b, src=src:
                            e.bn_stats(bnst[:, b, s4 * 6:(s4 + 1) * 6],
                                       src[:, s4 * 512:(s4 + 1) * 512]),
                            d, inc=True))
        t_ag = vec(lambda e, b=b: e.bn_aggr(stat[:, b, 0:2], bnst[:, b, :]),
                   toks, inc=True)
        if eps is None:
            return t_ag, None
        t5 = vec(lambda e, b=b, eps=eps:
                 e.tensor_scalar(stat[:, b, 2:3], stat[:, b, 1:2],
                                 eps, None, ALU.add), [t_ag], inc=True)
        t6 = vec(lambda e, b=b: e.reciprocal(stat[:, b, 2:3], stat[:, b, 2:3]),
                 [t5], inc=True)
        t7 = act(lambda e, b=b: e.activation(stat[:, b, 2:3], stat[:, b, 2:3],
                                             AF.Sqrt), [t6], inc=True)
        return t_ag, t7

    dma("gpsimd", ident[:, :], ident_d[:, :])
    dma("gpsimd", identF[:, :], ident_d[:, :])
    dma("gpsimd", ones_c[:, :], ones_c_d[:, :])
    dma("gpsimd", ones_r[:, :], ones_r_d[:, :])
    dma("gpsimd", idxt[:, :, :], idxt_d[:, :, :])
    dma("gpsimd", arrq[:, :], arrq_d[:, :])
    dma("gpsimd", arrk[:, :], arrk_d[:, :])
    dma("gpsimd", normw[:, :], normw_in[:, :])
    dma("gpsimd", slopes[:, :], slope_b[:, :])
    t_cst = dma("gpsimd", flag[:, :], flag_in[:, :])
    t_negs = vec(lambda e: e.tensor_scalar(negs[:, :], slopes[:, :], -1.0, None,
                                           ALU.mult), [t_cst], inc=True)

    wq_r = wqkvT.rearrange("(s p) w -> p s w", p=128)
    wv_r = wv.rearrange("(s p) w -> p s w", p=128)
    wg_r = wgT.rearrange("(s p) w -> p s w", p=128)
    wo_r = woT.rearrange("(s p) w -> p s w", p=128)
    gup_r = gup_w.rearrange("(ee s p) w -> p ee s w", p=128, ee=E)
    dwn_r = down_w.rearrange("(ee s p) w -> p ee s w", p=128, ee=E)
    slab_done = {}
    slab_loaded = {}
    slab_serial = [0]

    def load_slab(src_ap, extra=(), q="gpsimd"):
        s = slab_serial[0]
        d = list(extra)
        if s >= 2:
            d.append(slab_done.get(s - 2))
        slab_loaded[s] = dma(q, wsl[s % 2].rearrange("p c w -> p (c w)"),
                             src_ap, d)
        slab_serial[0] = s + 1
        return s, wsl[s % 2]

    class Ring:
        def __init__(self, bufs, init_tok=None, sembase="R"):
            self.bufs = bufs
            self.n = len(bufs)
            self.s = 0
            self.done = {}
            self.loaded = {}
            self.init_tok = init_tok
            self.sembase = sembase

        def load(self, src_ap, extra=()):
            i = self.s
            d = list(extra) + [self.done.get(i - self.n, self.init_tok)]
            buf = self.bufs[i % self.n]
            tok = dma("gpsimd", buf.rearrange("p c w -> p (c w)"), src_ap, d,
                      sem=f"{self.sembase}{i % self.n}")
            self.loaded[i] = tok
            self.s = i + 1
            return i, buf, tok

    x_r = x_in.rearrange("(tt p) dd -> p tt dd", p=128)
    p0_evac = []
    t_half = [None, None]
    for idx, tt in enumerate([0, 2, 4, 6, 1, 3, 5, 7]):
        b = idx % 2
        d = [p0_evac[2 * (idx - 2) + 1]] if idx >= 2 else []
        t_x = dma("sync", xin[:, b, :], x_r[:, tt, :], d)
        _, t_rstd = ln_bn(xin[:, b, :], b, [t_x])
        t_app = vec(lambda e, b=b: e.tensor_scalar(xn2s[:, b, :], xin[:, b, :],
                                                   stat[:, b, 0:1], stat[:, b, 2:3],
                                                   ALU.subtract, ALU.mult),
                    [t_rstd], inc=True)
        for half in range(2):
            bank = tbk[half]
            tdeps = [t_app, p0_evac[-1] if p0_evac else t_cst]
            tok_t = None
            for c8 in range(8):
                c = half * 8 + c8
                tok_t = tr(bank[:, c8 * 128:(c8 + 1) * 128],
                           xn2s[:, b, c * 128:(c + 1) * 128],
                           tdeps if c8 == 0 else (), inc=(c8 == 7))
            off = 512 * (tt % 2) + tt // 2
            t_ev = act(lambda e, bank=bank, off=off, half=half:
                       e.activation(xn1T[:, half * 8:(half + 1) * 8,
                                         off:off + 509:4],
                                    bank[:, :], AF.Copy), [tok_t], inc=True)
            p0_evac.append(t_ev)
        if idx == 3:
            t_half[0] = p0_evac[-1]
    t_half[1] = p0_evac[-1]
    t_xn1T = p0_evac[-1]
    dbg_dump("xn1T", xn1T[:, :, :], [t_xn1T])

    silu_evac = {}

    def kf_guard(h):
        if h <= 3:
            return p0_evac[13]
        if h <= 7 or h >= 12:
            return t_xn1T
        return None

    def proj_fm(buf, dst, kind, heads, tok_load):
        tok_mm = None
        for th in range(2):
            for obi, h in enumerate(heads):
                bi, bfree = next_bank()
                for kc in range(C):
                    deps = [tok_load, bfree, t_half[th]] if kc == 0 else ()
                    tok_mm = mm(fb[bi][:, :], buf[:, kc, obi * 128:(obi + 1) * 128],
                                xn1T[:, kc, th * 512:(th + 1) * 512],
                                start=(kc == 0), stop=(kc == C - 1),
                                deps=deps, inc=(kc == C - 1))
                gd = kf_guard(h) if kind == "k" else None
                t_ev = act(lambda e, bi=bi, dst=dst, h=h, th=th:
                           e.activation(dst[:, h, th * 512:(th + 1) * 512],
                                        fb[bi][:, :], AF.Silu), [tok_mm, gd],
                           inc=True)
                bank_free[bi] = t_ev
                silu_evac[(kind, h, th)] = t_ev
        return tok_mm

    for s in range(8):
        ss, buf = load_slab(wq_r[:, s, :])
        slab_done[ss] = proj_fm(buf, kf, "k", [2 * s, 2 * s + 1], slab_loaded[ss])
    dbg_dump("k", kf[:, :, :], [silu_evac[("k", NH - 1, 1)]])
    tb_guard = t_xn1T
    vring = Ring(vbuf, sembase="VB")
    for s in range(4):
        vi, buf, tok_ld = vring.load(wv_r[:, s, :])
        tok_mm = None
        for tt in range(8):
            bi, bfree = next_bank()
            for kc in range(C):
                deps = [tok_ld, bfree, t_half[tt // 4]] if kc == 0 else ()
                tok_mm = mm(fb[bi][:, :], xn1T[:, kc, tt * 128:(tt + 1) * 128],
                            buf[:, kc, :], start=(kc == 0), stop=(kc == C - 1),
                            deps=deps, inc=(kc == C - 1))
            t_ev = act(lambda e, bi=bi, tt=tt, s=s:
                       e.activation(v_tm[:, tt, 512 * s:512 * (s + 1)],
                                    fb[bi][:, :], AF.Silu), [tok_mm], inc=True)
            bank_free[bi] = t_ev
            silu_evac[("v", tt, s)] = t_ev
        vring.done[vi] = tok_mm
    dbg_dump("v", v_tm[:, :, :], [silu_evac[("v", 7, 3)]])

    cc_dma = {}
    kvst_dma = {}
    pass1_prev = None
    for h in range(NH):
        hb = h % 2
        t_lam = act(lambda e, h=h: e.activation(lamt[:, 0, 0:1], negs[:, h:h + 1],
                                                AF.Exp, scale=256.0),
                    [t_negs, pass1_prev], inc=True)
        t_kd = act(lambda e, h=h: e.activation(kdec[:, :], arrk[:, :], AF.Exp,
                                               scale=negs[:, h:h + 1]),
                   [t_negs, t_cst], inc=True)
        tok_kvup = None
        tok_st = None
        for t in range(NB):
            ktoks = []
            for j in range(2):
                bank = tbk[j]
                deps = [silu_evac[("k", h, (t * 256 + j * 128) // 512)], tb_guard]
                tok_t = tr(bank[:, 0:128],
                           kf[:, h, t * 256 + j * 128:t * 256 + (j + 1) * 128],
                           deps, inc=True)
                t_ke = vec(lambda e, j=j, bank=bank:
                           e.tensor_scalar(ktil[:, j, :], bank[:, 0:128],
                                           kdec[:, j:j + 1], None, ALU.mult),
                           [tok_t, t_kd], inc=True)
                tb_guard = t_ke
                ktoks.append(t_ke)
            if t >= 1:
                tok_st = vec(lambda e, t=t, hb=hb:
                             e.tensor_copy(kvst[:, hb, t - 1, :], kvrun[:, hb, :]),
                             [tok_kvup, kvst_dma.get(h - 2)], inc=True)
            bi, bfree = next_bank()
            tok_c = None
            for j in range(2):
                deps = [ktoks[j], silu_evac[("v", 2 * t + j, h // 4)]]
                if j == 0:
                    deps.append(bfree)
                tok_c = mm(fb[bi][:, 0:128], ktil[:, j, :],
                           v_tm[:, 2 * t + j, h * 128:(h + 1) * 128],
                           start=(j == 0), stop=(j == 1), deps=deps, inc=(j == 1))
            if t == 0:
                tok_kvup = vec(lambda e, hb=hb, bi=bi:
                               e.tensor_copy(kvrun[:, hb, :], fb[bi][:, 0:128]),
                               [tok_c, cc_dma.get(h - 2)], inc=True)
            else:
                tok_kvup = vec(lambda e, hb=hb, bi=bi:
                               e.scalar_tensor_tensor(kvrun[:, hb, :], kvrun[:, hb, :],
                                                      lamt[:, 0, 0:1], fb[bi][:, 0:128],
                                                      ALU.mult, ALU.add),
                               [tok_c, t_lam, tok_kvup], inc=True)
            bank_free[bi] = tok_kvup
        kvst_dma[h] = dma("sync", kvl_d[h * 128:(h + 1) * 128, :],
                          kvst[:, hb, :, :].rearrange("p a w -> p (a w)"), [tok_st],
                          sem=f"KS{h % 2}")
        cc_dma[h] = dma("sync", cc_in.ap()[:, h * 128:(h + 1) * 128],
                        kvrun[:, hb, :], [tok_kvup], sem="CI")
        pass1_prev = tok_kvup

    for s in range(8):
        ss, buf = load_slab(wq_r[:, 8 + s, :])
        slab_done[ss] = proj_fm(buf, qf, "q", [2 * s, 2 * s + 1], slab_loaded[ss])
    dbg_dump("q", qf[:, :, :], [silu_evac[("q", NH - 1, 1)]])

    t_cc = P.emit("gpsimd",
                  lambda e: e.collective_compute(
                      "AllGather", ALU.bypass,
                      replica_groups=[[0, 1], [2, 3], [4, 5], [6, 7]],
                      ins=[cc_in.ap()[:, :]], outs=[cc_out.ap()[:, :]]),
                  [cc_dma[h] for h in range(NH)], "CC", 1)

    for s in range(8):
        ss, buf = load_slab(wg_r[:, s, :], q="sync")
        tok_mm = None
        for obi in range(2):
            c = 2 * s + obi
            for th in range(2):
                bi, bfree = next_bank()
                for kc in range(C):
                    deps = [slab_loaded[ss], bfree, t_half[th]] if kc == 0 else ()
                    tok_mm = mm(fb[bi][:, :], buf[:, kc, obi * 128:(obi + 1) * 128],
                                xn1T[:, kc, th * 512:(th + 1) * 512],
                                start=(kc == 0), stop=(kc == C - 1),
                                deps=deps, inc=(kc == C - 1))
                t_sg = act(lambda e, bi=bi, c=c, th=th:
                           e.activation(gp[:, c, th * 512:(th + 1) * 512],
                                        fb[bi][:, :], AF.Sigmoid), [tok_mm], inc=True)
                t_nw = vec(lambda e, c=c, th=th:
                           e.tensor_scalar(gp[:, c, th * 512:(th + 1) * 512],
                                           gp[:, c, th * 512:(th + 1) * 512],
                                           normw[:, c:c + 1], None, ALU.mult),
                           [t_sg, t_cst], inc=True)
                bank_free[bi] = t_nw
        slab_done[ss] = tok_mm
        t_gp_last = t_nw
    t_gate_mm_last = slab_done[23]
    dbg_dump("gp", gp[:, :, :], [t_gp_last])

    t_kvld = dma("sync", kvinit[:, :], cc_out.ap()[0:128, :],
                 [t_cc, t_gate_mm_last], sem="KV")
    t_kvfl = vec(lambda e: e.tensor_scalar(kvinit[:, :], kvinit[:, :], flag[:, 0:1],
                                           None, ALU.mult), [t_kvld, t_cst], inc=True)
    dbg_dump("kvinit", kvinit[:, :], [t_kvfl])
    t_qkz = vec(lambda e: e.memset(qks[:, 0, 1, 0:128], 0.0), [], inc=True)
    t_qkz2 = vec(lambda e: e.memset(qks[:, 1, 1, 0:128], 0.0), [], inc=True)

    iters = []
    for hp in range(8):
        for t in range(NB):
            for sub in range(2):
                iters.append((2 * hp + sub, t))

    o_evac = {}
    head_drain = {}
    pass2_last = None
    kvt_free = [None, None]
    qtil_free = [None, None]
    qks_free = [t_qkz, t_qkz2]
    kvld_dma = {}
    kvld_free = [None, None]
    hstate = [None, None]
    pend_o = None
    for i in range(len(iters) + 1):
        if pend_o is not None:
            (ph, pt, ptb2, pqke0, pqke1, pkvt, pqt, pbio, pbfree_o) = pend_o
            mm(fb[pbio][:, 0:256], v_tm[:, 2 * pt, ph * 128:(ph + 1) * 128],
               qks[:, ptb2, 0, :], start=True, stop=False,
               deps=[pqke0, pbfree_o, silu_evac[("v", 2 * pt, ph // 4)]])
            mm(fb[pbio][:, 128:256], v_tm[:, 2 * pt + 1, ph * 128:(ph + 1) * 128],
               qks[:, ptb2, 1, 128:256], start=False, stop=False,
               deps=[pqke1, silu_evac[("v", 2 * pt + 1, ph // 4)]])
            tok_o = mm(fb[pbio][:, 0:256], kvt[:, ptb2, :], qtil[:, ptb2, :],
                       start=False, stop=True, deps=[pkvt, pqt], inc=True)
            kvt_free[ptb2] = tok_o
            qtil_free[ptb2] = tok_o
            qks_free[ptb2] = tok_o
            t_oe = act(lambda e, pbio=pbio, ph=ph, pt=pt:
                       e.activation(kf[:, ph, pt * 256:(pt + 1) * 256],
                                    fb[pbio][:, 0:256], AF.Copy), [tok_o], inc=True)
            o_evac[(ph, pt)] = t_oe
            bank_free[pbio] = t_oe
            pass2_last = t_oe
            if pt == 3:
                kvld_free[ph % 2] = t_oe
                head_drain[ph] = t_oe
        if i == len(iters):
            break
        h, t = iters[i]
        par = h % 2
        if t == 0:
            kvld_dma[h] = dma("scalar",
                              kvld[:, par, :, :].rearrange("p a w -> p (a w)"),
                              kvl_d[h * 128:(h + 1) * 128, :],
                              [kvst_dma[h], kvld_free[par]], sem=f"K{par}")
            deps_c = [t_negs, kvld_free[par]]
            t_dg = act(lambda e, h=h, par=par:
                       e.activation(diagT[:, par, :, :], idxt[:, :, :],
                                    AF.Exp, scale=negs[:, h:h + 1]),
                       deps_c, inc=True)
            t_qr = act(lambda e, h=h, par=par:
                       e.activation(qdrow[0:1, par, :], arrq[:, :], AF.Exp,
                                    scale=negs[0:1, h:h + 1]),
                       deps_c, inc=True)
            bi_q, bfree_q = next_bank()
            t_qb = mm(fb[bi_q][:, 0:256], ones_r[:, :], qdrow[0:1, par, :],
                      True, True, [t_qr, bfree_q], inc=True)
            t_qbe = act(lambda e, bi_q=bi_q, par=par:
                        e.activation(qdecb[:, par, :], fb[bi_q][:, 0:256], AF.Copy),
                        [t_qb], inc=True)
            bank_free[bi_q] = t_qbe
            lam_toks = [act(lambda e, h=h, par=par, t2=t2:
                            e.activation(lamt[:, par, t2:t2 + 1], negs[:, h:h + 1],
                                         AF.Exp, scale=256.0 * t2),
                            deps_c, inc=True)
                        for t2 in range(1, NB)]
            hstate[par] = {"qbe": t_qbe, "dg": t_dg, "lam": lam_toks}
        tb2 = i % 2
        if t == 0:
            t_kvt = P.emit("gpsimd",
                           lambda e, h=h, tb2=tb2:
                           e.tensor_copy(kvt[:, tb2, :],
                                         kvinit[:, h * 128:(h + 1) * 128]),
                           [t_kvfl, kvt_free[tb2]], "PL", 1)
        else:
            t_kvt = vec(lambda e, h=h, t=t, tb2=tb2, par=par:
                        e.scalar_tensor_tensor(kvt[:, tb2, :],
                                               kvinit[:, h * 128:(h + 1) * 128],
                                               lamt[:, par, t:t + 1],
                                               kvld[:, par, t - 1, :],
                                               ALU.mult, ALU.add),
                        [t_kvfl, hstate[par]["lam"][t - 1], kvld_dma[h],
                         kvt_free[tb2]], inc=True)
        th = t // 2
        t_qt = P.emit("gpsimd",
                      lambda e, h=h, t=t, tb2=tb2, par=par:
                      e.tensor_tensor(qtil[:, tb2, :],
                                      qf[:, h, t * 256:(t + 1) * 256],
                                      qdecb[:, par, :], ALU.mult),
                      [hstate[par]["qbe"], silu_evac[("q", h, th)], qtil_free[tb2]],
                      "PL", 1)
        bi_qk, bfree_qk = next_bank()
        qk_mm = []
        for j in range(2):
            deps = [silu_evac[("k", h, th)], silu_evac[("q", h, th)]]
            if j == 0:
                deps.append(bfree_qk)
            qk_mm.append(mm(fb[bi_qk][:, j * 256:(j + 1) * 256],
                            kf[:, h, t * 256 + j * 128:t * 256 + (j + 1) * 128],
                            qf[:, h, t * 256:(t + 1) * 256],
                            start=True, stop=True, deps=deps, inc=True))
        t_qke0 = vec(lambda e, bi_qk=bi_qk, tb2=tb2, par=par:
                     e.tensor_tensor(qks[:, tb2, 0, :], fb[bi_qk][:, 0:256],
                                     diagT[:, par, 0, :], ALU.mult),
                     [qk_mm[0], hstate[par]["dg"], qks_free[tb2]], inc=True)
        t_qke1 = vec(lambda e, bi_qk=bi_qk, tb2=tb2, par=par:
                     e.tensor_tensor(qks[:, tb2, 1, 128:256],
                                     fb[bi_qk][:, 384:512],
                                     diagT[:, par, 1, 128:256], ALU.mult),
                     [qk_mm[1], hstate[par]["dg"]], inc=True)
        bank_free[bi_qk] = t_qke1
        bi_o, bfree_o = next_bank()
        pend_o = (h, t, tb2, t_qke0, t_qke1, t_kvt, t_qt, bi_o, bfree_o)

    dbg_dump("o", kf[:, :, :], [pass2_last])

    wo_tok = {}
    wo_tok[2] = dma("sync", woW[:, :, :].rearrange("p c w -> p (c w)"),
                    wo_r[:, 2, :], [t_gate_mm_last], sem="WO2")
    wo_tok[0] = dma("sync", woB[:, 0, :, :].rearrange("p c w -> p (c w)"),
                    wo_r[:, 0, :], [head_drain[7], dbg_t.get("q")], sem="WO0")
    wo_tok[1] = dma("sync", woB[:, 1, :, :].rearrange("p c w -> p (c w)"),
                    wo_r[:, 1, :], [head_drain[15]], sem="WO1")

    def wo_res(s):
        return woB[:, s, :, :] if s < 2 else (woW[:, :, :] if s == 2
                                              else woE3[:, :, :])

    gz_tok = {}
    first_gz = True
    for e4 in range(4):
        for c in range(C):
            deps = [pass2_last] if first_gz else []
            first_gz = False
            gz_tok[(c, e4)] = vec(
                lambda e, c=c, e4=e4: e.tensor_tensor(
                    gz[:, c, e4 * 256:(e4 + 1) * 256],
                    gp[:, c, e4:T:4], kf[:, c, e4:T:4], ALU.mult),
                deps, inc=True)
    t_gz_all = gz_tok[(C - 1, 3)]
    wo_tok[3] = dma("sync", woE3[:, :, :].rearrange("p c w -> p (c w)"),
                    wo_r[:, 3, :], [t_gz_all], sem="WO3")
    wo_dep = [wo_tok[0], wo_tok[1], wo_tok[2], wo_tok[3]]

    p6_mm = {}
    p6_bfree = [bank_free[s] for s in range(4)]

    def p6_chains(gt):
        for s in range(4):
            bi = s
            for kc in range(C):
                deps = [wo_dep[s], p6_bfree[s]] if kc == 0 else []
                deps = list(deps) + [gz_tok[(kc, gt // 2)]]
                tok_mm = mm(fb[bi][:, :], gz[:, kc, gt * 128:(gt + 1) * 128],
                            wo_res(s)[:, kc, :], start=(kc == 0), stop=(kc == C - 1),
                            deps=deps, inc=(kc == C - 1))
            p6_mm[(gt, s)] = tok_mm

    ev_tok = {}
    aout_free = [None, None]

    def p6_evacs(gt, guard):
        gi = gt % 2
        for s in range(4):
            t_ev = act(lambda e, s=s, gi=gi, gt=gt:
                       e.activation(aout[:, gi, s * 512:(s + 1) * 512],
                                    fb[s][:, :], AF.Copy),
                       [p6_mm[(gt, s)], guard if s == 0 else None], inc=True)
            ev_tok[(gt, s)] = t_ev
            p6_bfree[s] = t_ev

    p6_chains(0)

    t_sq = {}
    p5mm = {}
    for c in range(C):
        sl = c % 4
        deps = [p5mm.get(c - 4)]
        if c == 0:
            deps.append(pass2_last)
        t_sq[c] = vec(lambda e, c=c, sl=sl:
                      e.tensor_tensor(sqG[:, sl, :], kf[:, c, :], kf[:, c, :],
                                      ALU.mult), deps, inc=True)
        cur = None
        for thh in range(2):
            bi = 4 + thh
            rhs = sqG[:, sl, :].rearrange("p (tq t4) -> p t4 tq",
                                          t4=4)[:, 2 * thh:2 * thh + 2, :]
            deps2 = [t_sq[c]]
            if c == 0:
                deps2.append(bank_free[bi])
            cur = mm(fb[bi][0:1, :], ones_c[:, :], rhs,
                     start=(c == 0), stop=(c == C - 1), deps=deps2, inc=True)
        p5mm[c] = cur
    r1 = vec(lambda e: e.tensor_scalar(rrow[0:1, 0:512], fb[4][0:1, :],
                                       1.0 / D, EPS_RMS, ALU.mult, ALU.add),
             [p5mm[C - 1]], inc=True)
    r2 = vec(lambda e: e.tensor_scalar(rrow[0:1, 512:1024], fb[5][0:1, :],
                                       1.0 / D, EPS_RMS, ALU.mult, ALU.add),
             [p5mm[C - 1]], inc=True)
    r3 = vec(lambda e: e.reciprocal(rrow[0:1, 0:512], rrow[0:1, 0:512]),
             [r1], inc=True)
    t_rm = vec(lambda e: e.reciprocal(rrow[0:1, 512:1024], rrow[0:1, 512:1024]),
               [r2, r3], inc=True)
    t_rs = act(lambda e: e.activation(rrow[0:1, :], rrow[0:1, :], AF.Sqrt), [t_rm],
               inc=True)
    bank_free[4] = r1
    bank_free[5] = r2
    tok_tr = None
    for gt8 in range(8):
        d = [t_rs, r1, r3] if gt8 == 0 else ()
        tok_tr = tr(fb[4][:, gt8:gt8 + 1], rrow[0:1, gt8 * 128:(gt8 + 1) * 128],
                    d, inc=(gt8 == 7), idt=identF[0:1, 0:1])
    t_rc = act(lambda e: e.activation(rstdc[:, :], fb[4][:, 0:8], AF.Copy),
               [tok_tr], inc=True)
    bank_free[4] = t_rc
    dbg_dump("rrow", rrow[0:1, :], [t_rs])

    xin_free = [None]
    p6_xn2_evac = []
    p6_spill = {}
    xn2_done = {}
    pend_tr = []

    def p6_epilogue(gt):
        gi = gt % 2
        b = gt % 2
        ag1, _ = ln_bn(aout[:, gi, :], b,
                       [[ev_tok[(gt, s)]] for s in range(4)],
                       per_chunk=True, eps=None)
        u1 = vec(lambda e, b=b, gt=gt: e.tensor_tensor(stat[:, b, 3:4],
                                                       rstdc[:, gt:gt + 1],
                                                       rstdc[:, gt:gt + 1],
                                                       ALU.mult), [t_rc, ag1],
                 inc=True)
        u2 = vec(lambda e, b=b: e.tensor_tensor(stat[:, b, 3:4], stat[:, b, 1:2],
                                                stat[:, b, 3:4], ALU.mult),
                 [u1], inc=True)
        u3 = vec(lambda e, b=b: e.tensor_scalar(stat[:, b, 3:4], stat[:, b, 3:4],
                                                EPS_LN, None, ALU.add), [u2],
                 inc=True)
        u4 = vec(lambda e, b=b: e.reciprocal(stat[:, b, 3:4], stat[:, b, 3:4]),
                 [u3], inc=True)
        u5 = act(lambda e, b=b: e.activation(stat[:, b, 3:4], stat[:, b, 3:4],
                                             AF.Sqrt), [u4], inc=True)
        u6 = vec(lambda e, b=b, gt=gt: e.tensor_tensor(stat[:, b, 4:5],
                                                       stat[:, b, 3:4],
                                                       rstdc[:, gt:gt + 1],
                                                       ALU.mult), [u5], inc=True)
        t_x = dma("sync", xin[:, 0, :], x_r[:, gt, :],
                  [xin_free[0] if gt > 0 else t_sq[3]])
        t_a = vec(lambda e, b=b, gi=gi:
                  e.tensor_scalar(aout[:, gi, :], aout[:, gi, :], stat[:, b, 0:1],
                                  stat[:, b, 4:5], ALU.subtract, ALU.mult),
                  [u6], inc=True)
        t_a2 = vec(lambda e, gi=gi:
                   e.tensor_tensor(aout[:, gi, :], aout[:, gi, :], xin[:, 0, :],
                                   ALU.add), [t_a, t_x], inc=True)
        xin_free[0] = t_a2
        _, t_rstd2 = ln_bn(aout[:, gi, :], b, [t_a2])
        t_app2 = vec(lambda e, b=b, gi=gi:
                     e.tensor_scalar(xn2s[:, b, :], aout[:, gi, :],
                                     stat[:, b, 0:1], stat[:, b, 2:3],
                                     ALU.subtract, ALU.mult),
                     [t_rstd2,
                      p6_xn2_evac[-1] if p6_xn2_evac else t_sq[15]],
                     inc=True)
        t_sp = dma("sync",
                   x2sp.rearrange("(tt p) dd -> p tt dd", p=128)[:, gt, :],
                   aout[:, gi, :], [t_app2], sem=f"SP{gt % 2}")
        p6_spill[gt] = t_sp
        aout_free[gi] = t_sp
        pend_tr.append((gt, t_app2))

    def p6_transposes(gt, t_app2):
        b = gt % 2
        evs = []
        for half in range(2):
            bank = tbk[half]
            tdeps = [t_app2, p6_xn2_evac[-1] if p6_xn2_evac else None]
            tok_t = None
            for c8 in range(8):
                c = half * 8 + c8
                tok_t = tr(bank[:, c8 * 128:(c8 + 1) * 128],
                           xn2s[:, b, c * 128:(c + 1) * 128],
                           tdeps if c8 == 0 else (), inc=(c8 == 7))
            t_ev = act(lambda e, bank=bank, gt=gt, half=half:
                       e.activation(xn2T[:, half * 8:(half + 1) * 8,
                                         gt * 128:(gt + 1) * 128],
                                    bank[:, :], AF.Copy), [tok_t], inc=True)
            p6_xn2_evac.append(t_ev)
            evs.append(t_ev)
        xn2_done[gt] = evs

    p6_evacs(0, t_sq[7])
    p6_epilogue(0)
    for gt in range(1, 8):
        if len(pend_tr) >= 2:
            g2, ta2 = pend_tr.pop(0)
            p6_transposes(g2, ta2)
        p6_chains(gt)
        guard = t_sq[11] if gt == 1 else aout_free[gt % 2]
        p6_evacs(gt, guard)
        p6_epilogue(gt)
    while pend_tr:
        g2, ta2 = pend_tr.pop(0)
        p6_transposes(g2, ta2)
    t_xn2T = p6_xn2_evac[-1]
    p6_mm_last = p6_mm[(7, 3)]
    for s in range(4):
        bank_free[s] = ev_tok[(7, s)]
    dbg_dump("xn2T", xn2T[:, :, :], [t_xn2T])

    ring8 = Ring(dw + [rw8], init_tok=p6_mm_last)
    x2r_free = [None, None]
    obuf_free = [None, None]
    sg_free = [None, None]
    out_toks = []
    down_last_prev = None
    for e_i in range(E):
        t_int = None
        xdep = [xn2_done[2 * e_i][1], xn2_done[2 * e_i + 1][1]]
        for s in range(8):
            ss, slab, tok_ld = ring8.load(gup_r[:, e_i, s, :])
            tokU = None
            for pr in range(2):
                i_blk = 2 * s + pr
                biG, bfreeG = next_bank()
                biU, bfreeU = next_bank()
                tokG = None
                for kc in range(C):
                    dG = ([tok_ld, bfreeG] + xdep) if kc == 0 else ()
                    tokG = mm(fb[biG][:, 0:256], slab[:, kc, 256 * pr:256 * pr + 128],
                              xn2T[:, kc, e_i * 256:(e_i + 1) * 256],
                              start=(kc == 0), stop=(kc == C - 1), deps=dG,
                              inc=(kc == C - 1))
                for kc in range(C):
                    dU = [bfreeU] if kc == 0 else ()
                    tokU = mm(fb[biU][:, 0:256],
                              slab[:, kc, 256 * pr + 128:256 * (pr + 1)],
                              xn2T[:, kc, e_i * 256:(e_i + 1) * 256],
                              start=(kc == 0), stop=(kc == C - 1), deps=dU,
                              inc=(kc == C - 1))
                sb = i_blk % 2
                t_sil = act(lambda e, biG=biG, sb=sb:
                            e.activation(sgate[:, sb, :], fb[biG][:, 0:256], AF.Silu),
                            [tokG, sg_free[sb]], inc=True)
                t_int = vec(lambda e, biU=biU, sb=sb, i_blk=i_blk:
                            e.tensor_tensor(inter[:, i_blk, :], fb[biU][:, 0:256],
                                            sgate[:, sb, :], ALU.mult),
                            [tokU, t_sil, down_last_prev],
                            inc=True)
                bank_free[biG] = t_int
                bank_free[biU] = t_int
                sg_free[sb] = t_int
            ring8.done[ss] = tokU
        inter_last = t_int
        ev_toks = [[], []]
        for s in range(4):
            ss, slab, tok_ld = ring8.load(dwn_r[:, e_i, s, :])
            tok_mm = None
            for gt in range(2):
                bi, bfree = next_bank()
                for kc in range(C):
                    deps = ([tok_ld, bfree, inter_last,
                             obuf_free[gt]] if kc == 0 else ())
                    tok_mm = mm(fb[bi][:, :],
                                inter[:, kc, gt * 128:(gt + 1) * 128],
                                slab[:, kc, :], start=(kc == 0), stop=(kc == C - 1),
                                deps=deps, inc=(kc == C - 1))
                t_ev = act(lambda e, bi=bi, gt=gt, s=s:
                           e.activation(obuf[:, gt, s * 512:(s + 1) * 512],
                                        fb[bi][:, :], AF.Copy), [tok_mm], inc=True)
                bank_free[bi] = t_ev
                ev_toks[gt].append(t_ev)
            ring8.done[ss] = tok_mm
            down_last_prev = tok_mm
        for gt in range(2):
            g_glob = e_i * 2 + gt
            b = gt
            t_x2 = dma("sync", x2r[:, b, :],
                       x2sp.rearrange("(tt p) dd -> p tt dd", p=128)[:, g_glob, :],
                       [p6_spill[g_glob], x2r_free[b], pass2_last, p6_mm_last])
            _, t_rstd4 = ln_bn(obuf[:, gt, :], b,
                               [[ev_toks[gt][s]] for s in range(4)],
                               per_chunk=True)
            t_a = vec(lambda e, b=b, gt=gt:
                      e.tensor_scalar(obuf[:, gt, :], obuf[:, gt, :],
                                      stat[:, b, 0:1], stat[:, b, 2:3],
                                      ALU.subtract, ALU.mult),
                      [t_rstd4], inc=True)
            t_a2 = vec(lambda e, b=b, gt=gt:
                       e.tensor_tensor(obuf[:, gt, :], obuf[:, gt, :], x2r[:, b, :],
                                       ALU.add), [t_a, t_x2], inc=True)
            t_out = dma("sync",
                        out_d.rearrange("(tt p) dd -> p tt dd", p=128)[:, g_glob, :],
                        obuf[:, gt, :], [t_a2])
            out_toks.append(t_out)
            obuf_free[gt] = t_out
            x2r_free[b] = t_out

    if _DEBUG:
        out_toks.append(dbg_dump("gz", gz[:, :, :], [t_gz_all]))
        out_toks.append(dbg_dump("x2", x2sp[:, :], [p6_spill[gt2] for gt2 in range(8)]))
        out_toks.append(dbg_dump("kvl", kvl_d[:, :], [kvst_dma[hh] for hh in range(NH)]))
        out_toks.append(dbg_dump("ccin", cc_in.ap()[:, :],
                                 [cc_dma[hh] for hh in range(NH)]))
        out_toks.append(dbg_dump("inter", inter[:, :, :], out_toks[-1:]))
        out_toks.extend([dbg_t[k] for k in ("xn1T", "k", "v", "q", "gp", "kvinit",
                                            "o", "rrow", "xn2T")])
    P.emit("sync", lambda e: e.nop(), out_toks)
    P.emit("gpsimd", lambda e: e.nop(), out_toks)

    from contextlib import ExitStack
    with ExitStack() as es:
        sems = {s: es.enter_context(nc.semaphore(s))
                for s in ("S", "G", "T", "V", "A", "CC", "R0", "R1", "R2",
                          "VB0", "VB1", "K0", "K1", "PL",
                          "WO0", "WO1", "WO2", "WO3", "SP0", "SP1",
                          "KV", "CI", "KS0", "KS1")}
        with nc.Block() as block:
            block.sync(lambda e: P.run("sync", e, sems))
            block.gpsimd(lambda e: P.run("gpsimd", e, sems))
            block.tensor(lambda e: P.run("tensor", e, sems))
            block.vector(lambda e: P.run("vector", e, sems))
            block.scalar(lambda e: P.run("scalar", e, sems))
    return nc


_PERM = 4 * (np.arange(1024) % 256) + np.arange(1024) // 256
_IPERM = np.argsort(_PERM)
_NC_CACHE = [None]
_LAST_RES = [None]
_DEBUG = bool(os.environ.get("KDEBUG"))


def kernel(L0_attn_hidden_states, L0_attn_slope_rate, L0_attn_w_qkv, L0_attn_w_gate,
           L0_attn_w_out, L0_attn_norm_w, L0_ffn_gate_up_proj, L0_ffn_down_proj):
    x = np.asarray(L0_attn_hidden_states, np.float32)
    slope = np.asarray(L0_attn_slope_rate, np.float32).reshape(1, NH)
    w_qkv = np.asarray(L0_attn_w_qkv, np.float32)
    w_gate = np.asarray(L0_attn_w_gate, np.float32)
    w_out = np.asarray(L0_attn_w_out, np.float32)
    norm_w = np.asarray(L0_attn_norm_w, np.float32)
    gup = np.asarray(L0_ffn_gate_up_proj, np.float32)
    down = np.asarray(L0_ffn_down_proj, np.float32)

    def slabify(wT, w=256):
        nsl = wT.shape[1] // w
        a = wT.reshape(C, 128, nsl, w)
        return np.ascontiguousarray(a.transpose(2, 1, 0, 3).reshape(nsl * 128, C * w))

    k_cols = np.concatenate([h * 384 + 128 + np.arange(128) for h in range(NH)])
    v_cols = np.concatenate([h * 384 + 256 + np.arange(128) for h in range(NH)])
    q_cols = np.concatenate([h * 384 + np.arange(128) for h in range(NH)])
    bf = ml_dtypes.bfloat16
    wqkvT = slabify(w_qkv.T[:, np.concatenate([k_cols, q_cols])]).astype(bf)
    wv_h = slabify(w_qkv.T[:, v_cols], 512).astype(bf)
    wgT = slabify(w_gate.T).astype(bf)
    woT = slabify(w_out.T, 512).astype(bf)
    gup_cols = np.concatenate([np.concatenate([i * 128 + np.arange(128),
                                               2048 + i * 128 + np.arange(128)])
                               for i in range(16)])
    gup_h = np.concatenate([slabify(gup[e][:, gup_cols], 512)
                            for e in range(E)]).astype(bf)
    down_h = np.concatenate([slabify(down[e], 512) for e in range(E)]).astype(bf)
    slope_bc = np.ascontiguousarray(np.broadcast_to(slope, (128, NH)))
    normw_h = np.ascontiguousarray(norm_w.reshape(C, 128).T)

    if _NC_CACHE[0] is None:
        _NC_CACHE[0] = build_nc()
    nc = _NC_CACHE[0]

    in_maps = []
    for c in range(8):
        b, hf = divmod(c, 2)
        x_sh = np.ascontiguousarray(x[b, hf * T:(hf + 1) * T][_PERM])
        in_maps.append({
            "x_sh": x_sh, "wqkvT": wqkvT, "wv": wv_h, "wgT": wgT,
            "woT": woT,
            "gup_w": gup_h, "down_w": down_h, "slope_b": slope_bc,
            "normw": normw_h,
            "flag": np.full((128, 1), 1.0 if c % 2 else 0.0, np.float32),
        })
    res = run_bass_kernel_spmd(nc, in_maps, core_ids=list(range(8)))
    if _DEBUG:
        _LAST_RES[0] = res
    out = np.empty((4, 2048, D), np.float32)
    for c in range(8):
        b, hf = divmod(c, 2)
        out[b, hf * T:(hf + 1) * T] = np.asarray(res.results[c]["out"])[_IPERM]
    return out
